# version 03
# baseline (speedup 1.0000x reference)
import numpy as np
import ml_dtypes
import concourse.bass as bass
import concourse.mybir as mybir
from concourse.bass_utils import run_bass_kernel_spmd
from concourse.bass_core_rust import IndirectOffsetOnAxis

F32 = mybir.dt.float32
BF16 = mybir.dt.bfloat16
I32 = mybir.dt.int32
ALU = mybir.AluOpType
AX = mybir.AxisListType
AF = mybir.ActivationFunctionType

T = 2048
TTN = 16
D = 1024
DCN = 8
NG = 16384
E = 8
CAP = 4096
H = 4096
HTN = 32
SCN = 8
STN = 32

ENGS = ("pe", "act", "dve", "pool", "sp")
ENG2BLOCK = {"pe": "tensor", "act": "scalar", "dve": "vector", "pool": "gpsimd", "sp": "sync"}

STAGE = 99
TAPS = ()


SEM_BASE = 1 << 24


NSPD = 12
NPLD = 12
COMPUTE_SEMS = ("pe", "act", "dve", "plc")
SEMNAMES = list(COMPUTE_SEMS) + ["cc"] + [f"spd{i}" for i in range(NSPD)] + [f"pld{i}" for i in range(NPLD)]


class Prog:

    def __init__(self):
        self.phases = []
        self.q = {e: [] for e in ENGS}
        self.val = {s: (SEM_BASE if s in COMPUTE_SEMS else 0) for s in SEMNAMES}
        self.rr = {"sp": 0, "pool": 0}
        self.res = {}

    def op(self, eng, fn, reads=(), writes=(), waits=(), inc=1, kind="compute"):
        need = dict()

        def add(tok):
            if tok is None:
                return
            s, v = tok
            if need.get(s, 0) < v:
                need[s] = v

        for w in waits:
            add(w)
        for (buf, lo, hi) in reads:
            for (l2, h2, tok, wr) in self.res.get(buf, ()):
                if wr and l2 < hi and lo < h2:
                    if tok[0] == "pe" and eng == "pe":
                        continue
                    add(tok)
        for (buf, lo, hi) in writes:
            for (l2, h2, tok, wr) in self.res.get(buf, ()):
                if l2 < hi and lo < h2:
                    if tok[0] == "pe" and eng == "pe" and wr:
                        continue
                    add(tok)
        if kind == "dma":
            assert eng in ("sp", "pool")
            i = self.rr[eng]
            self.rr[eng] = (i + 1) % (NSPD if eng == "sp" else NPLD)
            sname = (f"spd{i}" if eng == "sp" else f"pld{i}")
            add((sname, self.val[sname]))
            inc = 16
        elif kind == "cc":
            sname = "cc"
            inc = 1
        else:
            sname = {"pe": "pe", "act": "act", "dve": "dve", "pool": "plc", "sp": "plc"}[eng]
            assert eng != "sp", "sync engine has no compute ops with incs"
        self.val[sname] += inc
        tok = (sname, self.val[sname])
        self.q[eng].append((fn, tuple(sorted(need.items())), inc, sname))
        for (buf, lo, hi) in writes:
            ent = self.res.setdefault(buf, [])
            ent[:] = [x for x in ent if not (lo <= x[0] and x[1] <= hi)]
            ent.append((lo, hi, tok, True))
        for (buf, lo, hi) in reads:
            self.res.setdefault(buf, []).append((lo, hi, tok, False))
        return tok

    def nop_op(self, eng, fn, waits=()):
        self.q[eng].append((fn, tuple(sorted({s: v for (s, v) in waits}.items())), 0, None))

    def phase(self):
        if any(self.q[e] for e in ENGS):
            self.phases.append(self.q)
            self.q = {e: [] for e in ENGS}

    def emit(self, block, sems, nc=None):
        self.phase()

        @block.gpsimd
        def _(g):
            g.dma_reset()
            for s in SEMNAMES:
                g.sem_clear(sems[s])
            for s in COMPUTE_SEMS:
                g.sem_inc(sems[s], SEM_BASE)
        first = {e: True for e in ENGS}
        last_wait = {e: {s: 0 for s in SEMNAMES} for e in ENGS}
        for ph in self.phases:
            for eng in ENGS:
                ops = ph[eng]
                if not ops:
                    continue
                dec = getattr(block, ENG2BLOCK[eng])

                def body(ectx, ops=ops, eng=eng):
                    lw = last_wait[eng]
                    if first[eng]:
                        first[eng] = False
                        if eng in ("pe", "act", "dve"):
                            ectx.wait_ge(sems[eng], SEM_BASE)
                            lw[eng] = SEM_BASE
                        elif eng == "sp":
                            ectx.wait_ge(sems["dve"], SEM_BASE)
                            lw["dve"] = SEM_BASE
                    for fn, waits, inc, sname in ops:
                        for (ws, wval) in waits:
                            if lw[ws] >= wval:
                                continue
                            ectx.wait_ge(sems[ws], wval)
                            lw[ws] = wval
                        inst = fn(ectx)
                        if inc:
                            inst.then_inc(sems[sname], inc)

                dec(body)
        n = sum(len(ph[e]) for ph in self.phases for e in ENGS)
        return n


def build_nc(stage=None, taps=None, sbuf_taps=None, rstop=99):
    stage = STAGE if stage is None else stage
    taps = TAPS if taps is None else taps
    nc = bass.Bass("TRN2", target_bir_lowering=False, detect_race_conditions=False)
    P = Prog()

    def ein(name, shape, dtype):
        return nc.dram_tensor(name, shape, dtype, kind="ExternalInput").ap()

    xq = ein("xq", [T, D], F32)
    rwf = ein("rwf", [1, 3 * D], F32)
    scale1 = ein("scale1", [1, 1], F32)
    wg3 = ein("wg3", [DCN, 128, 8], F32)
    w1c = ein("w1c", [D, H], F32)
    w2c = ein("w2c", [H, D], F32)
    b1c = ein("b1c", [HTN, 128, 1], F32)
    b2c = ein("b2c", [1, D], F32)
    ut_in = ein("ut_in", [128, 128], F32)
    onesrow_in = ein("onesrow_in", [1, 128], F32)
    ones128_in = ein("ones128_in", [128, 1], F32)
    identf_in = ein("identf_in", [128, 128], F32)
    iotap_in = ein("iotap_in", [128, 1], F32)
    iota8t_in = ein("iota8t_in", [128, 1024], F32)
    iota128_in = ein("iota128_in", [128, 128], F32)
    scol32_in = ein("scol32_in", [128, 32], F32)
    diag0_in = ein("diag0_in", [128, 128], F32)
    diagu_in = ein("diagu_in", [128, 128], F32)
    diagl_in = ein("diagl_in", [128, 128], F32)
    sel0b_in = ein("sel0b_in", [128, 16], F32)
    sel1b_in = ein("sel1b_in", [128, 16], F32)
    cval_in = ein("cval_in", [128, 1], F32)
    myidx_in = ein("myidx_in", [128, 1], I32)

    out = nc.dram_tensor("out", [T, D], F32, kind="ExternalOutput").ap()

    xln_dram = nc.dram_tensor("xln_dram", [T, D], BF16).ap()
    agx_out = nc.dram_tensor("agx_out", [NG, D], BF16).ap()
    logit_dram = nc.dram_tensor("logit_dram", [T, 8], F32).ap()
    agl_out = nc.dram_tensor("agl_out", [NG, 8], F32).ap()
    pack_dram = nc.dram_tensor("pack_dram", [2048], F32).ap()
    off_dram = nc.dram_tensor("off_dram", [2048], F32).ap()
    gt_dram = nc.dram_tensor("gt_dram", [2, 128], F32).ap()
    ginc0_dram = nc.dram_tensor("ginc0_dram", [NG], F32).ap()
    ginc1_dram = nc.dram_tensor("ginc1_dram", [NG], F32).ap()
    slots_dram = nc.dram_tensor("slots_dram", [NG // 16, 32], F32).ap()
    gates_dram = nc.dram_tensor("gates_dram", [NG // 16, 32], F32).ap()
    myslots_dram = nc.dram_tensor("myslots_dram", [T, 2], F32).ap()
    mygates_dram = nc.dram_tensor("mygates_dram", [T, 2], F32).ap()
    disp_dram = nc.dram_tensor("disp_dram", [CAP, D], BF16).ap()
    y_dram = nc.dram_tensor("y_dram", [CAP, D], BF16).ap()
    agy_out = nc.dram_tensor("agy_out", [SCN * E * 512, D], BF16).ap()
    aln2_dram = nc.dram_tensor("aln2_dram", [T, D], BF16).ap()
    a_dram = nc.dram_tensor("a_dram", [T, T], BF16).ap()
    mb_dram = nc.dram_tensor("mb_dram", [6, 1024], F32).ap()
    w1t_dram = nc.dram_tensor("w1t_dram", [HTN, 128, DCN * 128], BF16).ap()
    w2bf_dram = nc.dram_tensor("w2bf_dram", [H, D], BF16).ap()
    dram_by_name = {t.tensor.name: t for t in [
        xln_dram, agx_out, logit_dram, agl_out, slots_dram, gates_dram, myslots_dram,
        mygates_dram, disp_dram, y_dram, agy_out, aln2_dram, a_dram]}

    from contextlib import ExitStack
    es = ExitStack()

    def sbt(name, shape, dtype):
        return es.enter_context(nc.sbuf_tensor(name, shape, dtype))

    XS = sbt("XS", [128, TTN * 1024], F32)
    BUF_A = sbt("BUF_A", [128, 16384], BF16)
    BUF_B = sbt("BUF_B", [128, 16384], BF16)
    BUF_C = sbt("BUF_C", [128, 16384], BF16)
    ASTG = sbt("ASTG", [128, 2048], F32)
    XC = sbt("XC", [128, 1024], F32)
    MTBUF = sbt("MTBUF", [128, 1024], BF16)
    GVTBUF = sbt("GVTBUF", [128, 1024], BF16)
    B2B = sbt("B2B", [128, 1024], F32)
    B1 = sbt("B1", [128, HTN], F32)
    WGB = sbt("WGB", [128, 64], BF16)
    WGF = sbt("WGF", [128, 64], F32)
    UT = sbt("UT", [128, 128], F32)
    ONESROW = sbt("ONESROW", [1, 128], F32)
    ONES128 = sbt("ONES128", [128, 1], F32)
    IDENTF = sbt("IDENTF", [128, 128], F32)
    IDENTB = sbt("IDENTB", [128, 128], BF16)
    IOTAP = sbt("IOTAP", [128, 1], F32)
    IOTA8T = sbt("IOTA8T", [128, 1024], F32)
    IOTA128T = sbt("IOTA128T", [128, 128], F32)
    DIAG0 = sbt("DIAG0", [128, 128], F32)
    DIAGU = sbt("DIAGU", [128, 128], F32)
    DIAGL = sbt("DIAGL", [128, 128], F32)
    SEL0B = sbt("SEL0B", [128, 16], F32)
    SEL1B = sbt("SEL1B", [128, 16], F32)
    CVAL = sbt("CVAL", [128, 1], F32)
    MYIDX = sbt("MYIDX", [128, 1], I32)
    RW = sbt("RW", [1, D], F32)
    RSC = sbt("RSC", [1, 2048], F32)
    SCB = sbt("SCB", [1, 8], F32)
    ALPHAC = sbt("ALPHAC", [128, 1], F32)
    SGNPOS = sbt("SGNPOS", [128, 1], F32)
    COLS = sbt("COLS", [128, 32], F32)
    LSTG = sbt("LSTG", [128, 128], F32)
    XLNBF = sbt("XLNBF", [128, 1024], BF16)
    TTILE = sbt("TTILE", [128, 16], F32)
    TPEX = sbt("TPEX", [128, 16], F32)
    CROW = sbt("CROW", [1, 16], F32)
    TROW = RSC
    GTB = sbt("GTB", [128, 256], F32)
    C0COL = sbt("C0COL", [128, 1], F32)
    TOKIDX = sbt("TOKIDX", [128, 32], I32)
    MYSLOT = sbt("MYSLOT", [128, 32], F32)
    MYSLOTI = sbt("MYSLOTI", [128, 32], I32)
    MYGATE = sbt("MYGATE", [128, 32], F32)
    SRCH = sbt("SRCH", [128, 512], F32)
    SRCHI = sbt("SRCHI", [128, 64], I32)
    SCOL32 = sbt("SCOL32", [128, 32], F32)
    MYRAW = sbt("MYRAW", [128, 64], F32)
    EPSC = sbt("EPSC", [128, 1], F32)

    AF32 = BUF_A[:, :].bitcast(F32)
    BF32 = BUF_B[:, :].bitcast(F32)
    LG = AF32[:, 0:1024]
    MARR = AF32[:, 1024:3072]
    GEXI = AF32[:, 3072:5120]
    GEX = AF32[:, 5120:7168]
    SCR1 = AF32[:, 7168:8192]
    OH1 = BF32[:, 0:1024]
    OH2 = BF32[:, 1024:2048]
    SCR2 = BF32[:, 2048:4096]
    RS2 = BF32[:, 4096:4352]
    RG2 = BF32[:, 4352:4608]
    RSMALL = BF32[:, 4608:6656]

    def RS(i, w=128):
        return RSMALL[:, i:i + w]

    PS0 = es.enter_context(nc.psum_tensor("ps0", [128, 2048], F32))
    PS1 = es.enter_context(nc.psum_tensor("ps1", [128, 2048], F32))
    PSROW = PS1[0:1, :]

    sems = {s: es.enter_context(nc.semaphore("sem_" + s)) for s in SEMNAMES}

    def rA(lo, hi):
        return ("A", lo, hi)

    def rB(lo, hi):
        return ("B", lo, hi)

    def rC(lo, hi):
        return ("C", lo, hi)

    def rXS(tt):
        return ("XS", tt * 1024, (tt + 1) * 1024)

    def rP0(lo=0, hi=2048):
        return ("P0", lo, hi)

    def rP1(lo=0, hi=2048):
        return ("P1", lo, hi)

    def rD(name, lo=0, hi=1 << 40):
        return ("D_" + name, lo, hi)

    def rS(name, lo=0, hi=1 << 40):
        return ("S_" + name, lo, hi)

    def DMA(eng, out_ap, in_ap, reads=(), writes=(), waits=(), **kw):
        return P.op(eng, lambda ec, o=out_ap, i=in_ap, kw=kw: ec.dma_start(out=o, in_=i, **kw),
                    reads=reads, writes=writes, waits=waits, kind="dma")

    DMA("sp", XS[:, :].rearrange("p (tt d) -> p tt d", d=1024),
        xq.rearrange("(tt p) d -> p tt d", p=128),
        writes=[("XS", 0, TTN * 1024)])
    for dst, src, key in [
            (UT[:, :], ut_in[:, :], "UT"),
            (ONESROW[:, :], onesrow_in[:, :], "ONESROW"), (ONES128[:, :], ones128_in[:, :], "ONES128"),
            (IDENTF[:, :], identf_in[:, :], "IDENTF"), (IOTAP[:, :], iotap_in[:, :], "IOTAP"),
            (IOTA8T[:, :], iota8t_in[:, :], "IOTA8T"), (IOTA128T[:, :], iota128_in[:, :], "IOTA128T"), (SCOL32[:, :], scol32_in[:, :], "SCOL32"), (DIAG0[:, :], diag0_in[:, :], "DIAG0"),
            (DIAGU[:, :], diagu_in[:, :], "DIAGU"), (DIAGL[:, :], diagl_in[:, :], "DIAGL"),
            (SEL0B[:, :], sel0b_in[:, :], "SEL0B"), (SEL1B[:, :], sel1b_in[:, :], "SEL1B"),
            (CVAL[:, :], cval_in[:, :], "CVAL"), (MYIDX[:, :], myidx_in[:, :], "MYIDX")]:
        DMA("sp", dst, src, writes=[rS(key)])
    DMA("sp", SCB[:, 0:1], scale1[:, :], writes=[rS("SCB", 0, 1)])
    for ht in range(HTN):
        DMA("sp", B1[:, ht:ht + 1], b1c[ht, :, :], writes=[rS("B1", ht, ht + 1)])
    for dc in range(DCN):
        DMA("sp", WGF[:, dc * 8:(dc + 1) * 8], wg3[dc, :, :],
            writes=[rS("WGF", dc * 8, (dc + 1) * 8)])
    P.op("dve", lambda ec: ec.tensor_copy(out=WGB[:, :], in_=WGF[:, :]),
         reads=[rS("WGF", 0, 64)], writes=[rS("WGB")])

    P.op("dve", lambda ec: ec.memset(EPSC[:, :], 1e-5), writes=[rS("EPSC")])
    BREG = {}

    def _setup_pool_regs(ec):
        r1 = ec.alloc_register("bnd127")
        ec.reg_mov(r1, 127)
        BREG["b127"] = r1
        r2 = ec.alloc_register("bndNG")
        ec.reg_mov(r2, NG - 1)
        BREG["bNG"] = r2
        return ec.nop()

    P.nop_op("pool", _setup_pool_regs)
    if rstop != 99:
        P.op("pool", lambda ec: ec.memset(BUF_A[:, :], 0.0), writes=[rA(0, 16384)])
        P.op("pool", lambda ec: ec.memset(BUF_B[:, :], 0.0), writes=[rB(0, 16384)])

    P.op("dve", lambda ec: ec.tensor_copy(out=IDENTB[:, :], in_=IDENTF[:, :]),
         reads=[rS("IDENTF")], writes=[rS("IDENTB")])

    DMA("sp", RSC[:, 0:1024], b2c[:, :], writes=[rS("RSC", 0, 1024)])
    for j in range(2):
        P.op("pe", lambda ec, j=j: ec.matmul(out=PS0[:, j * 512:(j + 1) * 512], lhsT=ONESROW[:, :],
                                             rhs=RSC[:, j * 512:(j + 1) * 512], start=True, stop=True),
             reads=[rS("ONESROW"), rS("RSC", j * 512, (j + 1) * 512)], writes=[rP0(j * 512, (j + 1) * 512)])
    P.op("dve", lambda ec: ec.tensor_copy(out=B2B[:, :], in_=PS0[:, 0:1024]),
         reads=[rP0(0, 1024)], writes=[rS("B2B")])

    P.op("dve", lambda ec: ec.reciprocal(out=SCB[:, 1:2], in_=SCB[:, 0:1]),
         reads=[rS("SCB", 0, 1)], writes=[rS("SCB", 1, 2)])
    P.op("dve", lambda ec: ec.tensor_single_scalar(out=SCB[:, 2:3], in_=SCB[:, 1:2], scalar=2.0, op=ALU.mult),
         reads=[rS("SCB", 1, 2)], writes=[rS("SCB", 2, 3)])
    P.op("dve", lambda ec: ec.tensor_single_scalar(out=SCB[:, 3:4], in_=SCB[:, 0:1], scalar=0.0, op=ALU.is_gt),
         reads=[rS("SCB", 0, 1)], writes=[rS("SCB", 3, 4)])
    P.op("pe", lambda ec: ec.matmul(out=PS1[:, 0:1], lhsT=ONESROW[:, :], rhs=SCB[:, 2:3], start=True, stop=True),
         reads=[rS("ONESROW"), rS("SCB", 2, 3)], writes=[rP1(0, 1)])
    P.op("pe", lambda ec: ec.matmul(out=PS1[:, 1:2], lhsT=ONESROW[:, :], rhs=SCB[:, 3:4], start=True, stop=True),
         reads=[rS("ONESROW"), rS("SCB", 3, 4)], writes=[rP1(1, 2)])
    P.op("dve", lambda ec: ec.tensor_copy(out=ALPHAC[:, :], in_=PS1[:, 0:1]),
         reads=[rP1(0, 1)], writes=[rS("ALPHAC")])
    P.op("dve", lambda ec: ec.tensor_copy(out=SGNPOS[:, :], in_=PS1[:, 1:2]),
         reads=[rP1(1, 2)], writes=[rS("SGNPOS")])

    def prep_cs(r, cdst, sdst):
        DMA("sp", RW[:, :], rwf[:, r * 1024:(r + 1) * 1024], writes=[rS("RW")])
        g0 = RW[:, :].rearrange("o (i two) -> o i two", two=2)[:, :, 0:1].squeeze(2)
        g1 = RW[:, :].rearrange("o (i two) -> o i two", two=2)[:, :, 1:2].squeeze(2)
        P.op("dve", lambda ec: ec.tensor_mul(out=RSC[:, 1024:1536], in0=g0, in1=g0),
             reads=[rS("RW")], writes=[rS("RSC", 1024, 1536)])
        P.op("dve", lambda ec: ec.tensor_mul(out=RSC[:, 1536:2048], in0=g1, in1=g1),
             reads=[rS("RW")], writes=[rS("RSC", 1536, 2048)])
        P.op("dve", lambda ec: ec.tensor_add(out=RSC[:, 1024:1536], in0=RSC[:, 1024:1536], in1=RSC[:, 1536:2048]),
             reads=[rS("RSC", 1024, 2048)], writes=[rS("RSC", 1024, 1536)])
        P.op("act", lambda ec: ec.activation(out=RSC[:, 1536:2048], in_=RSC[:, 1024:1536], func=AF.Sqrt),
             reads=[rS("RSC", 1024, 1536)], writes=[rS("RSC", 1536, 2048)])
        P.op("dve", lambda ec: ec.reciprocal(out=RSC[:, 1024:1536], in_=RSC[:, 1536:2048]),
             reads=[rS("RSC", 1536, 2048)], writes=[rS("RSC", 1024, 1536)])
        P.op("dve", lambda ec, cdst=cdst: ec.tensor_mul(out=RSC[:, cdst:cdst + 512], in0=g0, in1=RSC[:, 1024:1536]),
             reads=[rS("RW"), rS("RSC", 1024, 1536)], writes=[rS("RSC", cdst, cdst + 512)])
        P.op("dve", lambda ec, sdst=sdst: ec.tensor_mul(out=RSC[:, sdst:sdst + 512], in0=g1, in1=RSC[:, 1024:1536]),
             reads=[rS("RW"), rS("RSC", 1024, 1536)], writes=[rS("RSC", sdst, sdst + 512)])

    prep_cs(0, 0, 512)
    P.op("dve", lambda ec: ec.tensor_copy(out=ASTG[0:1, 0:1024], in_=RSC[:, 0:1024]),
         reads=[rS("RSC", 0, 1024)], writes=[("ASTG", 0, 1024)])
    prep_cs(1, 0, 512)
    P.op("dve", lambda ec: ec.tensor_mul(out=RSC[:, 1024:1536], in0=ASTG[0:1, 0:512], in1=RSC[:, 0:512]),
         reads=[("ASTG", 0, 512), rS("RSC", 0, 512)], writes=[rS("RSC", 1024, 1536)])
    P.op("dve", lambda ec: ec.tensor_mul(out=RSC[:, 1536:2048], in0=ASTG[0:1, 512:1024], in1=RSC[:, 512:1024]),
         reads=[("ASTG", 512, 1024), rS("RSC", 512, 1024)], writes=[rS("RSC", 1536, 2048)])
    P.op("dve", lambda ec: ec.tensor_add(out=RSC[:, 1024:1536], in0=RSC[:, 1024:1536], in1=RSC[:, 1536:2048]),
         reads=[rS("RSC", 1024, 2048)], writes=[rS("RSC", 1024, 1536)])
    P.op("dve", lambda ec: ec.tensor_mul(out=RSC[:, 1536:2048], in0=ASTG[0:1, 0:512], in1=RSC[:, 512:1024]),
         reads=[("ASTG", 0, 512), rS("RSC", 512, 1024)], writes=[rS("RSC", 1536, 2048)])
    P.op("dve", lambda ec: ec.tensor_mul(out=RSC[:, 0:512], in0=ASTG[0:1, 512:1024], in1=RSC[:, 0:512]),
         reads=[("ASTG", 512, 1024), rS("RSC", 0, 512)], writes=[rS("RSC", 0, 512)])
    P.op("dve", lambda ec: ec.tensor_sub(out=RSC[:, 1536:2048], in0=RSC[:, 1536:2048], in1=RSC[:, 0:512]),
         reads=[rS("RSC", 0, 512), rS("RSC", 1536, 2048)], writes=[rS("RSC", 1536, 2048)])

    def write_rep2(mbrow, src_lo, negate=False):
        tgt = ASTG[0:1, 1024:2048].rearrange("o (i two) -> o i two", two=2)
        if negate:
            P.op("dve", lambda ec: ec.tensor_single_scalar(out=ASTG[0:1, 512:1024], in_=RSC[:, src_lo:src_lo + 512],
                                                           scalar=-1.0, op=ALU.mult),
                 reads=[rS("RSC", src_lo, src_lo + 512)], writes=[("ASTG", 512, 1024)])
            srcap = ASTG[0:1, 512:1024]
            srcres = ("ASTG", 512, 1024)
        else:
            srcap = RSC[:, src_lo:src_lo + 512]
            srcres = rS("RSC", src_lo, src_lo + 512)
        P.op("dve", lambda ec: ec.tensor_copy(out=tgt[:, :, 0:1].squeeze(2), in_=srcap),
             reads=[srcres], writes=[("ASTG", 1024, 2048)])
        P.op("dve", lambda ec: ec.tensor_copy(out=tgt[:, :, 1:2].squeeze(2), in_=srcap),
             reads=[srcres], writes=[("ASTG", 1024, 2048)])
        return DMA("sp", mb_dram[mbrow:mbrow + 1, :], ASTG[0:1, 1024:2048],
                   reads=[("ASTG", 1024, 2048)], writes=[rD("mb", mbrow * 1024, (mbrow + 1) * 1024)])

    write_rep2(0, 1024)
    write_rep2(1, 1536)
    write_rep2(2, 1536, negate=True)
    prep_cs(2, 0, 512)
    write_rep2(3, 0)
    write_rep2(4, 512)
    write_rep2(5, 512, negate=True)

    for rows, dstbuf, key in [((0, 1, 2), MTBUF, "MTBUF"), ((3, 4, 5), GVTBUF, "GVTBUF")]:
        for dc in range(DCN):
            for j, r in enumerate(rows):
                DMA("sp", SRCH[:, 384 + j:385 + j], mb_dram[r:r + 1, dc * 128:(dc + 1) * 128].rearrange("o p -> (o p)").unsqueeze(1),
                    reads=[rD("mb", r * 1024 + dc * 128, r * 1024 + (dc + 1) * 128)],
                    writes=[rS("SRCH", 384 + j, 385 + j)])
            P.op("dve", lambda ec: ec.tensor_scalar(out=SRCH[:, 0:128], in0=DIAG0[:, :], scalar1=SRCH[:, 384:385],
                                                    scalar2=None, op0=ALU.mult),
                 reads=[rS("DIAG0"), rS("SRCH", 384, 385)], writes=[rS("SRCH", 0, 128)])
            P.op("dve", lambda ec: ec.tensor_scalar(out=SRCH[:, 128:256], in0=DIAGU[:, :], scalar1=SRCH[:, 385:386],
                                                    scalar2=None, op0=ALU.mult),
                 reads=[rS("DIAGU"), rS("SRCH", 385, 386)], writes=[rS("SRCH", 128, 256)])
            P.op("dve", lambda ec: ec.tensor_add(out=SRCH[:, 0:128], in0=SRCH[:, 0:128], in1=SRCH[:, 128:256]),
                 reads=[rS("SRCH", 0, 256)], writes=[rS("SRCH", 0, 128)])
            P.op("dve", lambda ec: ec.tensor_scalar(out=SRCH[:, 128:256], in0=DIAGL[:, :], scalar1=SRCH[:, 386:387],
                                                    scalar2=None, op0=ALU.mult),
                 reads=[rS("DIAGL"), rS("SRCH", 386, 387)], writes=[rS("SRCH", 128, 256)])
            P.op("dve", lambda ec: ec.tensor_add(out=SRCH[:, 0:128], in0=SRCH[:, 0:128], in1=SRCH[:, 128:256]),
                 reads=[rS("SRCH", 0, 256)], writes=[rS("SRCH", 0, 128)])
            P.op("dve", lambda ec, dc=dc, dstbuf=dstbuf, key=key: ec.tensor_copy(
                out=dstbuf[:, dc * 128:(dc + 1) * 128], in_=SRCH[:, 0:128]),
                reads=[rS("SRCH", 0, 128)], writes=[rS(key, dc * 128, (dc + 1) * 128)])
    P.phase()

    for ht in range(HTN):
        src = w1c.rearrange("(dc p) (ht h) -> ht p dc h", p=128, h=128)[ht]
        DMA("pool", w1t_dram[ht].rearrange("p (dc h) -> p dc h", h=128), src,
            writes=[rD("w1t", ht, ht + 1)])
    for i in range(8):
        DMA("pool", w2bf_dram[i * 512:(i + 1) * 512, :], w2c[i * 512:(i + 1) * 512, :],
            writes=[rD("w2bf", i * 512, (i + 1) * 512)])
    P.phase()

    def emit_ln(dram_dst, dname):
        toks = []
        XCB = [(XC[:, :], ("XC", 0, 1024)), (BUF_C[:, :].bitcast(F32)[:, 0:1024], rC(0, 2048))]
        for tt in range(TTN):
            xt = XS[:, tt * 1024:(tt + 1) * 1024]
            XCt, xcres = XCB[tt % 2]
            cb = (tt % 2) * 8
            P.op("dve", lambda ec, xt=xt, cb=cb: ec.reduce_sum(out=COLS[:, cb:cb + 1], in_=xt, axis=AX.X),
                 reads=[rXS(tt)], writes=[("COLS", cb, cb + 1)])
            P.op("dve", lambda ec, cb=cb: ec.tensor_single_scalar(out=COLS[:, cb + 1:cb + 2], in_=COLS[:, cb:cb + 1],
                                                                  scalar=1.0 / 1024, op=ALU.mult),
                 reads=[("COLS", cb, cb + 1)], writes=[("COLS", cb + 1, cb + 2)])
            P.op("dve", lambda ec, xt=xt, cb=cb, XCt=XCt: ec.tensor_scalar(out=XCt, in0=xt, scalar1=COLS[:, cb + 1:cb + 2],
                                                                  scalar2=None, op0=ALU.subtract),
                 reads=[rXS(tt), ("COLS", cb + 1, cb + 2)], writes=[xcres])
            P.op("act", lambda ec, cb=cb, tt=tt, XCt=XCt: ec.activation(out=ASTG[:, (tt % 2) * 1024:(tt % 2) * 1024 + 1024],
                                                               in_=XCt, func=AF.Square,
                                                               accum_out=COLS[:, cb + 2:cb + 3]),
                 reads=[xcres],
                 writes=[("ASTG", (tt % 2) * 1024, (tt % 2) * 1024 + 1024), ("COLS", cb + 2, cb + 3)])
            P.op("act", lambda ec, cb=cb: ec.activation(out=COLS[:, cb + 3:cb + 4], in_=COLS[:, cb + 2:cb + 3],
                                                        func=AF.Sqrt, bias=EPSC[:, 0:1], scale=1.0 / 1024),
                 reads=[("COLS", cb + 2, cb + 3), rS("EPSC")], writes=[("COLS", cb + 3, cb + 4)])
            P.op("dve", lambda ec, cb=cb: ec.reciprocal(out=COLS[:, cb + 4:cb + 5], in_=COLS[:, cb + 3:cb + 4]),
                 reads=[("COLS", cb + 3, cb + 4)], writes=[("COLS", cb + 4, cb + 5)])
            xlb = XLNBF[:, :] if tt % 2 == 0 else BUF_C[:, 2048:3072]
            xlbres = ("XLNBF", 0, 1024) if tt % 2 == 0 else rC(2048, 3072)
            P.op("dve", lambda ec, cb=cb, XCt=XCt, xlb=xlb: ec.tensor_scalar(out=xlb, in0=XCt, scalar1=COLS[:, cb + 4:cb + 5],
                                                           scalar2=None, op0=ALU.mult),
                 reads=[xcres, ("COLS", cb + 4, cb + 5)], writes=[xlbres])
            toks.append(DMA("sp", dram_dst[tt * 128:(tt + 1) * 128, :], xlb,
                            reads=[xlbres], writes=[rD(dname, tt * 128, (tt + 1) * 128)]))
        return toks

    def emit_ag(in_ap, out_ap, rkey, wkey):
        return P.op("pool", lambda ec, i=in_ap, o=out_ap: ec.collective_compute(
            "AllGather", ALU.bypass, replica_groups=[list(range(8))], ins=[i], outs=[o]),
            reads=[rkey], writes=[wkey], kind="cc")

    def emit_ffn(tag):
        emit_ln(xln_dram, "xln")
        P.phase()
        for dc in range(DCN):
            DMA("sp", BUF_B[:, dc * 2048:(dc + 1) * 2048], xln_dram[:, dc * 128:(dc + 1) * 128],
                reads=[rD("xln")], writes=[rB(dc * 2048, (dc + 1) * 2048)], transpose=True)
        for tt in range(TTN):
            for dc in range(DCN):
                P.op("pe", lambda ec, tt=tt, dc=dc: ec.matmul(
                    out=PS0[:, 0:8],
                    lhsT=BUF_B[:, dc * 2048 + tt * 128: dc * 2048 + (tt + 1) * 128],
                    rhs=WGB[:, dc * 8:(dc + 1) * 8],
                    start=(dc == 0), stop=(dc == DCN - 1)),
                    reads=[rB(dc * 2048 + tt * 128, dc * 2048 + (tt + 1) * 128), rS("WGB")],
                    writes=[rP0(0, 8)])
            P.op("dve", lambda ec, tt=tt: ec.tensor_copy(out=LSTG[:, tt * 8:(tt + 1) * 8], in_=PS0[:, 0:8]),
                 reads=[rP0(0, 8)], writes=[("LSTG", tt * 8, (tt + 1) * 8)])
        DMA("sp", logit_dram.rearrange("(tt p) e -> p tt e", p=128),
            LSTG[:, :].rearrange("p (tt e) -> p tt e", e=8),
            reads=[("LSTG", 0, 128)], writes=[rD("logit")])
        P.phase()
        emit_ag(logit_dram[:, :], agl_out[:, :], rD("logit"), rD("agl"))
        P.phase()
        emit_ag(xln_dram[:, :], agx_out[:, :], rD("xln"), rD("agx"))
        P.phase()
        if stage <= 1:
            return

        DMA("sp", LG.rearrange("p (tau e) -> p tau e", e=8), agl_out.rearrange("(tau p) e -> p tau e", p=128),
            reads=[rD("agl")], writes=[rA(0, 2048)])
        lg3 = LG.rearrange("p (g e) -> p g e", e=8)
        io3 = IOTA8T[:, :].rearrange("p (g e) -> p g e", e=8)
        sc3 = SCR1[:, :].rearrange("p (g e) -> p g e", e=8)
        P.op("dve", lambda ec: ec.reduce_max(out=RS(1536), in_=lg3, axis=AX.X),
             reads=[rA(0, 2048)], writes=[rB(12288, 12544)])
        if rstop == 1:
            return
        P.op("dve", lambda ec: ec.tensor_tensor(out=sc3, in0=lg3, in1=RS(1536).unsqueeze(2).to_broadcast([128, 128, 8]),
                                                op=ALU.is_equal),
             reads=[rA(0, 2048), rB(12288, 12544)], writes=[rA(14336, 16384)])
        if rstop == 2:
            return
        P.op("dve", lambda ec: ec.scalar_tensor_tensor(out=SCR1[:, :], in0=SCR1[:, :], scalar=-16.0, in1=IOTA8T[:, :],
                                                       op0=ALU.mult, op1=ALU.add),
             reads=[rA(14336, 16384), rS("IOTA8T")], writes=[rA(14336, 16384)])
        if rstop == 3:
            return
        P.op("dve", lambda ec: ec.tensor_reduce(out=RS(0), in_=sc3, axis=AX.X, op=ALU.min),
             reads=[rA(14336, 16384)], writes=[rB(9216, 9472)])
        if rstop == 4:
            return
        P.op("dve", lambda ec: ec.tensor_single_scalar(out=RS(0), in_=RS(0), scalar=16.0, op=ALU.add),
             reads=[rB(9216, 9472)], writes=[rB(9216, 9472)])
        P.op("dve", lambda ec: ec.tensor_tensor(out=OH1[:, :].rearrange("p (g e) -> p g e", e=8), in0=io3,
                                                in1=RS(0).unsqueeze(2).to_broadcast([128, 128, 8]), op=ALU.is_equal),
             reads=[rS("IOTA8T"), rB(9216, 9472)], writes=[rB(0, 2048)])
        P.op("dve", lambda ec: ec.scalar_tensor_tensor(out=SCR1[:, :], in0=OH1[:, :], scalar=-1e30, in1=LG[:, :],
                                                       op0=ALU.mult, op1=ALU.add),
             reads=[rB(0, 2048), rA(0, 2048)], writes=[rA(14336, 16384)])
        P.op("dve", lambda ec: ec.reduce_max(out=RS(1664), in_=sc3, axis=AX.X),
             reads=[rA(14336, 16384)], writes=[rB(12544, 12800)])
        P.op("dve", lambda ec: ec.tensor_tensor(out=sc3, in0=sc3, in1=RS(1664).unsqueeze(2).to_broadcast([128, 128, 8]),
                                                op=ALU.is_equal),
             reads=[rA(14336, 16384), rB(12544, 12800)], writes=[rA(14336, 16384)])
        P.op("dve", lambda ec: ec.scalar_tensor_tensor(out=SCR1[:, :], in0=SCR1[:, :], scalar=-16.0, in1=IOTA8T[:, :],
                                                       op0=ALU.mult, op1=ALU.add),
             reads=[rA(14336, 16384), rS("IOTA8T")], writes=[rA(14336, 16384)])
        P.op("dve", lambda ec: ec.tensor_reduce(out=RS(128), in_=sc3, axis=AX.X, op=ALU.min),
             reads=[rA(14336, 16384)], writes=[rB(9472, 9728)])
        P.op("dve", lambda ec: ec.tensor_single_scalar(out=RS(128), in_=RS(128), scalar=16.0, op=ALU.add),
             reads=[rB(9472, 9728)], writes=[rB(9472, 9728)])
        P.op("dve", lambda ec: ec.tensor_tensor(out=OH2[:, :].rearrange("p (g e) -> p g e", e=8), in0=io3,
                                                in1=RS(128).unsqueeze(2).to_broadcast([128, 128, 8]), op=ALU.is_equal),
             reads=[rS("IOTA8T"), rB(9472, 9728)], writes=[rB(2048, 4096)])
        P.op("dve", lambda ec: ec.tensor_sub(out=RS(1792), in0=RS(1664), in1=RS(1536)),
             reads=[rB(12288, 12800)], writes=[rB(12800, 13056)])
        P.op("act", lambda ec: ec.activation(out=RS(256), in_=RS(1792), func=AF.Sigmoid, scale=-1.0),
             reads=[rB(12800, 13056)], writes=[rB(9728, 9984)])
        P.op("dve", lambda ec: ec.tensor_scalar(out=RS(384), in0=RS(256), scalar1=-1.0, scalar2=1.0,
                                                op0=ALU.mult, op1=ALU.add),
             reads=[rB(9728, 9984)], writes=[rB(9984, 10240)])
        m3 = MARR.rearrange("p (tau j) -> p tau j", j=16)
        P.op("dve", lambda ec: ec.tensor_copy(out=m3[:, :, 0:8], in_=OH1[:, :].rearrange("p (g e) -> p g e", e=8)),
             reads=[rB(0, 2048)], writes=[rA(2048, 6144)])
        P.op("dve", lambda ec: ec.tensor_copy(out=m3[:, :, 8:16], in_=OH2[:, :].rearrange("p (g e) -> p g e", e=8)),
             reads=[rB(2048, 4096)], writes=[rA(2048, 6144)])
        for j in range(4):
            P.op("pe", lambda ec, j=j: ec.matmul(out=PS0[:, j * 512:(j + 1) * 512], lhsT=UT[:, :],
                                                 rhs=MARR[:, j * 512:(j + 1) * 512], start=True, stop=True),
                 reads=[rS("UT"), rA(2048, 6144)], writes=[rP0(j * 512, (j + 1) * 512)])
        P.op("dve", lambda ec: ec.tensor_copy(out=GEXI[:, :], in_=PS0[:, :]),
             reads=[rP0()], writes=[rA(6144, 10240)])
        for j in range(4):
            P.op("pe", lambda ec, j=j: ec.matmul(out=PSROW[:, j * 512:(j + 1) * 512], lhsT=ONES128[:, :],
                                                 rhs=MARR[:, j * 512:(j + 1) * 512], start=True, stop=True),
                 reads=[rS("ONES128"), rA(2048, 6144)], writes=[rP1(j * 512, (j + 1) * 512)])
        P.op("dve", lambda ec: ec.tensor_copy(out=TROW[:, :], in_=PSROW[:, :]),
             reads=[rP1()], writes=[rS("RSC", 0, 2048)])
        DMA("sp", pack_dram.unsqueeze(0), TROW[:, :], reads=[rS("RSC", 0, 2048)], writes=[rD("pack")])
        DMA("sp", TTILE[:, :], pack_dram.rearrange("(tau j) -> tau j", j=16),
            reads=[rD("pack")], writes=[rS("TTILE")])
        P.op("pe", lambda ec: ec.matmul(out=PS1[:, 0:16], lhsT=UT[:, :], rhs=TTILE[:, :], start=True, stop=True),
             reads=[rS("UT"), rS("TTILE")], writes=[rP1(0, 16)])
        P.op("dve", lambda ec: ec.tensor_copy(out=TPEX[:, :], in_=PS1[:, 0:16]),
             reads=[rP1(0, 16)], writes=[rS("TPEX")])
        P.op("pe", lambda ec: ec.matmul(out=PS0[0:1, 0:16], lhsT=ONES128[:, :], rhs=TTILE[:, :], start=True, stop=True),
             reads=[rS("ONES128"), rS("TTILE")], writes=[rP0(0, 16)])
        P.op("dve", lambda ec: ec.tensor_copy(out=CROW[:, :], in_=PS0[0:1, 0:16]),
             reads=[rP0(0, 16)], writes=[rS("CROW")])
        P.op("dve", lambda ec: ec.tensor_add(out=SCR2[:, 0:16], in0=TPEX[:, :], in1=TTILE[:, :]),
             reads=[rS("TPEX"), rS("TTILE")], writes=[rB(4096, 4128)])
        P.op("dve", lambda ec: ec.tensor_mul(out=SCR2[:, 16:32], in0=SCR2[:, 0:16], in1=SEL0B[:, :]),
             reads=[rB(4096, 4128), rS("SEL0B")], writes=[rB(4128, 4160)])
        P.op("dve", lambda ec: ec.reduce_sum(out=SCR2[:, 48:49], in_=SCR2[:, 16:32], axis=AX.X),
             reads=[rB(4128, 4160)], writes=[rB(4192, 4194)])
        P.op("dve", lambda ec: ec.tensor_mul(out=SCR2[:, 16:32], in0=SCR2[:, 0:16], in1=SEL1B[:, :]),
             reads=[rB(4096, 4128), rS("SEL1B")], writes=[rB(4128, 4160)])
        P.op("dve", lambda ec: ec.reduce_sum(out=SCR2[:, 49:50], in_=SCR2[:, 16:32], axis=AX.X),
             reads=[rB(4128, 4160)], writes=[rB(4194, 4196)])
        DMA("sp", gt_dram[0:1, :].rearrange("o p -> (o p)").unsqueeze(1), SCR2[:, 48:49],
            reads=[rB(4192, 4194)], writes=[rD("gt", 0, 128)])
        DMA("sp", gt_dram[1:2, :].rearrange("o p -> (o p)").unsqueeze(1), SCR2[:, 49:50],
            reads=[rB(4194, 4196)], writes=[rD("gt", 128, 256)])
        DMA("sp", TROW[:, 0:128], gt_dram[0:1, :], reads=[rD("gt", 0, 128)], writes=[rS("RSC", 0, 128)])
        DMA("sp", TROW[:, 128:256], gt_dram[1:2, :], reads=[rD("gt", 128, 256)], writes=[rS("RSC", 128, 256)])
        P.op("pe", lambda ec: ec.matmul(out=PS1[:, 256:384], lhsT=ONESROW[:, :], rhs=TROW[:, 0:128],
                                        start=True, stop=True),
             reads=[rS("ONESROW"), rS("RSC", 0, 128)], writes=[rP1(256, 384)])
        P.op("pe", lambda ec: ec.matmul(out=PS1[:, 384:512], lhsT=ONESROW[:, :], rhs=TROW[:, 128:256],
                                        start=True, stop=True),
             reads=[rS("ONESROW"), rS("RSC", 128, 256)], writes=[rP1(384, 512)])
        P.op("dve", lambda ec: ec.tensor_copy(out=GTB[:, :], in_=PS1[:, 256:512]),
             reads=[rP1(256, 512)], writes=[rS("GTB")])
        P.op("dve", lambda ec: ec.tensor_mul(out=TROW[:, 256:272], in0=CROW[:, :], in1=SEL0B[0:1, :]),
             reads=[rS("CROW"), rS("SEL0B")], writes=[rS("RSC", 256, 272)])
        P.op("dve", lambda ec: ec.reduce_sum(out=TROW[:, 272:273], in_=TROW[:, 256:272], axis=AX.X),
             reads=[rS("RSC", 256, 272)], writes=[rS("RSC", 272, 273)])
        P.op("pe", lambda ec: ec.matmul(out=PS1[:, 512:513], lhsT=ONESROW[:, :], rhs=TROW[:, 272:273],
                                        start=True, stop=True),
             reads=[rS("ONESROW"), rS("RSC", 272, 273)], writes=[rP1(512, 513)])
        P.op("dve", lambda ec: ec.tensor_copy(out=C0COL[:, :], in_=PS1[:, 512:513]),
             reads=[rP1(512, 513)], writes=[rS("C0COL")])
        DMA("sp", off_dram.rearrange("(tau j) -> tau j", j=16), TPEX[:, :],
            reads=[rS("TPEX")], writes=[rD("off")])
        DMA("sp", TROW[:, :], off_dram.unsqueeze(0), reads=[rD("off")], writes=[rS("RSC", 0, 2048)])
        for j in range(4):
            P.op("pe", lambda ec, j=j: ec.matmul(out=PS0[:, j * 512:(j + 1) * 512], lhsT=ONESROW[:, :],
                                                 rhs=TROW[:, j * 512:(j + 1) * 512], start=True, stop=True),
                 reads=[rS("ONESROW"), rS("RSC", 0, 2048)], writes=[rP0(j * 512, (j + 1) * 512)])
        P.op("dve", lambda ec: ec.tensor_add(out=GEX[:, :], in0=GEXI[:, :], in1=PS0[:, :]),
             reads=[rA(6144, 10240), rP0()], writes=[rA(10240, 14336)])
        gex3 = GEX.rearrange("p (tau j) -> p tau j", j=16)
        P.op("dve", lambda ec: ec.tensor_tensor(out=sc3, in0=gex3[:, :, 0:8],
                                                in1=OH1[:, :].rearrange("p (g e) -> p g e", e=8), op=ALU.mult),
             reads=[rA(10240, 14336), rB(0, 2048)], writes=[rA(14336, 16384)])
        P.op("dve", lambda ec: ec.reduce_sum(out=RS(512), in_=sc3, axis=AX.X),
             reads=[rA(14336, 16384)], writes=[rB(10240, 10496)])
        P.op("dve", lambda ec: ec.tensor_tensor(out=sc3, in0=gex3[:, :, 8:16],
                                                in1=OH2[:, :].rearrange("p (g e) -> p g e", e=8), op=ALU.mult),
             reads=[rA(10240, 14336), rB(2048, 4096)], writes=[rA(14336, 16384)])
        P.op("dve", lambda ec: ec.reduce_sum(out=RS(640), in_=sc3, axis=AX.X),
             reads=[rA(14336, 16384)], writes=[rB(10496, 10752)])
        P.op("pe", lambda ec: ec.matmul(out=PS1[:, 516:524], lhsT=ONESROW[:, :], rhs=CROW[0:1, 0:8],
                                        start=True, stop=True),
             reads=[rS("ONESROW"), rS("CROW")], writes=[rP1(516, 524)])
        P.op("dve", lambda ec: ec.tensor_copy(out=SRCH[:, 0:8], in_=PS1[:, 516:524]),
             reads=[rP1(516, 524)], writes=[rS("SRCH", 0, 8)])
        P.op("dve", lambda ec: ec.tensor_tensor(out=sc3, in0=OH2[:, :].rearrange("p (g e) -> p g e", e=8),
                                                in1=SRCH[:, 0:8].unsqueeze(1).to_broadcast([128, 128, 8]),
                                                op=ALU.mult),
             reads=[rB(2048, 4096), rS("SRCH", 0, 8)], writes=[rA(14336, 16384)])
        P.op("dve", lambda ec: ec.reduce_sum(out=RS(1536), in_=sc3, axis=AX.X),
             reads=[rA(14336, 16384)], writes=[rB(12288, 12544)])
        P.op("dve", lambda ec: ec.tensor_add(out=RS(640), in0=RS(640), in1=RS(1536)),
             reads=[rB(12288, 12544), rB(10496, 10752)], writes=[rB(10496, 10752)])
        P.op("dve", lambda ec: ec.tensor_single_scalar(out=RS(768), in_=RS(512), scalar=float(CAP), op=ALU.is_lt),
             reads=[rB(10240, 10496)], writes=[rB(10752, 11008)])
        P.op("dve", lambda ec: ec.tensor_single_scalar(out=RS(896), in_=RS(640), scalar=float(CAP), op=ALU.is_lt),
             reads=[rB(10496, 10752)], writes=[rB(11008, 11264)])
        P.op("dve", lambda ec: ec.tensor_mul(out=RS(1024), in0=RS(256), in1=RS(768)),
             reads=[rB(9728, 9984), rB(10752, 11008)], writes=[rB(11264, 11520)])
        P.op("dve", lambda ec: ec.tensor_mul(out=RS(1152), in0=RS(384), in1=RS(896)),
             reads=[rB(9984, 10240), rB(11008, 11264)], writes=[rB(11520, 11776)])
        for mi_, (selb, selkey, ekey, eoff) in enumerate([(SEL0B, "SEL0B", "RSM", 0), (SEL1B, "SEL1B", "RSM", 128)]):
            P.op("dve", lambda ec, selb=selb: ec.tensor_tensor(
                out=SCR2[:, :].rearrange("p (tau j) -> p tau j", j=16), in0=gex3,
                in1=selb[:, :].unsqueeze(1).to_broadcast([128, 128, 16]), op=ALU.mult),
                reads=[rA(10240, 14336), rS(selkey)], writes=[rB(4096, 8192)])
            P.op("dve", lambda ec: ec.reduce_sum(out=SRCH[:, 0:128],
                                                 in_=SCR2[:, :].rearrange("p (tau j) -> p tau j", j=16), axis=AX.X),
                 reads=[rB(4096, 8192)], writes=[rS("SRCH", 0, 128)])
            P.op("dve", lambda ec, eoff=eoff: ec.tensor_scalar(out=SRCH[:, 128:256], in0=RS(eoff),
                                                               scalar1=CVAL[:, :], scalar2=None, op0=ALU.is_equal),
                 reads=[rB(9216 + 2*(eoff), 9216 + 2*(eoff + 128)), rS("CVAL")], writes=[rS("SRCH", 128, 256)])
            P.op("dve", lambda ec: ec.tensor_add(out=SRCH[:, 0:128], in0=SRCH[:, 0:128], in1=SRCH[:, 128:256]),
                 reads=[rS("SRCH", 0, 256)], writes=[rS("SRCH", 0, 128)])
            P.op("pe", lambda ec: ec.transpose(out=PS1[:, 1024:1152],
                                               in_=SRCH[:, 0:128], identity=IDENTF[:, :]),
                 reads=[rS("SRCH", 0, 128), rS("IDENTF")], writes=[rP1(1024, 1152)])
            P.op("dve", lambda ec: ec.tensor_copy(out=SRCH[:, 256:384], in_=PS1[:, 1024:1152]),
                 reads=[rP1(1024, 1152)], writes=[rS("SRCH", 256, 384)])
            DMA("sp", (ginc0_dram if mi_ == 0 else ginc1_dram).rearrange("(tau p) -> tau p", p=128),
                SRCH[:, 256:384],
                reads=[rS("SRCH", 256, 384)], writes=[rD("ginc", mi_ * NG, (mi_ + 1) * NG)])
        P.phase()
        if stage <= 2:
            return

        CF32 = BUF_C[:, :].bitcast(F32)
        P.op("dve", lambda ec: ec.tensor_scalar(out=SRCH[:, 64:96], in0=SCOL32[:, :], scalar1=C0COL[:, :],
                                                scalar2=None, op0=ALU.subtract),
             reads=[rS("SCOL32"), rS("C0COL")], writes=[rS("SRCH", 64, 96)])
        P.op("dve", lambda ec: ec.tensor_scalar(out=SRCH[:, 96:128], in0=SCOL32[:, :], scalar1=C0COL[:, :],
                                                scalar2=None, op0=ALU.is_lt),
             reads=[rS("SCOL32"), rS("C0COL")], writes=[rS("SRCH", 96, 128)])
        for m, (gtl, gth, rcol, rres) in enumerate([(0, 128, SCOL32[:, :], rS("SCOL32")),
                                                    (128, 256, SRCH[:, 64:96], rS("SRCH", 64, 96))]):
            P.op("dve", lambda ec, gtl=gtl, gth=gth, rcol=rcol: ec.tensor_tensor(
                out=CF32[:, 4096:8192].rearrange("p (c t) -> p c t", t=128),
                in0=GTB[:, gtl:gth].unsqueeze(1).to_broadcast([128, 32, 128]),
                in1=rcol.unsqueeze(2).to_broadcast([128, 32, 128]), op=ALU.is_le),
                reads=[rS("GTB"), rres], writes=[rC(8192, 16384)])
            P.op("dve", lambda ec, m=m: ec.reduce_sum(out=SRCH[:, m * 32:m * 32 + 32],
                                                      in_=CF32[:, 4096:8192].rearrange("p (c t) -> p c t", t=128),
                                                      axis=AX.X),
                 reads=[rC(8192, 16384)], writes=[rS("SRCH", m * 32, m * 32 + 32)])
        P.op("dve", lambda ec: ec.tensor_copy(out=SRCHI[:, :], in_=SRCH[:, 0:64]),
             reads=[rS("SRCH", 0, 64)], writes=[rS("SRCHI", 0, 64)])
        for m in range(2):
            gsrc = ginc0_dram if m == 0 else ginc1_dram
            for sc in range(STN):
                j = m * 32 + sc
                P.op("pool", lambda ec, gsrc=gsrc, j=j, m=m, sc=sc: ec.indirect_dma_start(
                    out=CF32[:, j * 128:(j + 1) * 128], out_offset=None,
                    in_=gsrc.rearrange("(b w) -> b w", w=128),
                    in_offset=IndirectOffsetOnAxis(SRCHI[:, j:j + 1], 0),
                    bounds_check=BREG["b127"], oob_is_err=False),
                    reads=[rD("ginc", m * NG, (m + 1) * NG), rS("SRCHI", j, j + 1)],
                    writes=[rC(j * 256, (j + 1) * 256)], kind="dma")
        for m, (rcol, rres) in enumerate([(SCOL32[:, :], rS("SCOL32")), (SRCH[:, 64:96], rS("SRCH", 64, 96))]):
            for h in range(2):
                P.op("dve", lambda ec, m=m, h=h, rcol=rcol: ec.tensor_tensor(
                    out=SCR2[:, :].rearrange("p (c t) -> p c t", t=128),
                    in0=CF32[:, m * 4096 + h * 2048: m * 4096 + (h + 1) * 2048].rearrange("p (c t) -> p c t", t=128),
                    in1=rcol[:, h * 16:(h + 1) * 16].unsqueeze(2).to_broadcast([128, 16, 128]), op=ALU.is_le),
                    reads=[rC(m * 8192 + h * 4096, m * 8192 + (h + 1) * 4096), rres], writes=[rB(4096, 8192)])
                P.op("dve", lambda ec, m=m, h=h: ec.reduce_sum(
                    out=SRCH[:, 128 + m * 32 + h * 16: 128 + m * 32 + (h + 1) * 16],
                    in_=SCR2[:, :].rearrange("p (c t) -> p c t", t=128), axis=AX.X),
                    reads=[rB(4096, 8192)],
                    writes=[rS("SRCH", 128 + m * 32 + h * 16, 128 + m * 32 + (h + 1) * 16)])
        P.op("dve", lambda ec: ec.scalar_tensor_tensor(out=SRCH[:, 192:224], in0=SRCH[:, 0:32], scalar=128.0,
                                                       in1=SRCH[:, 128:160], op0=ALU.mult, op1=ALU.add),
             reads=[rS("SRCH", 0, 32), rS("SRCH", 128, 160)], writes=[rS("SRCH", 192, 224)])
        P.op("dve", lambda ec: ec.scalar_tensor_tensor(out=SRCH[:, 224:256], in0=SRCH[:, 32:64], scalar=128.0,
                                                       in1=SRCH[:, 160:192], op0=ALU.mult, op1=ALU.add),
             reads=[rS("SRCH", 32, 64), rS("SRCH", 160, 192)], writes=[rS("SRCH", 224, 256)])
        P.op("dve", lambda ec: ec.tensor_sub(out=SRCH[:, 256:288], in0=SRCH[:, 192:224], in1=SRCH[:, 224:256]),
             reads=[rS("SRCH", 192, 256)], writes=[rS("SRCH", 256, 288)])
        P.op("dve", lambda ec: ec.tensor_mul(out=SRCH[:, 256:288], in0=SRCH[:, 256:288], in1=SRCH[:, 96:128]),
             reads=[rS("SRCH", 256, 288), rS("SRCH", 96, 128)], writes=[rS("SRCH", 256, 288)])
        P.op("dve", lambda ec: ec.tensor_add(out=SRCH[:, 256:288], in0=SRCH[:, 256:288], in1=SRCH[:, 224:256]),
             reads=[rS("SRCH", 224, 288)], writes=[rS("SRCH", 256, 288)])
        P.op("dve", lambda ec: ec.tensor_copy(out=TOKIDX[:, :], in_=SRCH[:, 256:288]),
             reads=[rS("SRCH", 256, 288)], writes=[rS("TOKIDX", 0, 32)])
        P.phase()

        for st in range(STN):
            par = st % 2
            dst = BUF_C[:, 8192 + par * 1024: 8192 + (par + 1) * 1024]
            dres = rC(8192 + par * 1024, 8192 + (par + 1) * 1024)
            P.op("pool", lambda ec, dst=dst: ec.memset(dst, 0.0), reads=[], writes=[dres])
            P.op("pool", lambda ec, st=st, dst=dst: ec.indirect_dma_start(
                out=dst, out_offset=None,
                in_=agx_out[:, :],
                in_offset=IndirectOffsetOnAxis(TOKIDX[:, st:st + 1], 0),
                bounds_check=BREG["bNG"], oob_is_err=False),
                reads=[rD("agx"), rS("TOKIDX", st, st + 1)], writes=[dres], kind="dma")
            DMA("sp", disp_dram[st * 128:(st + 1) * 128, :], dst,
                reads=[dres], writes=[rD("disp", st * 128, (st + 1) * 128)])
        P.phase()
        for k, (posoff, keepoff, eoff, dstoff) in enumerate([(512, 768, 0, 1280), (640, 896, 128, 1408)]):
            P.op("dve", lambda ec, posoff=posoff: ec.tensor_single_scalar(out=RS(1536), in_=RS(posoff),
                                                                          scalar=2048.0, op=ALU.is_ge),
                 reads=[rB(9216 + 2*(posoff), 9216 + 2*(posoff + 128))], writes=[rB(12288, 12544)])
            P.op("dve", lambda ec, posoff=posoff: ec.scalar_tensor_tensor(out=RS(1664), in0=RS(1536), scalar=-2048.0,
                                                                          in1=RS(posoff), op0=ALU.mult, op1=ALU.add),
                 reads=[rB(12288, 12544), rB(9216 + 2*(posoff), 9216 + 2*(posoff + 128))], writes=[rB(12544, 12800)])
            P.op("dve", lambda ec: ec.tensor_single_scalar(out=RS(1536), in_=RS(1536), scalar=16384.0, op=ALU.mult),
                 reads=[rB(12288, 12544)], writes=[rB(12288, 12544)])
            P.op("dve", lambda ec, eoff=eoff: ec.scalar_tensor_tensor(out=RS(1536), in0=RS(eoff), scalar=2048.0,
                                                                      in1=RS(1536), op0=ALU.mult, op1=ALU.add),
                 reads=[rB(9216 + 2*(eoff), 9216 + 2*(eoff + 128)), rB(12288, 12544)], writes=[rB(12288, 12544)])
            P.op("dve", lambda ec: ec.tensor_add(out=RS(1536), in0=RS(1536), in1=RS(1664)),
                 reads=[rB(12288, 12800)], writes=[rB(12288, 12544)])
            P.op("dve", lambda ec, eoff=eoff: ec.tensor_single_scalar(out=RS(1664), in_=RS(eoff), scalar=2048.0,
                                                                      op=ALU.mult),
                 reads=[rB(9216 + 2*(eoff), 9216 + 2*(eoff + 128))], writes=[rB(12544, 12800)])
            P.op("dve", lambda ec: ec.tensor_sub(out=RS(1536), in0=RS(1536), in1=RS(1664)),
                 reads=[rB(12288, 12800)], writes=[rB(12288, 12544)])
            P.op("dve", lambda ec, keepoff=keepoff: ec.tensor_mul(out=RS(1536), in0=RS(1536), in1=RS(keepoff)),
                 reads=[rB(12288, 12544), rB(9216 + 2*(keepoff), 9216 + 2*(keepoff + 128))], writes=[rB(12288, 12544)])
            P.op("dve", lambda ec, dstoff=dstoff: ec.tensor_add(out=RS(dstoff), in0=RS(1536), in1=RS(1664)),
                 reads=[rB(12288, 12800)], writes=[rB(9216 + 2*(dstoff), 9216 + 2*(dstoff + 128))])
        rs2v = RS2.rearrange("p (tau k) -> p tau k", k=2)
        rg2v = RG2.rearrange("p (tau k) -> p tau k", k=2)
        P.op("dve", lambda ec: ec.tensor_copy(out=rs2v[:, :, 0:1].squeeze(2), in_=RS(1280)),
             reads=[rB(11776, 12032)], writes=[rB(8192, 8704)])
        P.op("dve", lambda ec: ec.tensor_copy(out=rs2v[:, :, 1:2].squeeze(2), in_=RS(1408)),
             reads=[rB(12032, 12288)], writes=[rB(8192, 8704)])
        P.op("dve", lambda ec: ec.tensor_copy(out=rg2v[:, :, 0:1].squeeze(2), in_=RS(1024)),
             reads=[rB(11264, 11520)], writes=[rB(8704, 9216)])
        P.op("dve", lambda ec: ec.tensor_copy(out=rg2v[:, :, 1:2].squeeze(2), in_=RS(1152)),
             reads=[rB(11520, 11776)], writes=[rB(8704, 9216)])
        P.op("dve", lambda ec: ec.tensor_single_scalar(out=MYRAW[:, 0:1], in_=CVAL[:, :], scalar=16.0, op=ALU.mult),
             reads=[rS("CVAL")], writes=[rS("MYRAW", 0, 1)])
        for tt in range(TTN):
            P.op("dve", lambda ec, tt=tt: ec.tensor_single_scalar(out=MYRAW[:, 1:2], in_=MYRAW[:, 0:1],
                                                                  scalar=float(tt), op=ALU.add),
                 reads=[rS("MYRAW", 0, 1)], writes=[rS("MYRAW", 1, 2)])
            P.op("dve", lambda ec: ec.tensor_scalar(out=SRCH[:, 128:256],
                                                    in0=IOTA128T[:, :], scalar1=MYRAW[:, 1:2],
                                                    scalar2=None, op0=ALU.is_equal),
                 reads=[rS("IOTA128T"), rS("MYRAW", 1, 2)], writes=[rS("SRCH", 128, 256)])
            for (packv, packres, dst, dstres) in [
                    (rs2v, rB(8192, 8704), MYSLOT, "MYSLOT"), (rg2v, rB(8704, 9216), MYGATE, "MYGATE")]:
                P.op("dve", lambda ec, packv=packv: ec.tensor_tensor(
                    out=SRCH[:, 256:512].rearrange("p (k tau) -> p k tau", tau=128),
                    in0=packv.transpose([0, 2, 1]),
                    in1=SRCH[:, 128:256].unsqueeze(1).to_broadcast([128, 2, 128]), op=ALU.mult),
                    reads=[packres, rS("SRCH", 128, 256)], writes=[rS("SRCH", 256, 512)])
                P.op("dve", lambda ec, tt=tt, dst=dst: ec.reduce_sum(
                    out=dst[:, 2 * tt:2 * tt + 2],
                    in_=SRCH[:, 256:512].rearrange("p (k tau) -> p k tau", tau=128), axis=AX.X),
                    reads=[rS("SRCH", 256, 512)], writes=[rS(dstres, 2 * tt, 2 * tt + 2)])
        P.op("dve", lambda ec: ec.tensor_copy(out=MYSLOTI[:, :], in_=MYSLOT[:, :]),
             reads=[rS("MYSLOT", 0, 32)], writes=[rS("MYSLOTI")])
        DMA("sp", myslots_dram.rearrange("(tt p) k -> p tt k", p=128),
            MYSLOT[:, :].rearrange("p (tt k) -> p tt k", k=2),
            reads=[rS("MYSLOT", 0, 32)], writes=[rD("myslots")])
        DMA("sp", mygates_dram.rearrange("(tt p) k -> p tt k", p=128),
            MYGATE[:, :].rearrange("p (tt k) -> p tt k", k=2),
            reads=[rS("MYGATE", 0, 32)], writes=[rD("mygates")])
        P.phase()
        if stage <= 3:
            return

        PSH = [PS0[:, 0:512], PS0[:, 512:1024]]
        psh_res = [rP0(0, 512), rP0(512, 1024)]
        psy = [(PS0[:, 0:1024], rP0(0, 1024)), (PS0[:, 1024:2048], rP0(1024, 2048)),
               (PS1[:, 0:1024], rP1(0, 1024)), (PS1[:, 1024:2048], rP1(1024, 2048))]
        W1B = [(BUF_C[:, 10240:11264], rC(10240, 11264)), (BUF_C[:, 11264:12288], rC(11264, 12288))]
        W2B = [(BUF_C[:, 12288:13312], rC(12288, 13312)), (BUF_C[:, 13312:14336], rC(13312, 14336))]
        YSTG = BUF_C[:, 14336:15360]
        ystg_res = rC(14336, 15360)
        for sc8 in range(SCN):
            par8 = sc8 % 2
            dbase = par8 * 4096
            for dc in range(DCN):
                DMA("sp", BUF_B[:, dbase + dc * 512: dbase + (dc + 1) * 512],
                    disp_dram[sc8 * 512:(sc8 + 1) * 512, dc * 128:(dc + 1) * 128],
                    reads=[rD("disp", sc8 * 512, (sc8 + 1) * 512)],
                    writes=[rB(dbase + dc * 512, dbase + (dc + 1) * 512)], transpose=True)
            for ht in range(HTN):
                parw = ht % 2
                DMA("sp", W1B[parw][0], w1t_dram[ht], reads=[rD("w1t", ht, ht + 1)], writes=[W1B[parw][1]])
                for dc in range(DCN):
                    P.op("pe", lambda ec, ht=ht, dc=dc, parw=parw, dbase=dbase: ec.matmul(
                        out=PSH[ht % 2], lhsT=BUF_C[:, 10240 + parw * 1024 + dc * 128: 10240 + parw * 1024 + (dc + 1) * 128],
                        rhs=BUF_B[:, dbase + dc * 512: dbase + (dc + 1) * 512],
                        start=(dc == 0), stop=(dc == DCN - 1)),
                        reads=[W1B[parw][1], rB(dbase + dc * 512, dbase + (dc + 1) * 512)],
                        writes=[psh_res[ht % 2]])
                P.op("act", lambda ec, ht=ht: ec.activation(out=BUF_A[:, ht * 512:(ht + 1) * 512],
                                                            in_=PSH[ht % 2], func=AF.Gelu,
                                                            bias=B1[:, ht:ht + 1], scale=1.0),
                     reads=[psh_res[ht % 2], rS("B1", ht, ht + 1)], writes=[rA(ht * 512, (ht + 1) * 512)])
            for ht in range(HTN):
                parw = ht % 2
                DMA("sp", W2B[parw][0], w2bf_dram[ht * 128:(ht + 1) * 128, :],
                    reads=[rD("w2bf", ht * 128, (ht + 1) * 128)], writes=[W2B[parw][1]])
                for st4 in range(4):
                    for hf in range(2):
                        P.op("pe", lambda ec, ht=ht, st4=st4, hf=hf, parw=parw: ec.matmul(
                            out=psy[st4][0][:, hf * 512:(hf + 1) * 512],
                            lhsT=BUF_A[:, ht * 512 + st4 * 128: ht * 512 + (st4 + 1) * 128],
                            rhs=BUF_C[:, 12288 + parw * 1024 + hf * 512: 12288 + parw * 1024 + (hf + 1) * 512],
                            start=(ht == 0), stop=(ht == HTN - 1)),
                            reads=[rA(ht * 512, (ht + 1) * 512), W2B[parw][1]],
                            writes=[psy[st4][1]])
            for st4 in range(4):
                P.op("dve", lambda ec, st4=st4: ec.tensor_add(out=YSTG[:, :], in0=psy[st4][0], in1=B2B[:, :]),
                     reads=[psy[st4][1], rS("B2B")], writes=[ystg_res])
                DMA("sp", y_dram[sc8 * 512 + st4 * 128: sc8 * 512 + (st4 + 1) * 128, :], YSTG[:, :],
                    reads=[ystg_res], writes=[rD("y", sc8 * 512 + st4 * 128, sc8 * 512 + (st4 + 1) * 128)])
            if sc8 in (3, 7):
                half = sc8 // 4
                P.phase()
                emit_ag(y_dram[half * 2048:(half + 1) * 2048, :],
                        agy_out[half * 16384:(half + 1) * 16384, :],
                        rD("y", half * 2048, (half + 1) * 2048),
                        rD("agy", half * 16384, (half + 1) * 16384))
                P.phase()

        for tt in range(TTN):
            par = tt % 2
            yk0 = BUF_C[:, 8192 + par * 2048: 8192 + par * 2048 + 1024]
            yk1 = BUF_C[:, 8192 + par * 2048 + 1024: 8192 + par * 2048 + 2048]
            r0 = rC(8192 + par * 2048, 8192 + par * 2048 + 1024)
            r1 = rC(8192 + par * 2048 + 1024, 8192 + par * 2048 + 2048)
            P.op("pool", lambda ec, tt=tt, yk0=yk0: ec.indirect_dma_start(
                out=yk0, out_offset=None, in_=agy_out[:, :],
                in_offset=IndirectOffsetOnAxis(MYSLOTI[:, 2 * tt:2 * tt + 1], 0)),
                reads=[rD("agy"), rS("MYSLOTI")], writes=[r0], kind="dma")
            P.op("pool", lambda ec, tt=tt, yk1=yk1: ec.indirect_dma_start(
                out=yk1, out_offset=None, in_=agy_out[:, :],
                in_offset=IndirectOffsetOnAxis(MYSLOTI[:, 2 * tt + 1:2 * tt + 2], 0)),
                reads=[rD("agy"), rS("MYSLOTI")], writes=[r1], kind="dma")
            astg_r = ("ASTG", par * 1024, (par + 1) * 1024)
            P.op("dve", lambda ec, tt=tt, yk0=yk0, par=par: ec.tensor_scalar(
                out=ASTG[:, par * 1024:(par + 1) * 1024], in0=yk0, scalar1=MYGATE[:, 2 * tt:2 * tt + 1],
                scalar2=None, op0=ALU.mult),
                reads=[r0, rS("MYGATE")], writes=[astg_r])
            P.op("dve", lambda ec, tt=tt, yk1=yk1, par=par: ec.scalar_tensor_tensor(
                out=ASTG[:, par * 1024:(par + 1) * 1024], in0=yk1, scalar=MYGATE[:, 2 * tt + 1:2 * tt + 2],
                in1=ASTG[:, par * 1024:(par + 1) * 1024], op0=ALU.mult, op1=ALU.add),
                reads=[r1, rS("MYGATE"), astg_r], writes=[astg_r])
            P.op("dve", lambda ec, tt=tt, par=par: ec.scalar_tensor_tensor(
                out=XS[:, tt * 1024:(tt + 1) * 1024], in0=ASTG[:, par * 1024:(par + 1) * 1024], scalar=0.5,
                in1=XS[:, tt * 1024:(tt + 1) * 1024], op0=ALU.mult, op1=ALU.add),
                reads=[astg_r, rXS(tt)], writes=[rXS(tt)])
        P.phase()

    def emit_attn():
        emit_ln(aln2_dram, "aln2")
        P.phase()
        for dc in range(DCN):
            DMA("sp", BUF_A[:, dc * 2048:(dc + 1) * 2048], aln2_dram[:, dc * 128:(dc + 1) * 128],
                reads=[rD("aln2")], writes=[rA(dc * 2048, (dc + 1) * 2048)], transpose=True)
        for dc in range(DCN):
            pb, pres = (PS0, rP0) if dc % 2 == 0 else (PS1, rP1)
            for j in range(4):
                P.op("pe", lambda ec, dc=dc, j=j, pb=pb: ec.matmul(
                    out=pb[:, j * 512:(j + 1) * 512], lhsT=MTBUF[:, dc * 128:(dc + 1) * 128],
                    rhs=BUF_A[:, dc * 2048 + j * 512: dc * 2048 + (j + 1) * 512], start=True, stop=True),
                    reads=[rS("MTBUF"), rA(dc * 2048 + j * 512, dc * 2048 + (j + 1) * 512)],
                    writes=[pres(j * 512, (j + 1) * 512)])
            P.op("dve", lambda ec, dc=dc, pb=pb: ec.tensor_copy(out=BUF_B[:, dc * 2048:(dc + 1) * 2048], in_=pb[:, :]),
                 reads=[pres()], writes=[rB(dc * 2048, (dc + 1) * 2048)])
        for kt in range(TTN):
            pb, pres = (PS0, rP0) if kt % 2 == 0 else (PS1, rP1)
            for dc in range(DCN):
                P.op("pe", lambda ec, kt=kt, dc=dc, pb=pb: ec.matmul(
                    out=pb[:, dc * 128:(dc + 1) * 128],
                    lhsT=BUF_A[:, dc * 2048 + kt * 128: dc * 2048 + (kt + 1) * 128],
                    rhs=GVTBUF[:, dc * 128:(dc + 1) * 128], start=True, stop=True),
                    reads=[rA(dc * 2048 + kt * 128, dc * 2048 + (kt + 1) * 128), rS("GVTBUF")],
                    writes=[pres(dc * 128, (dc + 1) * 128)])
            P.op("dve", lambda ec, kt=kt, pb=pb: ec.tensor_copy(out=BUF_C[:, kt * 1024:(kt + 1) * 1024],
                                                                in_=pb[:, 0:1024]),
                 reads=[pres(0, 1024)], writes=[rC(kt * 1024, (kt + 1) * 1024)])
        P.phase()
        def emit_atrans(q):
            pbq, presq = (PS0, rP0) if q % 2 == 0 else (PS1, rP1)
            for kt in range(8):
                off = kt * 2048
                P.op("pe", lambda ec, kt=kt, pbq=pbq: ec.matmul(
                    out=pbq[:, 0:128],
                    lhsT=ASTG[:, kt * 128:(kt + 1) * 128], rhs=IDENTF[:, :],
                    start=True, stop=True, is_transpose=True),
                    reads=[("ASTG", kt * 128, (kt + 1) * 128), rS("IDENTF")],
                    writes=[presq(0, 128)])
                P.op("dve", lambda ec, kt=kt, q=q, off=off, pbq=pbq: ec.tensor_copy(
                    out=BUF_A[:, off + q * 128: off + (q + 1) * 128], in_=pbq[:, 0:128]),
                    reads=[presq(0, 128)], writes=[rA(off + q * 128, off + (q + 1) * 128)])
            P.op("dve", lambda ec: ec.tensor_copy(out=XC[:, 0:512].bitcast(BF16), in_=ASTG[:, 1024:2048]),
                 reads=[("ASTG", 1024, 2048)], writes=[("XC", 0, 1024)])
            DMA("sp", a_dram[q * 128:(q + 1) * 128, 0:1024], XC[:, 0:512].bitcast(BF16),
                reads=[("XC", 0, 512)], writes=[rD("a", q * 128, (q + 1) * 128)])

        for qt in range(TTN):
            pb, pres = (PS0, rP0) if qt % 2 == 0 else (PS1, rP1)
            for dc in range(DCN):
                for j in range(4):
                    P.op("pe", lambda ec, qt=qt, dc=dc, j=j, pb=pb: ec.matmul(
                        out=pb[:, j * 512:(j + 1) * 512],
                        lhsT=BUF_A[:, dc * 2048 + qt * 128: dc * 2048 + (qt + 1) * 128],
                        rhs=BUF_B[:, dc * 2048 + j * 512: dc * 2048 + (j + 1) * 512],
                        start=(dc == 0), stop=(dc == DCN - 1)),
                        reads=[rA(dc * 2048 + qt * 128, dc * 2048 + (qt + 1) * 128),
                               rB(dc * 2048 + j * 512, dc * 2048 + (j + 1) * 512)],
                        writes=[pres(j * 512, (j + 1) * 512)])
            cb = 16 + (qt % 2) * 8
            P.op("dve", lambda ec, pb=pb, cb=cb: ec.reduce_max(out=COLS[:, cb:cb + 1], in_=pb[:, :], axis=AX.X),
                 reads=[pres()], writes=[("COLS", cb, cb + 1)])
            P.op("dve", lambda ec, pb=pb, cb=cb: ec.tensor_reduce(out=COLS[:, cb + 1:cb + 2], in_=pb[:, :],
                                                                  axis=AX.X, op=ALU.min),
                 reads=[pres()], writes=[("COLS", cb + 1, cb + 2)])
            P.op("dve", lambda ec, cb=cb: ec.tensor_sub(out=COLS[:, cb + 2:cb + 3], in0=COLS[:, cb:cb + 1],
                                                        in1=COLS[:, cb + 1:cb + 2]),
                 reads=[("COLS", cb, cb + 2)], writes=[("COLS", cb + 2, cb + 3)])
            P.op("dve", lambda ec, cb=cb: ec.tensor_mul(out=COLS[:, cb + 2:cb + 3], in0=COLS[:, cb + 2:cb + 3],
                                                        in1=SGNPOS[:, :]),
                 reads=[("COLS", cb + 2, cb + 3), rS("SGNPOS")], writes=[("COLS", cb + 2, cb + 3)])
            P.op("dve", lambda ec, cb=cb: ec.tensor_add(out=COLS[:, cb + 2:cb + 3], in0=COLS[:, cb + 2:cb + 3],
                                                        in1=COLS[:, cb + 1:cb + 2]),
                 reads=[("COLS", cb + 1, cb + 3)], writes=[("COLS", cb + 2, cb + 3)])
            P.op("dve", lambda ec, cb=cb: ec.tensor_mul(out=COLS[:, cb + 3:cb + 4], in0=COLS[:, cb + 2:cb + 3],
                                                        in1=ALPHAC[:, :]),
                 reads=[("COLS", cb + 2, cb + 3), rS("ALPHAC")], writes=[("COLS", cb + 3, cb + 4)])
            P.op("dve", lambda ec, cb=cb: ec.tensor_single_scalar(out=COLS[:, cb + 3:cb + 4], in_=COLS[:, cb + 3:cb + 4],
                                                                  scalar=-1.0, op=ALU.mult),
                 reads=[("COLS", cb + 3, cb + 4)], writes=[("COLS", cb + 3, cb + 4)])
            if qt > 0:
                emit_atrans(qt - 1)
            P.op("act", lambda ec, pb=pb, cb=cb: ec.activation(
                out=ASTG[:, :], in_=pb[:, :], func=AF.Exp,
                bias=COLS[:, cb + 3:cb + 4], scale=ALPHAC[:, :]),
                reads=[pres(), ("COLS", cb + 3, cb + 4), rS("ALPHAC")],
                writes=[("ASTG", 0, 2048)])
        emit_atrans(TTN - 1)
        for kt in range(8, TTN):
            off = (kt % 8) * 2048
            DMA("sp", BUF_B[:, off:off + 2048], a_dram[:, (kt - 8) * 128:(kt - 7) * 128],
                reads=[rD("a")], writes=[rB(off, off + 2048)], transpose=True)
        for qt in range(TTN):
            pb, pres = (PS0, rP0) if qt % 2 == 0 else (PS1, rP1)
            for kt in range(TTN):
                dstbuf = BUF_A if kt < 8 else BUF_B
                rr = rA if kt < 8 else rB
                off = (kt % 8) * 2048
                for hf in range(2):
                    P.op("pe", lambda ec, qt=qt, kt=kt, hf=hf, pb=pb, dstbuf=dstbuf, off=off: ec.matmul(
                        out=pb[:, hf * 512:(hf + 1) * 512],
                        lhsT=dstbuf[:, off + qt * 128: off + (qt + 1) * 128],
                        rhs=BUF_C[:, kt * 1024 + hf * 512: kt * 1024 + (hf + 1) * 512],
                        start=(kt == 0), stop=(kt == TTN - 1)),
                        reads=[rr(off + qt * 128, off + (qt + 1) * 128),
                               rC(kt * 1024 + hf * 512, kt * 1024 + (hf + 1) * 512)],
                        writes=[pres(hf * 512, (hf + 1) * 512)])
            cb = 16 + (qt % 2) * 8
            P.op("act", lambda ec, pb=pb, cb=cb, qt=qt: ec.activation(
                out=ASTG[:, (qt % 2) * 1024:(qt % 2) * 1024 + 1024], in_=pb[:, 0:1024], func=AF.Square,
                accum_out=COLS[:, cb + 6:cb + 7]),
                reads=[pres(0, 1024)],
                writes=[("ASTG", (qt % 2) * 1024, (qt % 2) * 1024 + 1024), ("COLS", cb + 6, cb + 7)])
            P.op("dve", lambda ec, cb=cb: ec.tensor_single_scalar(out=COLS[:, cb + 6:cb + 7], in_=COLS[:, cb + 6:cb + 7],
                                                                  scalar=1e-8, op=ALU.max),
                 reads=[("COLS", cb + 6, cb + 7)], writes=[("COLS", cb + 6, cb + 7)])
            P.op("act", lambda ec, cb=cb: ec.activation(out=COLS[:, cb + 7:cb + 8], in_=COLS[:, cb + 6:cb + 7],
                                                        func=AF.Sqrt),
                 reads=[("COLS", cb + 6, cb + 7)], writes=[("COLS", cb + 7, cb + 8)])
            P.op("dve", lambda ec, cb=cb: ec.reciprocal(out=COLS[:, cb + 6:cb + 7], in_=COLS[:, cb + 7:cb + 8]),
                 reads=[("COLS", cb + 7, cb + 8)], writes=[("COLS", cb + 6, cb + 7)])
            P.op("dve", lambda ec, qt=qt, pb=pb, cb=cb: ec.scalar_tensor_tensor(
                out=XS[:, qt * 1024:(qt + 1) * 1024], in0=pb[:, 0:1024], scalar=COLS[:, cb + 6:cb + 7],
                in1=XS[:, qt * 1024:(qt + 1) * 1024], op0=ALU.mult, op1=ALU.add),
                reads=[pres(0, 1024), ("COLS", cb + 6, cb + 7), rXS(qt)], writes=[rXS(qt)])
        P.phase()

    ck_dram = {}

    def ckpt(name):
        t = nc.dram_tensor("ck_" + name, [T, D], F32, kind="ExternalOutput").ap()
        ck_dram[name] = t
        DMA("sp", t.rearrange("(tt p) d -> p tt d", p=128),
            XS[:, :].rearrange("p (tt d) -> p tt d", d=1024),
            reads=[("XS", 0, TTN * 1024)], writes=[("CK_" + name, 0, 1)])

    emit_ffn("f1")
    if stage in (45, 46):
        ckpt("x1")
    if stage >= 5:
        emit_attn()
    if stage == 46:
        ckpt("x2")
    if stage >= 6 and stage != 45:
        emit_ffn("f3")

    fin = DMA("sp", out.rearrange("(tt p) d -> p tt d", p=128),
              XS[:, :].rearrange("p (tt d) -> p tt d", d=1024),
              reads=[("XS", 0, TTN * 1024)], writes=[("OUT", 0, 1)])
    tap_toks = [fin]
    allw = [(s, P.val[s]) for s in SEMNAMES]
    _sbufs = {"WGB": WGB[:, :], "WGF": WGF[:, :], "BUFB": BUF_B[:, :], "BUFA": BUF_A[:, :], "LSTG": LSTG[:, :], "TOKIDX": TOKIDX[:, :], "GTB": GTB[:, :], "C0COL": C0COL[:, :], "MYSLOT": MYSLOT[:, :], "MYGATE": MYGATE[:, :], "IOTA8T": IOTA8T[:, :], "UT": UT[:, :]}
    sbuf_taps = {n: _sbufs[n] for n in (sbuf_taps or ())}
    for name, ap in (sbuf_taps or {}).items():
        shp = [ap.shape[0], ap.free_size()]
        tp = nc.dram_tensor("stap_" + name, shp, ap.dtype, kind="ExternalOutput").ap()
        tap_toks.append(P.op("pool", lambda ec, tp=tp, ap=ap: ec.dma_start(out=tp, in_=ap),
                             waits=allw, kind="dma"))
    for name in taps:
        srct = dram_by_name[name]
        tp = nc.dram_tensor("tap_" + name, list(srct.tensor.shape), srct.dtype, kind="ExternalOutput").ap()
        tap_toks.append(DMA("pool", tp, srct, waits=allw))
    P.nop_op("sp", lambda ec: ec.nop(), waits=tap_toks + [(s, P.val[s]) for s in SEMNAMES])
    P.phase()

    with nc.Block() as block:
        n = P.emit(block, sems, nc)
    es.close()
    return nc


def host_inputs(L0_attn_Q, L0_attn_rot_w, L0_attn_scale, L0_attn_bias,
                L0_ffn_wg, L0_ffn_w1, L0_ffn_b1, L0_ffn_w2, L0_ffn_b2):
    Q = np.ascontiguousarray(np.asarray(L0_attn_Q, dtype=np.float32))
    rw = np.ascontiguousarray(np.asarray(L0_attn_rot_w, dtype=np.float32)).reshape(1, 3 * D)
    sc = np.asarray(L0_attn_scale, dtype=np.float32).reshape(1, 1)
    wg = np.ascontiguousarray(np.asarray(L0_ffn_wg, dtype=np.float32)).reshape(DCN, 128, 8)
    w1 = np.ascontiguousarray(np.asarray(L0_ffn_w1, dtype=np.float32))
    b1 = np.ascontiguousarray(np.asarray(L0_ffn_b1, dtype=np.float32))
    w2 = np.ascontiguousarray(np.asarray(L0_ffn_w2, dtype=np.float32))
    b2 = np.ascontiguousarray(np.asarray(L0_ffn_b2, dtype=np.float32))

    ut = np.triu(np.ones((128, 128), np.float32), 1)
    onesrow = np.ones((1, 128), np.float32)
    ones128 = np.ones((128, 1), np.float32)
    identf = np.eye(128, dtype=np.float32)
    iotap = np.arange(128, dtype=np.float32).reshape(128, 1)
    iota8t = np.tile(np.arange(8, dtype=np.float32), (128, 128))
    iota128 = np.tile(np.arange(128, dtype=np.float32), (128, 1))
    scol32 = (np.arange(128, dtype=np.float32)[:, None] + 128.0 * np.arange(32, dtype=np.float32)[None, :])
    diag0 = np.eye(128, dtype=np.float32)
    diagu = np.zeros((128, 128), np.float32)
    diagl = np.zeros((128, 128), np.float32)
    for i in range(64):
        diagu[2 * i, 2 * i + 1] = 1.0
        diagl[2 * i + 1, 2 * i] = 1.0

    in_maps = []
    for c in range(8):
        sel0b = np.zeros((128, 16), np.float32)
        sel0b[:, c] = 1.0
        sel1b = np.zeros((128, 16), np.float32)
        sel1b[:, 8 + c] = 1.0
        cval = np.full((128, 1), float(c), np.float32)
        myidx = (c * 128 + np.arange(128, dtype=np.int32)).reshape(128, 1)
        in_maps.append({
            "xq": Q[c], "rwf": rw, "scale1": sc, "wg3": wg,
            "w1c": w1[c], "w2c": w2[c], "b1c": b1[c].reshape(HTN, 128, 1), "b2c": b2[c].reshape(1, D),
            "ut_in": ut, "onesrow_in": onesrow, "ones128_in": ones128, "identf_in": identf,
            "iotap_in": iotap, "iota8t_in": iota8t, "iota128_in": iota128, "scol32_in": scol32, "diag0_in": diag0, "diagu_in": diagu, "diagl_in": diagl,
            "sel0b_in": sel0b, "sel1b_in": sel1b, "cval_in": cval, "myidx_in": myidx,
        })
    return in_maps


def kernel(L0_attn_Q, L0_attn_rot_w, L0_attn_scale, L0_attn_bias,
           L0_ffn_wg, L0_ffn_w1, L0_ffn_b1, L0_ffn_w2, L0_ffn_b2):
    in_maps = host_inputs(L0_attn_Q, L0_attn_rot_w, L0_attn_scale, L0_attn_bias,
                          L0_ffn_wg, L0_ffn_w1, L0_ffn_b1, L0_ffn_w2, L0_ffn_b2)
    nc = build_nc()
    res = run_bass_kernel_spmd(nc, in_maps, core_ids=list(range(8)))
    return np.stack([r["out"] for r in res.results], axis=0)


# version 04
# speedup vs baseline: 1.0067x; 1.0067x over previous
import numpy as np
import ml_dtypes
import concourse.bass as bass
import concourse.mybir as mybir
from concourse.bass_utils import run_bass_kernel_spmd
from concourse.bass_core_rust import IndirectOffsetOnAxis

F32 = mybir.dt.float32
BF16 = mybir.dt.bfloat16
I32 = mybir.dt.int32
ALU = mybir.AluOpType
AX = mybir.AxisListType
AF = mybir.ActivationFunctionType

T = 2048
TTN = 16
D = 1024
DCN = 8
NG = 16384
E = 8
CAP = 4096
H = 4096
HTN = 32
SCN = 8
STN = 32

ENGS = ("pe", "act", "dve", "pool", "sp")
ENG2BLOCK = {"pe": "tensor", "act": "scalar", "dve": "vector", "pool": "gpsimd", "sp": "sync"}

STAGE = 99
TAPS = ()


SEM_BASE = 1 << 24


NSPD = 12
NPLD = 12
COMPUTE_SEMS = ("pe", "act", "dve", "plc")
SEMNAMES = list(COMPUTE_SEMS) + ["cc"] + [f"spd{i}" for i in range(NSPD)] + [f"pld{i}" for i in range(NPLD)]


class Prog:

    def __init__(self):
        self.phases = []
        self.q = {e: [] for e in ENGS}
        self.val = {s: (SEM_BASE if s in COMPUTE_SEMS else 0) for s in SEMNAMES}
        self.rr = {"sp": 0, "pool": 0}
        self.res = {}

    def op(self, eng, fn, reads=(), writes=(), waits=(), inc=1, kind="compute"):
        need = dict()

        def add(tok):
            if tok is None:
                return
            s, v = tok
            if need.get(s, 0) < v:
                need[s] = v

        for w in waits:
            add(w)
        for (buf, lo, hi) in reads:
            for (l2, h2, tok, wr) in self.res.get(buf, ()):
                if wr and l2 < hi and lo < h2:
                    if tok[0] == "pe" and eng == "pe":
                        continue
                    add(tok)
        for (buf, lo, hi) in writes:
            for (l2, h2, tok, wr) in self.res.get(buf, ()):
                if l2 < hi and lo < h2:
                    if tok[0] == "pe" and eng == "pe" and wr:
                        continue
                    add(tok)
        if kind == "dma":
            assert eng in ("sp", "pool")
            i = self.rr[eng]
            self.rr[eng] = (i + 1) % (NSPD if eng == "sp" else NPLD)
            sname = (f"spd{i}" if eng == "sp" else f"pld{i}")
            add((sname, self.val[sname]))
            inc = 16
        elif kind == "cc":
            sname = "cc"
            inc = 1
        else:
            sname = {"pe": "pe", "act": "act", "dve": "dve", "pool": "plc", "sp": "plc"}[eng]
            assert eng != "sp", "sync engine has no compute ops with incs"
        self.val[sname] += inc
        tok = (sname, self.val[sname])
        self.q[eng].append((fn, tuple(sorted(need.items())), inc, sname))
        for (buf, lo, hi) in writes:
            ent = self.res.setdefault(buf, [])
            ent[:] = [x for x in ent if not (lo <= x[0] and x[1] <= hi)]
            ent.append((lo, hi, tok, True))
        for (buf, lo, hi) in reads:
            self.res.setdefault(buf, []).append((lo, hi, tok, False))
        return tok

    def nop_op(self, eng, fn, waits=()):
        self.q[eng].append((fn, tuple(sorted({s: v for (s, v) in waits}.items())), 0, None))

    def phase(self):
        if any(self.q[e] for e in ENGS):
            self.phases.append(self.q)
            self.q = {e: [] for e in ENGS}

    def emit(self, block, sems, nc=None):
        self.phase()

        @block.gpsimd
        def _(g):
            g.dma_reset()
            for s in SEMNAMES:
                g.sem_clear(sems[s])
            for s in COMPUTE_SEMS:
                g.sem_inc(sems[s], SEM_BASE)
        first = {e: True for e in ENGS}
        last_wait = {e: {s: 0 for s in SEMNAMES} for e in ENGS}
        for ph in self.phases:
            for eng in ENGS:
                ops = ph[eng]
                if not ops:
                    continue
                dec = getattr(block, ENG2BLOCK[eng])

                def body(ectx, ops=ops, eng=eng):
                    lw = last_wait[eng]
                    if first[eng]:
                        first[eng] = False
                        if eng in ("pe", "act", "dve"):
                            ectx.wait_ge(sems[eng], SEM_BASE)
                            lw[eng] = SEM_BASE
                        elif eng == "sp":
                            ectx.wait_ge(sems["dve"], SEM_BASE)
                            lw["dve"] = SEM_BASE
                    for fn, waits, inc, sname in ops:
                        for (ws, wval) in waits:
                            if lw[ws] >= wval:
                                continue
                            ectx.wait_ge(sems[ws], wval)
                            lw[ws] = wval
                        inst = fn(ectx)
                        if inc:
                            inst.then_inc(sems[sname], inc)

                dec(body)
        n = sum(len(ph[e]) for ph in self.phases for e in ENGS)
        return n


def build_nc(stage=None, taps=None, sbuf_taps=None, rstop=99):
    stage = STAGE if stage is None else stage
    taps = TAPS if taps is None else taps
    nc = bass.Bass("TRN2", target_bir_lowering=False, detect_race_conditions=False)
    P = Prog()

    def ein(name, shape, dtype):
        return nc.dram_tensor(name, shape, dtype, kind="ExternalInput").ap()

    xq = ein("xq", [T, D], F32)
    rwf = ein("rwf", [1, 3 * D], F32)
    scale1 = ein("scale1", [1, 1], F32)
    wg3 = ein("wg3", [DCN, 128, 8], F32)
    w1c = ein("w1c", [D, H], F32)
    w2c = ein("w2c", [H, D], F32)
    b1c = ein("b1c", [HTN, 128, 1], F32)
    b2c = ein("b2c", [1, D], F32)
    ut_in = ein("ut_in", [128, 128], F32)
    onesrow_in = ein("onesrow_in", [1, 128], F32)
    ones128_in = ein("ones128_in", [128, 1], F32)
    identf_in = ein("identf_in", [128, 128], F32)
    iotap_in = ein("iotap_in", [128, 1], F32)
    iota8t_in = ein("iota8t_in", [128, 1024], F32)
    iota128_in = ein("iota128_in", [128, 128], F32)
    scol32_in = ein("scol32_in", [128, 32], F32)
    diag0_in = ein("diag0_in", [128, 128], F32)
    diagu_in = ein("diagu_in", [128, 128], F32)
    diagl_in = ein("diagl_in", [128, 128], F32)
    sel0b_in = ein("sel0b_in", [128, 16], F32)
    sel1b_in = ein("sel1b_in", [128, 16], F32)
    cval_in = ein("cval_in", [128, 1], F32)
    myidx_in = ein("myidx_in", [128, 1], I32)

    out = nc.dram_tensor("out", [T, D], F32, kind="ExternalOutput").ap()

    xln_dram = nc.dram_tensor("xln_dram", [T, D], BF16).ap()
    agx_out = nc.dram_tensor("agx_out", [NG, D], BF16).ap()
    logit_dram = nc.dram_tensor("logit_dram", [T, 8], F32).ap()
    agl_out = nc.dram_tensor("agl_out", [NG, 8], F32).ap()
    pack_dram = nc.dram_tensor("pack_dram", [2048], F32).ap()
    off_dram = nc.dram_tensor("off_dram", [2048], F32).ap()
    gt_dram = nc.dram_tensor("gt_dram", [2, 128], F32).ap()
    ginc0_dram = nc.dram_tensor("ginc0_dram", [NG], F32).ap()
    ginc1_dram = nc.dram_tensor("ginc1_dram", [NG], F32).ap()
    slots_dram = nc.dram_tensor("slots_dram", [NG // 16, 32], F32).ap()
    gates_dram = nc.dram_tensor("gates_dram", [NG // 16, 32], F32).ap()
    myslots_dram = nc.dram_tensor("myslots_dram", [T, 2], F32).ap()
    mygates_dram = nc.dram_tensor("mygates_dram", [T, 2], F32).ap()
    disp_dram = nc.dram_tensor("disp_dram", [CAP, D], BF16).ap()
    y_dram = nc.dram_tensor("y_dram", [CAP, D], BF16).ap()
    agy_out = nc.dram_tensor("agy_out", [SCN * E * 512, D], BF16).ap()
    aln2_dram = nc.dram_tensor("aln2_dram", [T, D], BF16).ap()
    a_dram = nc.dram_tensor("a_dram", [T, T], BF16).ap()
    mb_dram = nc.dram_tensor("mb_dram", [6, 1024], F32).ap()
    w1t_dram = nc.dram_tensor("w1t_dram", [HTN, 128, DCN * 128], BF16).ap()
    w2bf_dram = nc.dram_tensor("w2bf_dram", [H, D], BF16).ap()
    dram_by_name = {t.tensor.name: t for t in [
        xln_dram, agx_out, logit_dram, agl_out, slots_dram, gates_dram, myslots_dram,
        mygates_dram, disp_dram, y_dram, agy_out, aln2_dram, a_dram]}

    from contextlib import ExitStack
    es = ExitStack()

    def sbt(name, shape, dtype):
        return es.enter_context(nc.sbuf_tensor(name, shape, dtype))

    XS = sbt("XS", [128, TTN * 1024], F32)
    BUF_A = sbt("BUF_A", [128, 16384], BF16)
    BUF_B = sbt("BUF_B", [128, 16384], BF16)
    BUF_C = sbt("BUF_C", [128, 16384], BF16)
    ASTG = sbt("ASTG", [128, 2048], F32)
    XC = sbt("XC", [128, 1024], F32)
    MTBUF = sbt("MTBUF", [128, 1024], BF16)
    GVTBUF = sbt("GVTBUF", [128, 1024], BF16)
    B2B = sbt("B2B", [128, 1024], F32)
    B1 = sbt("B1", [128, HTN], F32)
    WGB = sbt("WGB", [128, 64], BF16)
    WGF = sbt("WGF", [128, 64], F32)
    UT = sbt("UT", [128, 128], F32)
    ONESROW = sbt("ONESROW", [1, 128], F32)
    ONES128 = sbt("ONES128", [128, 1], F32)
    IDENTF = sbt("IDENTF", [128, 128], F32)
    IDENTB = sbt("IDENTB", [128, 128], BF16)
    IOTAP = sbt("IOTAP", [128, 1], F32)
    IOTA8T = sbt("IOTA8T", [128, 1024], F32)
    IOTA128T = sbt("IOTA128T", [128, 128], F32)
    DIAG0 = sbt("DIAG0", [128, 128], F32)
    DIAGU = sbt("DIAGU", [128, 128], F32)
    DIAGL = sbt("DIAGL", [128, 128], F32)
    SEL0B = sbt("SEL0B", [128, 16], F32)
    SEL1B = sbt("SEL1B", [128, 16], F32)
    CVAL = sbt("CVAL", [128, 1], F32)
    MYIDX = sbt("MYIDX", [128, 1], I32)
    RW = sbt("RW", [1, D], F32)
    RSC = sbt("RSC", [1, 2048], F32)
    SCB = sbt("SCB", [1, 8], F32)
    ALPHAC = sbt("ALPHAC", [128, 1], F32)
    SGNPOS = sbt("SGNPOS", [128, 1], F32)
    COLS = sbt("COLS", [128, 32], F32)
    LSTG = sbt("LSTG", [128, 128], F32)
    XLNBF = sbt("XLNBF", [128, 1024], BF16)
    TTILE = sbt("TTILE", [128, 16], F32)
    TPEX = sbt("TPEX", [128, 16], F32)
    CROW = sbt("CROW", [1, 16], F32)
    TROW = RSC
    GTB = sbt("GTB", [128, 256], F32)
    C0COL = sbt("C0COL", [128, 1], F32)
    TOKIDX = sbt("TOKIDX", [128, 32], I32)
    MYSLOT = sbt("MYSLOT", [128, 32], F32)
    MYSLOTI = sbt("MYSLOTI", [128, 32], I32)
    MYGATE = sbt("MYGATE", [128, 32], F32)
    SRCH = sbt("SRCH", [128, 512], F32)
    SRCHI = sbt("SRCHI", [128, 64], I32)
    SCOL32 = sbt("SCOL32", [128, 32], F32)
    MYRAW = sbt("MYRAW", [128, 64], F32)
    EPSC = sbt("EPSC", [128, 1], F32)

    AF32 = BUF_A[:, :].bitcast(F32)
    BF32 = BUF_B[:, :].bitcast(F32)
    LG = AF32[:, 0:1024]
    MARR = AF32[:, 1024:3072]
    GEXI = AF32[:, 3072:5120]
    GEX = AF32[:, 5120:7168]
    SCR1 = AF32[:, 7168:8192]
    OH1 = BF32[:, 0:1024]
    OH2 = BF32[:, 1024:2048]
    SCR2 = BF32[:, 2048:4096]
    RS2 = BF32[:, 4096:4352]
    RG2 = BF32[:, 4352:4608]
    RSMALL = BF32[:, 4608:6656]

    def RS(i, w=128):
        return RSMALL[:, i:i + w]

    PS0 = es.enter_context(nc.psum_tensor("ps0", [128, 2048], F32))
    PS1 = es.enter_context(nc.psum_tensor("ps1", [128, 2048], F32))
    PSROW = PS1[0:1, :]

    sems = {s: es.enter_context(nc.semaphore("sem_" + s)) for s in SEMNAMES}

    def rA(lo, hi):
        return ("A", lo, hi)

    def rB(lo, hi):
        return ("B", lo, hi)

    def rC(lo, hi):
        return ("C", lo, hi)

    def rXS(tt):
        return ("XS", tt * 1024, (tt + 1) * 1024)

    def rP0(lo=0, hi=2048):
        return ("P0", lo, hi)

    def rP1(lo=0, hi=2048):
        return ("P1", lo, hi)

    def rD(name, lo=0, hi=1 << 40):
        return ("D_" + name, lo, hi)

    def rS(name, lo=0, hi=1 << 40):
        return ("S_" + name, lo, hi)

    def DMA(eng, out_ap, in_ap, reads=(), writes=(), waits=(), **kw):
        return P.op(eng, lambda ec, o=out_ap, i=in_ap, kw=kw: ec.dma_start(out=o, in_=i, **kw),
                    reads=reads, writes=writes, waits=waits, kind="dma")

    DMA("sp", XS[:, :].rearrange("p (tt d) -> p tt d", d=1024),
        xq.rearrange("(tt p) d -> p tt d", p=128),
        writes=[("XS", 0, TTN * 1024)])
    for dst, src, key in [
            (UT[:, :], ut_in[:, :], "UT"),
            (ONESROW[:, :], onesrow_in[:, :], "ONESROW"), (ONES128[:, :], ones128_in[:, :], "ONES128"),
            (IDENTF[:, :], identf_in[:, :], "IDENTF"), (IOTAP[:, :], iotap_in[:, :], "IOTAP"),
            (IOTA8T[:, :], iota8t_in[:, :], "IOTA8T"), (IOTA128T[:, :], iota128_in[:, :], "IOTA128T"), (SCOL32[:, :], scol32_in[:, :], "SCOL32"), (DIAG0[:, :], diag0_in[:, :], "DIAG0"),
            (DIAGU[:, :], diagu_in[:, :], "DIAGU"), (DIAGL[:, :], diagl_in[:, :], "DIAGL"),
            (SEL0B[:, :], sel0b_in[:, :], "SEL0B"), (SEL1B[:, :], sel1b_in[:, :], "SEL1B"),
            (CVAL[:, :], cval_in[:, :], "CVAL"), (MYIDX[:, :], myidx_in[:, :], "MYIDX")]:
        DMA("sp", dst, src, writes=[rS(key)])
    DMA("sp", SCB[:, 0:1], scale1[:, :], writes=[rS("SCB", 0, 1)])
    for ht in range(HTN):
        DMA("sp", B1[:, ht:ht + 1], b1c[ht, :, :], writes=[rS("B1", ht, ht + 1)])
    for dc in range(DCN):
        DMA("sp", WGF[:, dc * 8:(dc + 1) * 8], wg3[dc, :, :],
            writes=[rS("WGF", dc * 8, (dc + 1) * 8)])
    P.op("dve", lambda ec: ec.tensor_copy(out=WGB[:, :], in_=WGF[:, :]),
         reads=[rS("WGF", 0, 64)], writes=[rS("WGB")])

    P.op("dve", lambda ec: ec.memset(EPSC[:, :], 1e-5), writes=[rS("EPSC")])
    BREG = {}

    def _setup_pool_regs(ec):
        r1 = ec.alloc_register("bnd127")
        ec.reg_mov(r1, 127)
        BREG["b127"] = r1
        r2 = ec.alloc_register("bndNG")
        ec.reg_mov(r2, NG - 1)
        BREG["bNG"] = r2
        return ec.nop()

    P.nop_op("pool", _setup_pool_regs)
    if rstop != 99:
        P.op("pool", lambda ec: ec.memset(BUF_A[:, :], 0.0), writes=[rA(0, 16384)])
        P.op("pool", lambda ec: ec.memset(BUF_B[:, :], 0.0), writes=[rB(0, 16384)])

    P.op("dve", lambda ec: ec.tensor_copy(out=IDENTB[:, :], in_=IDENTF[:, :]),
         reads=[rS("IDENTF")], writes=[rS("IDENTB")])

    DMA("sp", RSC[:, 0:1024], b2c[:, :], writes=[rS("RSC", 0, 1024)])
    for j in range(2):
        P.op("pe", lambda ec, j=j: ec.matmul(out=PS0[:, j * 512:(j + 1) * 512], lhsT=ONESROW[:, :],
                                             rhs=RSC[:, j * 512:(j + 1) * 512], start=True, stop=True),
             reads=[rS("ONESROW"), rS("RSC", j * 512, (j + 1) * 512)], writes=[rP0(j * 512, (j + 1) * 512)])
    P.op("dve", lambda ec: ec.tensor_copy(out=B2B[:, :], in_=PS0[:, 0:1024]),
         reads=[rP0(0, 1024)], writes=[rS("B2B")])

    P.op("dve", lambda ec: ec.reciprocal(out=SCB[:, 1:2], in_=SCB[:, 0:1]),
         reads=[rS("SCB", 0, 1)], writes=[rS("SCB", 1, 2)])
    P.op("dve", lambda ec: ec.tensor_single_scalar(out=SCB[:, 2:3], in_=SCB[:, 1:2], scalar=2.0, op=ALU.mult),
         reads=[rS("SCB", 1, 2)], writes=[rS("SCB", 2, 3)])
    P.op("dve", lambda ec: ec.tensor_single_scalar(out=SCB[:, 3:4], in_=SCB[:, 0:1], scalar=0.0, op=ALU.is_gt),
         reads=[rS("SCB", 0, 1)], writes=[rS("SCB", 3, 4)])
    P.op("pe", lambda ec: ec.matmul(out=PS1[:, 0:1], lhsT=ONESROW[:, :], rhs=SCB[:, 2:3], start=True, stop=True),
         reads=[rS("ONESROW"), rS("SCB", 2, 3)], writes=[rP1(0, 1)])
    P.op("pe", lambda ec: ec.matmul(out=PS1[:, 1:2], lhsT=ONESROW[:, :], rhs=SCB[:, 3:4], start=True, stop=True),
         reads=[rS("ONESROW"), rS("SCB", 3, 4)], writes=[rP1(1, 2)])
    P.op("dve", lambda ec: ec.tensor_copy(out=ALPHAC[:, :], in_=PS1[:, 0:1]),
         reads=[rP1(0, 1)], writes=[rS("ALPHAC")])
    P.op("dve", lambda ec: ec.tensor_copy(out=SGNPOS[:, :], in_=PS1[:, 1:2]),
         reads=[rP1(1, 2)], writes=[rS("SGNPOS")])

    def prep_cs(r, cdst, sdst):
        DMA("sp", RW[:, :], rwf[:, r * 1024:(r + 1) * 1024], writes=[rS("RW")])
        g0 = RW[:, :].rearrange("o (i two) -> o i two", two=2)[:, :, 0:1].squeeze(2)
        g1 = RW[:, :].rearrange("o (i two) -> o i two", two=2)[:, :, 1:2].squeeze(2)
        P.op("dve", lambda ec: ec.tensor_mul(out=RSC[:, 1024:1536], in0=g0, in1=g0),
             reads=[rS("RW")], writes=[rS("RSC", 1024, 1536)])
        P.op("dve", lambda ec: ec.tensor_mul(out=RSC[:, 1536:2048], in0=g1, in1=g1),
             reads=[rS("RW")], writes=[rS("RSC", 1536, 2048)])
        P.op("dve", lambda ec: ec.tensor_add(out=RSC[:, 1024:1536], in0=RSC[:, 1024:1536], in1=RSC[:, 1536:2048]),
             reads=[rS("RSC", 1024, 2048)], writes=[rS("RSC", 1024, 1536)])
        P.op("act", lambda ec: ec.activation(out=RSC[:, 1536:2048], in_=RSC[:, 1024:1536], func=AF.Sqrt),
             reads=[rS("RSC", 1024, 1536)], writes=[rS("RSC", 1536, 2048)])
        P.op("dve", lambda ec: ec.reciprocal(out=RSC[:, 1024:1536], in_=RSC[:, 1536:2048]),
             reads=[rS("RSC", 1536, 2048)], writes=[rS("RSC", 1024, 1536)])
        P.op("dve", lambda ec, cdst=cdst: ec.tensor_mul(out=RSC[:, cdst:cdst + 512], in0=g0, in1=RSC[:, 1024:1536]),
             reads=[rS("RW"), rS("RSC", 1024, 1536)], writes=[rS("RSC", cdst, cdst + 512)])
        P.op("dve", lambda ec, sdst=sdst: ec.tensor_mul(out=RSC[:, sdst:sdst + 512], in0=g1, in1=RSC[:, 1024:1536]),
             reads=[rS("RW"), rS("RSC", 1024, 1536)], writes=[rS("RSC", sdst, sdst + 512)])

    prep_cs(0, 0, 512)
    P.op("dve", lambda ec: ec.tensor_copy(out=ASTG[0:1, 0:1024], in_=RSC[:, 0:1024]),
         reads=[rS("RSC", 0, 1024)], writes=[("ASTG", 0, 1024)])
    prep_cs(1, 0, 512)
    P.op("dve", lambda ec: ec.tensor_mul(out=RSC[:, 1024:1536], in0=ASTG[0:1, 0:512], in1=RSC[:, 0:512]),
         reads=[("ASTG", 0, 512), rS("RSC", 0, 512)], writes=[rS("RSC", 1024, 1536)])
    P.op("dve", lambda ec: ec.tensor_mul(out=RSC[:, 1536:2048], in0=ASTG[0:1, 512:1024], in1=RSC[:, 512:1024]),
         reads=[("ASTG", 512, 1024), rS("RSC", 512, 1024)], writes=[rS("RSC", 1536, 2048)])
    P.op("dve", lambda ec: ec.tensor_add(out=RSC[:, 1024:1536], in0=RSC[:, 1024:1536], in1=RSC[:, 1536:2048]),
         reads=[rS("RSC", 1024, 2048)], writes=[rS("RSC", 1024, 1536)])
    P.op("dve", lambda ec: ec.tensor_mul(out=RSC[:, 1536:2048], in0=ASTG[0:1, 0:512], in1=RSC[:, 512:1024]),
         reads=[("ASTG", 0, 512), rS("RSC", 512, 1024)], writes=[rS("RSC", 1536, 2048)])
    P.op("dve", lambda ec: ec.tensor_mul(out=RSC[:, 0:512], in0=ASTG[0:1, 512:1024], in1=RSC[:, 0:512]),
         reads=[("ASTG", 512, 1024), rS("RSC", 0, 512)], writes=[rS("RSC", 0, 512)])
    P.op("dve", lambda ec: ec.tensor_sub(out=RSC[:, 1536:2048], in0=RSC[:, 1536:2048], in1=RSC[:, 0:512]),
         reads=[rS("RSC", 0, 512), rS("RSC", 1536, 2048)], writes=[rS("RSC", 1536, 2048)])

    def write_rep2(mbrow, src_lo, negate=False):
        tgt = ASTG[0:1, 1024:2048].rearrange("o (i two) -> o i two", two=2)
        if negate:
            P.op("dve", lambda ec: ec.tensor_single_scalar(out=ASTG[0:1, 512:1024], in_=RSC[:, src_lo:src_lo + 512],
                                                           scalar=-1.0, op=ALU.mult),
                 reads=[rS("RSC", src_lo, src_lo + 512)], writes=[("ASTG", 512, 1024)])
            srcap = ASTG[0:1, 512:1024]
            srcres = ("ASTG", 512, 1024)
        else:
            srcap = RSC[:, src_lo:src_lo + 512]
            srcres = rS("RSC", src_lo, src_lo + 512)
        P.op("dve", lambda ec: ec.tensor_copy(out=tgt[:, :, 0:1].squeeze(2), in_=srcap),
             reads=[srcres], writes=[("ASTG", 1024, 2048)])
        P.op("dve", lambda ec: ec.tensor_copy(out=tgt[:, :, 1:2].squeeze(2), in_=srcap),
             reads=[srcres], writes=[("ASTG", 1024, 2048)])
        return DMA("sp", mb_dram[mbrow:mbrow + 1, :], ASTG[0:1, 1024:2048],
                   reads=[("ASTG", 1024, 2048)], writes=[rD("mb", mbrow * 1024, (mbrow + 1) * 1024)])

    write_rep2(0, 1024)
    write_rep2(1, 1536)
    write_rep2(2, 1536, negate=True)
    prep_cs(2, 0, 512)
    write_rep2(3, 0)
    write_rep2(4, 512)
    write_rep2(5, 512, negate=True)

    for rows, dstbuf, key in [((0, 1, 2), MTBUF, "MTBUF"), ((3, 4, 5), GVTBUF, "GVTBUF")]:
        for dc in range(DCN):
            for j, r in enumerate(rows):
                DMA("sp", SRCH[:, 384 + j:385 + j], mb_dram[r:r + 1, dc * 128:(dc + 1) * 128].rearrange("o p -> (o p)").unsqueeze(1),
                    reads=[rD("mb", r * 1024 + dc * 128, r * 1024 + (dc + 1) * 128)],
                    writes=[rS("SRCH", 384 + j, 385 + j)])
            P.op("dve", lambda ec: ec.tensor_scalar(out=SRCH[:, 0:128], in0=DIAG0[:, :], scalar1=SRCH[:, 384:385],
                                                    scalar2=None, op0=ALU.mult),
                 reads=[rS("DIAG0"), rS("SRCH", 384, 385)], writes=[rS("SRCH", 0, 128)])
            P.op("dve", lambda ec: ec.tensor_scalar(out=SRCH[:, 128:256], in0=DIAGU[:, :], scalar1=SRCH[:, 385:386],
                                                    scalar2=None, op0=ALU.mult),
                 reads=[rS("DIAGU"), rS("SRCH", 385, 386)], writes=[rS("SRCH", 128, 256)])
            P.op("dve", lambda ec: ec.tensor_add(out=SRCH[:, 0:128], in0=SRCH[:, 0:128], in1=SRCH[:, 128:256]),
                 reads=[rS("SRCH", 0, 256)], writes=[rS("SRCH", 0, 128)])
            P.op("dve", lambda ec: ec.tensor_scalar(out=SRCH[:, 128:256], in0=DIAGL[:, :], scalar1=SRCH[:, 386:387],
                                                    scalar2=None, op0=ALU.mult),
                 reads=[rS("DIAGL"), rS("SRCH", 386, 387)], writes=[rS("SRCH", 128, 256)])
            P.op("dve", lambda ec: ec.tensor_add(out=SRCH[:, 0:128], in0=SRCH[:, 0:128], in1=SRCH[:, 128:256]),
                 reads=[rS("SRCH", 0, 256)], writes=[rS("SRCH", 0, 128)])
            P.op("dve", lambda ec, dc=dc, dstbuf=dstbuf, key=key: ec.tensor_copy(
                out=dstbuf[:, dc * 128:(dc + 1) * 128], in_=SRCH[:, 0:128]),
                reads=[rS("SRCH", 0, 128)], writes=[rS(key, dc * 128, (dc + 1) * 128)])
    P.phase()

    for ht in range(HTN):
        src = w1c.rearrange("(dc p) (ht h) -> ht p dc h", p=128, h=128)[ht]
        DMA("pool", w1t_dram[ht].rearrange("p (dc h) -> p dc h", h=128), src,
            writes=[rD("w1t", ht, ht + 1)])
    for i in range(8):
        DMA("pool", w2bf_dram[i * 512:(i + 1) * 512, :], w2c[i * 512:(i + 1) * 512, :],
            writes=[rD("w2bf", i * 512, (i + 1) * 512)])
    P.phase()

    def emit_ln(dram_dst, dname):
        toks = []
        XCB = [(XC[:, :], ("XC", 0, 1024)), (BUF_C[:, :].bitcast(F32)[:, 0:1024], rC(0, 2048))]
        for tt in range(TTN):
            xt = XS[:, tt * 1024:(tt + 1) * 1024]
            XCt, xcres = XCB[tt % 2]
            cb = (tt % 2) * 8
            P.op("dve", lambda ec, xt=xt, cb=cb: ec.reduce_sum(out=COLS[:, cb:cb + 1], in_=xt, axis=AX.X),
                 reads=[rXS(tt)], writes=[("COLS", cb, cb + 1)])
            P.op("dve", lambda ec, cb=cb: ec.tensor_single_scalar(out=COLS[:, cb + 1:cb + 2], in_=COLS[:, cb:cb + 1],
                                                                  scalar=1.0 / 1024, op=ALU.mult),
                 reads=[("COLS", cb, cb + 1)], writes=[("COLS", cb + 1, cb + 2)])
            P.op("dve", lambda ec, xt=xt, cb=cb, XCt=XCt: ec.tensor_scalar(out=XCt, in0=xt, scalar1=COLS[:, cb + 1:cb + 2],
                                                                  scalar2=None, op0=ALU.subtract),
                 reads=[rXS(tt), ("COLS", cb + 1, cb + 2)], writes=[xcres])
            P.op("act", lambda ec, cb=cb, tt=tt, XCt=XCt: ec.activation(out=ASTG[:, (tt % 2) * 1024:(tt % 2) * 1024 + 1024],
                                                               in_=XCt, func=AF.Square,
                                                               accum_out=COLS[:, cb + 2:cb + 3]),
                 reads=[xcres],
                 writes=[("ASTG", (tt % 2) * 1024, (tt % 2) * 1024 + 1024), ("COLS", cb + 2, cb + 3)])
            P.op("act", lambda ec, cb=cb: ec.activation(out=COLS[:, cb + 3:cb + 4], in_=COLS[:, cb + 2:cb + 3],
                                                        func=AF.Sqrt, bias=EPSC[:, 0:1], scale=1.0 / 1024),
                 reads=[("COLS", cb + 2, cb + 3), rS("EPSC")], writes=[("COLS", cb + 3, cb + 4)])
            P.op("dve", lambda ec, cb=cb: ec.reciprocal(out=COLS[:, cb + 4:cb + 5], in_=COLS[:, cb + 3:cb + 4]),
                 reads=[("COLS", cb + 3, cb + 4)], writes=[("COLS", cb + 4, cb + 5)])
            xlb = XLNBF[:, :] if tt % 2 == 0 else BUF_C[:, 2048:3072]
            xlbres = ("XLNBF", 0, 1024) if tt % 2 == 0 else rC(2048, 3072)
            P.op("dve", lambda ec, cb=cb, XCt=XCt, xlb=xlb: ec.tensor_scalar(out=xlb, in0=XCt, scalar1=COLS[:, cb + 4:cb + 5],
                                                           scalar2=None, op0=ALU.mult),
                 reads=[xcres, ("COLS", cb + 4, cb + 5)], writes=[xlbres])
            toks.append(DMA("sp", dram_dst[tt * 128:(tt + 1) * 128, :], xlb,
                            reads=[xlbres], writes=[rD(dname, tt * 128, (tt + 1) * 128)]))
        return toks

    def emit_ag(in_ap, out_ap, rkey, wkey):
        return P.op("pool", lambda ec, i=in_ap, o=out_ap: ec.collective_compute(
            "AllGather", ALU.bypass, replica_groups=[list(range(8))], ins=[i], outs=[o]),
            reads=[rkey], writes=[wkey], kind="cc")

    def emit_ffn(tag):
        emit_ln(xln_dram, "xln")
        P.phase()
        for dc in range(DCN):
            DMA("sp", BUF_B[:, dc * 2048:(dc + 1) * 2048], xln_dram[:, dc * 128:(dc + 1) * 128],
                reads=[rD("xln")], writes=[rB(dc * 2048, (dc + 1) * 2048)], transpose=True)
        for tt in range(TTN):
            for dc in range(DCN):
                P.op("pe", lambda ec, tt=tt, dc=dc: ec.matmul(
                    out=PS0[:, 0:8],
                    lhsT=BUF_B[:, dc * 2048 + tt * 128: dc * 2048 + (tt + 1) * 128],
                    rhs=WGB[:, dc * 8:(dc + 1) * 8],
                    start=(dc == 0), stop=(dc == DCN - 1)),
                    reads=[rB(dc * 2048 + tt * 128, dc * 2048 + (tt + 1) * 128), rS("WGB")],
                    writes=[rP0(0, 8)])
            P.op("dve", lambda ec, tt=tt: ec.tensor_copy(out=LSTG[:, tt * 8:(tt + 1) * 8], in_=PS0[:, 0:8]),
                 reads=[rP0(0, 8)], writes=[("LSTG", tt * 8, (tt + 1) * 8)])
        DMA("sp", logit_dram.rearrange("(tt p) e -> p tt e", p=128),
            LSTG[:, :].rearrange("p (tt e) -> p tt e", e=8),
            reads=[("LSTG", 0, 128)], writes=[rD("logit")])
        P.phase()
        emit_ag(logit_dram[:, :], agl_out[:, :], rD("logit"), rD("agl"))
        P.phase()
        emit_ag(xln_dram[:, :], agx_out[:, :], rD("xln"), rD("agx"))
        P.phase()
        if stage <= 1:
            return

        DMA("sp", LG.rearrange("p (tau e) -> p tau e", e=8), agl_out.rearrange("(tau p) e -> p tau e", p=128),
            reads=[rD("agl")], writes=[rA(0, 2048)])
        lg3 = LG.rearrange("p (g e) -> p g e", e=8)
        io3 = IOTA8T[:, :].rearrange("p (g e) -> p g e", e=8)
        sc3 = SCR1[:, :].rearrange("p (g e) -> p g e", e=8)
        P.op("dve", lambda ec: ec.reduce_max(out=RS(1536), in_=lg3, axis=AX.X),
             reads=[rA(0, 2048)], writes=[rB(12288, 12544)])
        if rstop == 1:
            return
        P.op("dve", lambda ec: ec.tensor_tensor(out=sc3, in0=lg3, in1=RS(1536).unsqueeze(2).to_broadcast([128, 128, 8]),
                                                op=ALU.is_equal),
             reads=[rA(0, 2048), rB(12288, 12544)], writes=[rA(14336, 16384)])
        if rstop == 2:
            return
        P.op("dve", lambda ec: ec.scalar_tensor_tensor(out=SCR1[:, :], in0=SCR1[:, :], scalar=-16.0, in1=IOTA8T[:, :],
                                                       op0=ALU.mult, op1=ALU.add),
             reads=[rA(14336, 16384), rS("IOTA8T")], writes=[rA(14336, 16384)])
        if rstop == 3:
            return
        P.op("dve", lambda ec: ec.tensor_reduce(out=RS(0), in_=sc3, axis=AX.X, op=ALU.min),
             reads=[rA(14336, 16384)], writes=[rB(9216, 9472)])
        if rstop == 4:
            return
        P.op("dve", lambda ec: ec.tensor_single_scalar(out=RS(0), in_=RS(0), scalar=16.0, op=ALU.add),
             reads=[rB(9216, 9472)], writes=[rB(9216, 9472)])
        P.op("dve", lambda ec: ec.tensor_tensor(out=OH1[:, :].rearrange("p (g e) -> p g e", e=8), in0=io3,
                                                in1=RS(0).unsqueeze(2).to_broadcast([128, 128, 8]), op=ALU.is_equal),
             reads=[rS("IOTA8T"), rB(9216, 9472)], writes=[rB(0, 2048)])
        P.op("dve", lambda ec: ec.scalar_tensor_tensor(out=SCR1[:, :], in0=OH1[:, :], scalar=-1e30, in1=LG[:, :],
                                                       op0=ALU.mult, op1=ALU.add),
             reads=[rB(0, 2048), rA(0, 2048)], writes=[rA(14336, 16384)])
        P.op("dve", lambda ec: ec.reduce_max(out=RS(1664), in_=sc3, axis=AX.X),
             reads=[rA(14336, 16384)], writes=[rB(12544, 12800)])
        P.op("dve", lambda ec: ec.tensor_tensor(out=sc3, in0=sc3, in1=RS(1664).unsqueeze(2).to_broadcast([128, 128, 8]),
                                                op=ALU.is_equal),
             reads=[rA(14336, 16384), rB(12544, 12800)], writes=[rA(14336, 16384)])
        P.op("dve", lambda ec: ec.scalar_tensor_tensor(out=SCR1[:, :], in0=SCR1[:, :], scalar=-16.0, in1=IOTA8T[:, :],
                                                       op0=ALU.mult, op1=ALU.add),
             reads=[rA(14336, 16384), rS("IOTA8T")], writes=[rA(14336, 16384)])
        P.op("dve", lambda ec: ec.tensor_reduce(out=RS(128), in_=sc3, axis=AX.X, op=ALU.min),
             reads=[rA(14336, 16384)], writes=[rB(9472, 9728)])
        P.op("dve", lambda ec: ec.tensor_single_scalar(out=RS(128), in_=RS(128), scalar=16.0, op=ALU.add),
             reads=[rB(9472, 9728)], writes=[rB(9472, 9728)])
        P.op("dve", lambda ec: ec.tensor_tensor(out=OH2[:, :].rearrange("p (g e) -> p g e", e=8), in0=io3,
                                                in1=RS(128).unsqueeze(2).to_broadcast([128, 128, 8]), op=ALU.is_equal),
             reads=[rS("IOTA8T"), rB(9472, 9728)], writes=[rB(2048, 4096)])
        P.op("dve", lambda ec: ec.tensor_sub(out=RS(1792), in0=RS(1664), in1=RS(1536)),
             reads=[rB(12288, 12800)], writes=[rB(12800, 13056)])
        P.op("act", lambda ec: ec.activation(out=RS(256), in_=RS(1792), func=AF.Sigmoid, scale=-1.0),
             reads=[rB(12800, 13056)], writes=[rB(9728, 9984)])
        P.op("dve", lambda ec: ec.tensor_scalar(out=RS(384), in0=RS(256), scalar1=-1.0, scalar2=1.0,
                                                op0=ALU.mult, op1=ALU.add),
             reads=[rB(9728, 9984)], writes=[rB(9984, 10240)])
        m3 = MARR.rearrange("p (tau j) -> p tau j", j=16)
        P.op("dve", lambda ec: ec.tensor_copy(out=m3[:, :, 0:8], in_=OH1[:, :].rearrange("p (g e) -> p g e", e=8)),
             reads=[rB(0, 2048)], writes=[rA(2048, 6144)])
        P.op("dve", lambda ec: ec.tensor_copy(out=m3[:, :, 8:16], in_=OH2[:, :].rearrange("p (g e) -> p g e", e=8)),
             reads=[rB(2048, 4096)], writes=[rA(2048, 6144)])
        for j in range(4):
            P.op("pe", lambda ec, j=j: ec.matmul(out=PS0[:, j * 512:(j + 1) * 512], lhsT=UT[:, :],
                                                 rhs=MARR[:, j * 512:(j + 1) * 512], start=True, stop=True),
                 reads=[rS("UT"), rA(2048, 6144)], writes=[rP0(j * 512, (j + 1) * 512)])
        P.op("dve", lambda ec: ec.tensor_copy(out=GEXI[:, :], in_=PS0[:, :]),
             reads=[rP0()], writes=[rA(6144, 10240)])
        for j in range(4):
            P.op("pe", lambda ec, j=j: ec.matmul(out=PSROW[:, j * 512:(j + 1) * 512], lhsT=ONES128[:, :],
                                                 rhs=MARR[:, j * 512:(j + 1) * 512], start=True, stop=True),
                 reads=[rS("ONES128"), rA(2048, 6144)], writes=[rP1(j * 512, (j + 1) * 512)])
        P.op("dve", lambda ec: ec.tensor_copy(out=TROW[:, :], in_=PSROW[:, :]),
             reads=[rP1()], writes=[rS("RSC", 0, 2048)])
        DMA("sp", pack_dram.unsqueeze(0), TROW[:, :], reads=[rS("RSC", 0, 2048)], writes=[rD("pack")])
        DMA("sp", TTILE[:, :], pack_dram.rearrange("(tau j) -> tau j", j=16),
            reads=[rD("pack")], writes=[rS("TTILE")])
        P.op("pe", lambda ec: ec.matmul(out=PS1[:, 0:16], lhsT=UT[:, :], rhs=TTILE[:, :], start=True, stop=True),
             reads=[rS("UT"), rS("TTILE")], writes=[rP1(0, 16)])
        P.op("dve", lambda ec: ec.tensor_copy(out=TPEX[:, :], in_=PS1[:, 0:16]),
             reads=[rP1(0, 16)], writes=[rS("TPEX")])
        P.op("pe", lambda ec: ec.matmul(out=PS0[0:1, 0:16], lhsT=ONES128[:, :], rhs=TTILE[:, :], start=True, stop=True),
             reads=[rS("ONES128"), rS("TTILE")], writes=[rP0(0, 16)])
        P.op("dve", lambda ec: ec.tensor_copy(out=CROW[:, :], in_=PS0[0:1, 0:16]),
             reads=[rP0(0, 16)], writes=[rS("CROW")])
        P.op("dve", lambda ec: ec.tensor_add(out=SCR2[:, 0:16], in0=TPEX[:, :], in1=TTILE[:, :]),
             reads=[rS("TPEX"), rS("TTILE")], writes=[rB(4096, 4128)])
        P.op("dve", lambda ec: ec.tensor_mul(out=SCR2[:, 16:32], in0=SCR2[:, 0:16], in1=SEL0B[:, :]),
             reads=[rB(4096, 4128), rS("SEL0B")], writes=[rB(4128, 4160)])
        P.op("dve", lambda ec: ec.reduce_sum(out=SCR2[:, 48:49], in_=SCR2[:, 16:32], axis=AX.X),
             reads=[rB(4128, 4160)], writes=[rB(4192, 4194)])
        P.op("dve", lambda ec: ec.tensor_mul(out=SCR2[:, 16:32], in0=SCR2[:, 0:16], in1=SEL1B[:, :]),
             reads=[rB(4096, 4128), rS("SEL1B")], writes=[rB(4128, 4160)])
        P.op("dve", lambda ec: ec.reduce_sum(out=SCR2[:, 49:50], in_=SCR2[:, 16:32], axis=AX.X),
             reads=[rB(4128, 4160)], writes=[rB(4194, 4196)])
        DMA("sp", gt_dram[0:1, :].rearrange("o p -> (o p)").unsqueeze(1), SCR2[:, 48:49],
            reads=[rB(4192, 4194)], writes=[rD("gt", 0, 128)])
        DMA("sp", gt_dram[1:2, :].rearrange("o p -> (o p)").unsqueeze(1), SCR2[:, 49:50],
            reads=[rB(4194, 4196)], writes=[rD("gt", 128, 256)])
        DMA("sp", TROW[:, 0:128], gt_dram[0:1, :], reads=[rD("gt", 0, 128)], writes=[rS("RSC", 0, 128)])
        DMA("sp", TROW[:, 128:256], gt_dram[1:2, :], reads=[rD("gt", 128, 256)], writes=[rS("RSC", 128, 256)])
        P.op("pe", lambda ec: ec.matmul(out=PS1[:, 256:384], lhsT=ONESROW[:, :], rhs=TROW[:, 0:128],
                                        start=True, stop=True),
             reads=[rS("ONESROW"), rS("RSC", 0, 128)], writes=[rP1(256, 384)])
        P.op("pe", lambda ec: ec.matmul(out=PS1[:, 384:512], lhsT=ONESROW[:, :], rhs=TROW[:, 128:256],
                                        start=True, stop=True),
             reads=[rS("ONESROW"), rS("RSC", 128, 256)], writes=[rP1(384, 512)])
        P.op("dve", lambda ec: ec.tensor_copy(out=GTB[:, :], in_=PS1[:, 256:512]),
             reads=[rP1(256, 512)], writes=[rS("GTB")])
        P.op("dve", lambda ec: ec.tensor_mul(out=TROW[:, 256:272], in0=CROW[:, :], in1=SEL0B[0:1, :]),
             reads=[rS("CROW"), rS("SEL0B")], writes=[rS("RSC", 256, 272)])
        P.op("dve", lambda ec: ec.reduce_sum(out=TROW[:, 272:273], in_=TROW[:, 256:272], axis=AX.X),
             reads=[rS("RSC", 256, 272)], writes=[rS("RSC", 272, 273)])
        P.op("pe", lambda ec: ec.matmul(out=PS1[:, 512:513], lhsT=ONESROW[:, :], rhs=TROW[:, 272:273],
                                        start=True, stop=True),
             reads=[rS("ONESROW"), rS("RSC", 272, 273)], writes=[rP1(512, 513)])
        P.op("dve", lambda ec: ec.tensor_copy(out=C0COL[:, :], in_=PS1[:, 512:513]),
             reads=[rP1(512, 513)], writes=[rS("C0COL")])
        DMA("sp", off_dram.rearrange("(tau j) -> tau j", j=16), TPEX[:, :],
            reads=[rS("TPEX")], writes=[rD("off")])
        DMA("sp", TROW[:, :], off_dram.unsqueeze(0), reads=[rD("off")], writes=[rS("RSC", 0, 2048)])
        for j in range(4):
            P.op("pe", lambda ec, j=j: ec.matmul(out=PS0[:, j * 512:(j + 1) * 512], lhsT=ONESROW[:, :],
                                                 rhs=TROW[:, j * 512:(j + 1) * 512], start=True, stop=True),
                 reads=[rS("ONESROW"), rS("RSC", 0, 2048)], writes=[rP0(j * 512, (j + 1) * 512)])
        P.op("dve", lambda ec: ec.tensor_add(out=GEX[:, :], in0=GEXI[:, :], in1=PS0[:, :]),
             reads=[rA(6144, 10240), rP0()], writes=[rA(10240, 14336)])
        gex3 = GEX.rearrange("p (tau j) -> p tau j", j=16)
        P.op("dve", lambda ec: ec.tensor_tensor(out=sc3, in0=gex3[:, :, 0:8],
                                                in1=OH1[:, :].rearrange("p (g e) -> p g e", e=8), op=ALU.mult),
             reads=[rA(10240, 14336), rB(0, 2048)], writes=[rA(14336, 16384)])
        P.op("dve", lambda ec: ec.reduce_sum(out=RS(512), in_=sc3, axis=AX.X),
             reads=[rA(14336, 16384)], writes=[rB(10240, 10496)])
        P.op("dve", lambda ec: ec.tensor_tensor(out=sc3, in0=gex3[:, :, 8:16],
                                                in1=OH2[:, :].rearrange("p (g e) -> p g e", e=8), op=ALU.mult),
             reads=[rA(10240, 14336), rB(2048, 4096)], writes=[rA(14336, 16384)])
        P.op("dve", lambda ec: ec.reduce_sum(out=RS(640), in_=sc3, axis=AX.X),
             reads=[rA(14336, 16384)], writes=[rB(10496, 10752)])
        P.op("pe", lambda ec: ec.matmul(out=PS1[:, 516:524], lhsT=ONESROW[:, :], rhs=CROW[0:1, 0:8],
                                        start=True, stop=True),
             reads=[rS("ONESROW"), rS("CROW")], writes=[rP1(516, 524)])
        P.op("dve", lambda ec: ec.tensor_copy(out=SRCH[:, 0:8], in_=PS1[:, 516:524]),
             reads=[rP1(516, 524)], writes=[rS("SRCH", 0, 8)])
        P.op("dve", lambda ec: ec.tensor_tensor(out=sc3, in0=OH2[:, :].rearrange("p (g e) -> p g e", e=8),
                                                in1=SRCH[:, 0:8].unsqueeze(1).to_broadcast([128, 128, 8]),
                                                op=ALU.mult),
             reads=[rB(2048, 4096), rS("SRCH", 0, 8)], writes=[rA(14336, 16384)])
        P.op("dve", lambda ec: ec.reduce_sum(out=RS(1536), in_=sc3, axis=AX.X),
             reads=[rA(14336, 16384)], writes=[rB(12288, 12544)])
        P.op("dve", lambda ec: ec.tensor_add(out=RS(640), in0=RS(640), in1=RS(1536)),
             reads=[rB(12288, 12544), rB(10496, 10752)], writes=[rB(10496, 10752)])
        P.op("dve", lambda ec: ec.tensor_single_scalar(out=RS(768), in_=RS(512), scalar=float(CAP), op=ALU.is_lt),
             reads=[rB(10240, 10496)], writes=[rB(10752, 11008)])
        P.op("dve", lambda ec: ec.tensor_single_scalar(out=RS(896), in_=RS(640), scalar=float(CAP), op=ALU.is_lt),
             reads=[rB(10496, 10752)], writes=[rB(11008, 11264)])
        P.op("dve", lambda ec: ec.tensor_mul(out=RS(1024), in0=RS(256), in1=RS(768)),
             reads=[rB(9728, 9984), rB(10752, 11008)], writes=[rB(11264, 11520)])
        P.op("dve", lambda ec: ec.tensor_mul(out=RS(1152), in0=RS(384), in1=RS(896)),
             reads=[rB(9984, 10240), rB(11008, 11264)], writes=[rB(11520, 11776)])
        for mi_, (selb, selkey, ekey, eoff) in enumerate([(SEL0B, "SEL0B", "RSM", 0), (SEL1B, "SEL1B", "RSM", 128)]):
            P.op("dve", lambda ec, selb=selb: ec.tensor_tensor(
                out=SCR2[:, :].rearrange("p (tau j) -> p tau j", j=16), in0=gex3,
                in1=selb[:, :].unsqueeze(1).to_broadcast([128, 128, 16]), op=ALU.mult),
                reads=[rA(10240, 14336), rS(selkey)], writes=[rB(4096, 8192)])
            P.op("dve", lambda ec: ec.reduce_sum(out=SRCH[:, 0:128],
                                                 in_=SCR2[:, :].rearrange("p (tau j) -> p tau j", j=16), axis=AX.X),
                 reads=[rB(4096, 8192)], writes=[rS("SRCH", 0, 128)])
            P.op("dve", lambda ec, eoff=eoff: ec.tensor_scalar(out=SRCH[:, 128:256], in0=RS(eoff),
                                                               scalar1=CVAL[:, :], scalar2=None, op0=ALU.is_equal),
                 reads=[rB(9216 + 2*(eoff), 9216 + 2*(eoff + 128)), rS("CVAL")], writes=[rS("SRCH", 128, 256)])
            P.op("dve", lambda ec: ec.tensor_add(out=SRCH[:, 0:128], in0=SRCH[:, 0:128], in1=SRCH[:, 128:256]),
                 reads=[rS("SRCH", 0, 256)], writes=[rS("SRCH", 0, 128)])
            P.op("pe", lambda ec: ec.transpose(out=PS1[:, 1024:1152],
                                               in_=SRCH[:, 0:128], identity=IDENTF[:, :]),
                 reads=[rS("SRCH", 0, 128), rS("IDENTF")], writes=[rP1(1024, 1152)])
            P.op("dve", lambda ec: ec.tensor_copy(out=SRCH[:, 256:384], in_=PS1[:, 1024:1152]),
                 reads=[rP1(1024, 1152)], writes=[rS("SRCH", 256, 384)])
            DMA("sp", (ginc0_dram if mi_ == 0 else ginc1_dram).rearrange("(tau p) -> tau p", p=128),
                SRCH[:, 256:384],
                reads=[rS("SRCH", 256, 384)], writes=[rD("ginc", mi_ * NG, (mi_ + 1) * NG)])
        P.phase()
        if stage <= 2:
            return

        CF32 = BUF_C[:, :].bitcast(F32)
        P.op("dve", lambda ec: ec.tensor_scalar(out=SRCH[:, 64:96], in0=SCOL32[:, :], scalar1=C0COL[:, :],
                                                scalar2=None, op0=ALU.subtract),
             reads=[rS("SCOL32"), rS("C0COL")], writes=[rS("SRCH", 64, 96)])
        P.op("dve", lambda ec: ec.tensor_scalar(out=SRCH[:, 96:128], in0=SCOL32[:, :], scalar1=C0COL[:, :],
                                                scalar2=None, op0=ALU.is_lt),
             reads=[rS("SCOL32"), rS("C0COL")], writes=[rS("SRCH", 96, 128)])
        for m, (gtl, gth, rcol, rres) in enumerate([(0, 128, SCOL32[:, :], rS("SCOL32")),
                                                    (128, 256, SRCH[:, 64:96], rS("SRCH", 64, 96))]):
            P.op("dve", lambda ec, gtl=gtl, gth=gth, rcol=rcol: ec.tensor_tensor(
                out=CF32[:, 4096:8192].rearrange("p (c t) -> p c t", t=128),
                in0=GTB[:, gtl:gth].unsqueeze(1).to_broadcast([128, 32, 128]),
                in1=rcol.unsqueeze(2).to_broadcast([128, 32, 128]), op=ALU.is_le),
                reads=[rS("GTB"), rres], writes=[rC(8192, 16384)])
            P.op("dve", lambda ec, m=m: ec.reduce_sum(out=SRCH[:, m * 32:m * 32 + 32],
                                                      in_=CF32[:, 4096:8192].rearrange("p (c t) -> p c t", t=128),
                                                      axis=AX.X),
                 reads=[rC(8192, 16384)], writes=[rS("SRCH", m * 32, m * 32 + 32)])
        P.op("dve", lambda ec: ec.tensor_copy(out=SRCHI[:, :], in_=SRCH[:, 0:64]),
             reads=[rS("SRCH", 0, 64)], writes=[rS("SRCHI", 0, 64)])
        for m in range(2):
            gsrc = ginc0_dram if m == 0 else ginc1_dram
            for sc in range(STN):
                j = m * 32 + sc
                P.op("pool", lambda ec, gsrc=gsrc, j=j, m=m, sc=sc: ec.indirect_dma_start(
                    out=CF32[:, j * 128:(j + 1) * 128], out_offset=None,
                    in_=gsrc.rearrange("(b w) -> b w", w=128),
                    in_offset=IndirectOffsetOnAxis(SRCHI[:, j:j + 1], 0),
                    bounds_check=BREG["b127"], oob_is_err=False),
                    reads=[rD("ginc", m * NG, (m + 1) * NG), rS("SRCHI", j, j + 1)],
                    writes=[rC(j * 256, (j + 1) * 256)], kind="dma")
        for m, (rcol, rres) in enumerate([(SCOL32[:, :], rS("SCOL32")), (SRCH[:, 64:96], rS("SRCH", 64, 96))]):
            for h in range(2):
                P.op("dve", lambda ec, m=m, h=h, rcol=rcol: ec.tensor_tensor(
                    out=SCR2[:, :].rearrange("p (c t) -> p c t", t=128),
                    in0=CF32[:, m * 4096 + h * 2048: m * 4096 + (h + 1) * 2048].rearrange("p (c t) -> p c t", t=128),
                    in1=rcol[:, h * 16:(h + 1) * 16].unsqueeze(2).to_broadcast([128, 16, 128]), op=ALU.is_le),
                    reads=[rC(m * 8192 + h * 4096, m * 8192 + (h + 1) * 4096), rres], writes=[rB(4096, 8192)])
                P.op("dve", lambda ec, m=m, h=h: ec.reduce_sum(
                    out=SRCH[:, 128 + m * 32 + h * 16: 128 + m * 32 + (h + 1) * 16],
                    in_=SCR2[:, :].rearrange("p (c t) -> p c t", t=128), axis=AX.X),
                    reads=[rB(4096, 8192)],
                    writes=[rS("SRCH", 128 + m * 32 + h * 16, 128 + m * 32 + (h + 1) * 16)])
        P.op("dve", lambda ec: ec.scalar_tensor_tensor(out=SRCH[:, 192:224], in0=SRCH[:, 0:32], scalar=128.0,
                                                       in1=SRCH[:, 128:160], op0=ALU.mult, op1=ALU.add),
             reads=[rS("SRCH", 0, 32), rS("SRCH", 128, 160)], writes=[rS("SRCH", 192, 224)])
        P.op("dve", lambda ec: ec.scalar_tensor_tensor(out=SRCH[:, 224:256], in0=SRCH[:, 32:64], scalar=128.0,
                                                       in1=SRCH[:, 160:192], op0=ALU.mult, op1=ALU.add),
             reads=[rS("SRCH", 32, 64), rS("SRCH", 160, 192)], writes=[rS("SRCH", 224, 256)])
        P.op("dve", lambda ec: ec.tensor_sub(out=SRCH[:, 256:288], in0=SRCH[:, 192:224], in1=SRCH[:, 224:256]),
             reads=[rS("SRCH", 192, 256)], writes=[rS("SRCH", 256, 288)])
        P.op("dve", lambda ec: ec.tensor_mul(out=SRCH[:, 256:288], in0=SRCH[:, 256:288], in1=SRCH[:, 96:128]),
             reads=[rS("SRCH", 256, 288), rS("SRCH", 96, 128)], writes=[rS("SRCH", 256, 288)])
        P.op("dve", lambda ec: ec.tensor_add(out=SRCH[:, 256:288], in0=SRCH[:, 256:288], in1=SRCH[:, 224:256]),
             reads=[rS("SRCH", 224, 288)], writes=[rS("SRCH", 256, 288)])
        P.op("dve", lambda ec: ec.tensor_copy(out=TOKIDX[:, :], in_=SRCH[:, 256:288]),
             reads=[rS("SRCH", 256, 288)], writes=[rS("TOKIDX", 0, 32)])
        P.phase()

        for st in range(STN):
            par = st % 2
            dst = BUF_C[:, 8192 + par * 1024: 8192 + (par + 1) * 1024]
            dres = rC(8192 + par * 1024, 8192 + (par + 1) * 1024)
            P.op("pool", lambda ec, dst=dst: ec.memset(dst, 0.0), reads=[], writes=[dres])
            P.op("pool", lambda ec, st=st, dst=dst: ec.indirect_dma_start(
                out=dst, out_offset=None,
                in_=agx_out[:, :],
                in_offset=IndirectOffsetOnAxis(TOKIDX[:, st:st + 1], 0),
                bounds_check=BREG["bNG"], oob_is_err=False),
                reads=[rD("agx"), rS("TOKIDX", st, st + 1)], writes=[dres], kind="dma")
            DMA("sp", disp_dram[st * 128:(st + 1) * 128, :], dst,
                reads=[dres], writes=[rD("disp", st * 128, (st + 1) * 128)])
        P.phase()
        for k, (posoff, keepoff, eoff, dstoff) in enumerate([(512, 768, 0, 1280), (640, 896, 128, 1408)]):
            P.op("dve", lambda ec, posoff=posoff: ec.tensor_single_scalar(out=RS(1536), in_=RS(posoff),
                                                                          scalar=2048.0, op=ALU.is_ge),
                 reads=[rB(9216 + 2*(posoff), 9216 + 2*(posoff + 128))], writes=[rB(12288, 12544)])
            P.op("dve", lambda ec, posoff=posoff: ec.scalar_tensor_tensor(out=RS(1664), in0=RS(1536), scalar=-2048.0,
                                                                          in1=RS(posoff), op0=ALU.mult, op1=ALU.add),
                 reads=[rB(12288, 12544), rB(9216 + 2*(posoff), 9216 + 2*(posoff + 128))], writes=[rB(12544, 12800)])
            P.op("dve", lambda ec: ec.tensor_single_scalar(out=RS(1536), in_=RS(1536), scalar=16384.0, op=ALU.mult),
                 reads=[rB(12288, 12544)], writes=[rB(12288, 12544)])
            P.op("dve", lambda ec, eoff=eoff: ec.scalar_tensor_tensor(out=RS(1536), in0=RS(eoff), scalar=2048.0,
                                                                      in1=RS(1536), op0=ALU.mult, op1=ALU.add),
                 reads=[rB(9216 + 2*(eoff), 9216 + 2*(eoff + 128)), rB(12288, 12544)], writes=[rB(12288, 12544)])
            P.op("dve", lambda ec: ec.tensor_add(out=RS(1536), in0=RS(1536), in1=RS(1664)),
                 reads=[rB(12288, 12800)], writes=[rB(12288, 12544)])
            P.op("dve", lambda ec, eoff=eoff: ec.tensor_single_scalar(out=RS(1664), in_=RS(eoff), scalar=2048.0,
                                                                      op=ALU.mult),
                 reads=[rB(9216 + 2*(eoff), 9216 + 2*(eoff + 128))], writes=[rB(12544, 12800)])
            P.op("dve", lambda ec: ec.tensor_sub(out=RS(1536), in0=RS(1536), in1=RS(1664)),
                 reads=[rB(12288, 12800)], writes=[rB(12288, 12544)])
            P.op("dve", lambda ec, keepoff=keepoff: ec.tensor_mul(out=RS(1536), in0=RS(1536), in1=RS(keepoff)),
                 reads=[rB(12288, 12544), rB(9216 + 2*(keepoff), 9216 + 2*(keepoff + 128))], writes=[rB(12288, 12544)])
            P.op("dve", lambda ec, dstoff=dstoff: ec.tensor_add(out=RS(dstoff), in0=RS(1536), in1=RS(1664)),
                 reads=[rB(12288, 12800)], writes=[rB(9216 + 2*(dstoff), 9216 + 2*(dstoff + 128))])
        rs2v = RS2.rearrange("p (tau k) -> p tau k", k=2)
        rg2v = RG2.rearrange("p (tau k) -> p tau k", k=2)
        P.op("dve", lambda ec: ec.tensor_copy(out=rs2v[:, :, 0:1].squeeze(2), in_=RS(1280)),
             reads=[rB(11776, 12032)], writes=[rB(8192, 8704)])
        P.op("dve", lambda ec: ec.tensor_copy(out=rs2v[:, :, 1:2].squeeze(2), in_=RS(1408)),
             reads=[rB(12032, 12288)], writes=[rB(8192, 8704)])
        P.op("dve", lambda ec: ec.tensor_copy(out=rg2v[:, :, 0:1].squeeze(2), in_=RS(1024)),
             reads=[rB(11264, 11520)], writes=[rB(8704, 9216)])
        P.op("dve", lambda ec: ec.tensor_copy(out=rg2v[:, :, 1:2].squeeze(2), in_=RS(1152)),
             reads=[rB(11520, 11776)], writes=[rB(8704, 9216)])
        P.op("dve", lambda ec: ec.tensor_single_scalar(out=MYRAW[:, 0:1], in_=CVAL[:, :], scalar=16.0, op=ALU.mult),
             reads=[rS("CVAL")], writes=[rS("MYRAW", 0, 1)])
        for tt in range(TTN):
            P.op("dve", lambda ec, tt=tt: ec.tensor_single_scalar(out=MYRAW[:, 1:2], in_=MYRAW[:, 0:1],
                                                                  scalar=float(tt), op=ALU.add),
                 reads=[rS("MYRAW", 0, 1)], writes=[rS("MYRAW", 1, 2)])
            P.op("dve", lambda ec: ec.tensor_scalar(out=SRCH[:, 128:256],
                                                    in0=IOTA128T[:, :], scalar1=MYRAW[:, 1:2],
                                                    scalar2=None, op0=ALU.is_equal),
                 reads=[rS("IOTA128T"), rS("MYRAW", 1, 2)], writes=[rS("SRCH", 128, 256)])
            for (packv, packres, dst, dstres) in [
                    (rs2v, rB(8192, 8704), MYSLOT, "MYSLOT"), (rg2v, rB(8704, 9216), MYGATE, "MYGATE")]:
                P.op("dve", lambda ec, packv=packv: ec.tensor_tensor(
                    out=SRCH[:, 256:512].rearrange("p (k tau) -> p k tau", tau=128),
                    in0=packv.transpose([0, 2, 1]),
                    in1=SRCH[:, 128:256].unsqueeze(1).to_broadcast([128, 2, 128]), op=ALU.mult),
                    reads=[packres, rS("SRCH", 128, 256)], writes=[rS("SRCH", 256, 512)])
                P.op("dve", lambda ec, tt=tt, dst=dst: ec.reduce_sum(
                    out=dst[:, 2 * tt:2 * tt + 2],
                    in_=SRCH[:, 256:512].rearrange("p (k tau) -> p k tau", tau=128), axis=AX.X),
                    reads=[rS("SRCH", 256, 512)], writes=[rS(dstres, 2 * tt, 2 * tt + 2)])
        P.op("dve", lambda ec: ec.tensor_copy(out=MYSLOTI[:, :], in_=MYSLOT[:, :]),
             reads=[rS("MYSLOT", 0, 32)], writes=[rS("MYSLOTI")])
        DMA("sp", myslots_dram.rearrange("(tt p) k -> p tt k", p=128),
            MYSLOT[:, :].rearrange("p (tt k) -> p tt k", k=2),
            reads=[rS("MYSLOT", 0, 32)], writes=[rD("myslots")])
        DMA("sp", mygates_dram.rearrange("(tt p) k -> p tt k", p=128),
            MYGATE[:, :].rearrange("p (tt k) -> p tt k", k=2),
            reads=[rS("MYGATE", 0, 32)], writes=[rD("mygates")])
        P.phase()
        if stage <= 3:
            return

        PSH = [PS0[:, 0:512], PS0[:, 512:1024]]
        psh_res = [rP0(0, 512), rP0(512, 1024)]
        psy = [(PS0[:, 0:1024], rP0(0, 1024)), (PS0[:, 1024:2048], rP0(1024, 2048)),
               (PS1[:, 0:1024], rP1(0, 1024)), (PS1[:, 1024:2048], rP1(1024, 2048))]
        W1B = [(BUF_C[:, 10240:11264], rC(10240, 11264)), (BUF_C[:, 11264:12288], rC(11264, 12288))]
        W2B = [(BUF_C[:, 12288:13312], rC(12288, 13312)), (BUF_C[:, 13312:14336], rC(13312, 14336))]
        YSTG = BUF_C[:, 14336:15360]
        ystg_res = rC(14336, 15360)
        for sc8 in range(SCN):
            par8 = sc8 % 2
            dbase = par8 * 4096
            for dc in range(DCN):
                DMA("sp", BUF_B[:, dbase + dc * 512: dbase + (dc + 1) * 512],
                    disp_dram[sc8 * 512:(sc8 + 1) * 512, dc * 128:(dc + 1) * 128],
                    reads=[rD("disp", sc8 * 512, (sc8 + 1) * 512)],
                    writes=[rB(dbase + dc * 512, dbase + (dc + 1) * 512)], transpose=True)
            for ht in range(HTN):
                parw = ht % 2
                DMA("sp", W1B[parw][0], w1t_dram[ht], reads=[rD("w1t", ht, ht + 1)], writes=[W1B[parw][1]])
                for dc in range(DCN):
                    P.op("pe", lambda ec, ht=ht, dc=dc, parw=parw, dbase=dbase: ec.matmul(
                        out=PSH[ht % 2], lhsT=BUF_C[:, 10240 + parw * 1024 + dc * 128: 10240 + parw * 1024 + (dc + 1) * 128],
                        rhs=BUF_B[:, dbase + dc * 512: dbase + (dc + 1) * 512],
                        start=(dc == 0), stop=(dc == DCN - 1)),
                        reads=[W1B[parw][1], rB(dbase + dc * 512, dbase + (dc + 1) * 512)],
                        writes=[psh_res[ht % 2]])
                P.op("act", lambda ec, ht=ht: ec.activation(out=BUF_A[:, ht * 512:(ht + 1) * 512],
                                                            in_=PSH[ht % 2], func=AF.Gelu,
                                                            bias=B1[:, ht:ht + 1], scale=1.0),
                     reads=[psh_res[ht % 2], rS("B1", ht, ht + 1)], writes=[rA(ht * 512, (ht + 1) * 512)])
            for ht in range(HTN):
                parw = ht % 2
                DMA("sp", W2B[parw][0], w2bf_dram[ht * 128:(ht + 1) * 128, :],
                    reads=[rD("w2bf", ht * 128, (ht + 1) * 128)], writes=[W2B[parw][1]])
                for st4 in range(4):
                    for hf in range(2):
                        P.op("pe", lambda ec, ht=ht, st4=st4, hf=hf, parw=parw: ec.matmul(
                            out=psy[st4][0][:, hf * 512:(hf + 1) * 512],
                            lhsT=BUF_A[:, ht * 512 + st4 * 128: ht * 512 + (st4 + 1) * 128],
                            rhs=BUF_C[:, 12288 + parw * 1024 + hf * 512: 12288 + parw * 1024 + (hf + 1) * 512],
                            start=(ht == 0), stop=(ht == HTN - 1)),
                            reads=[rA(ht * 512, (ht + 1) * 512), W2B[parw][1]],
                            writes=[psy[st4][1]])
            for st4 in range(4):
                P.op("dve", lambda ec, st4=st4: ec.tensor_add(out=YSTG[:, :], in0=psy[st4][0], in1=B2B[:, :]),
                     reads=[psy[st4][1], rS("B2B")], writes=[ystg_res])
                DMA("sp", y_dram[sc8 * 512 + st4 * 128: sc8 * 512 + (st4 + 1) * 128, :], YSTG[:, :],
                    reads=[ystg_res], writes=[rD("y", sc8 * 512 + st4 * 128, sc8 * 512 + (st4 + 1) * 128)])
            if sc8 in (3, 7):
                half = sc8 // 4
                P.phase()
                emit_ag(y_dram[half * 2048:(half + 1) * 2048, :],
                        agy_out[half * 16384:(half + 1) * 16384, :],
                        rD("y", half * 2048, (half + 1) * 2048),
                        rD("agy", half * 16384, (half + 1) * 16384))
                P.phase()

        for tt in range(TTN):
            par = tt % 2
            yk0 = BUF_C[:, 8192 + par * 2048: 8192 + par * 2048 + 1024]
            yk1 = BUF_C[:, 8192 + par * 2048 + 1024: 8192 + par * 2048 + 2048]
            r0 = rC(8192 + par * 2048, 8192 + par * 2048 + 1024)
            r1 = rC(8192 + par * 2048 + 1024, 8192 + par * 2048 + 2048)
            P.op("pool", lambda ec, tt=tt, yk0=yk0: ec.indirect_dma_start(
                out=yk0, out_offset=None, in_=agy_out[:, :],
                in_offset=IndirectOffsetOnAxis(MYSLOTI[:, 2 * tt:2 * tt + 1], 0)),
                reads=[rD("agy"), rS("MYSLOTI")], writes=[r0], kind="dma")
            P.op("pool", lambda ec, tt=tt, yk1=yk1: ec.indirect_dma_start(
                out=yk1, out_offset=None, in_=agy_out[:, :],
                in_offset=IndirectOffsetOnAxis(MYSLOTI[:, 2 * tt + 1:2 * tt + 2], 0)),
                reads=[rD("agy"), rS("MYSLOTI")], writes=[r1], kind="dma")
            astg_r = ("ASTG", par * 1024, (par + 1) * 1024)
            P.op("dve", lambda ec, tt=tt, yk0=yk0, par=par: ec.tensor_scalar(
                out=ASTG[:, par * 1024:(par + 1) * 1024], in0=yk0, scalar1=MYGATE[:, 2 * tt:2 * tt + 1],
                scalar2=None, op0=ALU.mult),
                reads=[r0, rS("MYGATE")], writes=[astg_r])
            P.op("dve", lambda ec, tt=tt, yk1=yk1, par=par: ec.scalar_tensor_tensor(
                out=ASTG[:, par * 1024:(par + 1) * 1024], in0=yk1, scalar=MYGATE[:, 2 * tt + 1:2 * tt + 2],
                in1=ASTG[:, par * 1024:(par + 1) * 1024], op0=ALU.mult, op1=ALU.add),
                reads=[r1, rS("MYGATE"), astg_r], writes=[astg_r])
            P.op("dve", lambda ec, tt=tt, par=par: ec.scalar_tensor_tensor(
                out=XS[:, tt * 1024:(tt + 1) * 1024], in0=ASTG[:, par * 1024:(par + 1) * 1024], scalar=0.5,
                in1=XS[:, tt * 1024:(tt + 1) * 1024], op0=ALU.mult, op1=ALU.add),
                reads=[astg_r, rXS(tt)], writes=[rXS(tt)])
        P.phase()

    def emit_attn():
        emit_ln(aln2_dram, "aln2")
        P.phase()
        for dc in range(DCN):
            DMA("sp", BUF_A[:, dc * 2048:(dc + 1) * 2048], aln2_dram[:, dc * 128:(dc + 1) * 128],
                reads=[rD("aln2")], writes=[rA(dc * 2048, (dc + 1) * 2048)], transpose=True)
        for dc in range(DCN):
            pb, pres = (PS0, rP0) if dc % 2 == 0 else (PS1, rP1)
            for j in range(4):
                P.op("pe", lambda ec, dc=dc, j=j, pb=pb: ec.matmul(
                    out=pb[:, j * 512:(j + 1) * 512], lhsT=MTBUF[:, dc * 128:(dc + 1) * 128],
                    rhs=BUF_A[:, dc * 2048 + j * 512: dc * 2048 + (j + 1) * 512], start=True, stop=True),
                    reads=[rS("MTBUF"), rA(dc * 2048 + j * 512, dc * 2048 + (j + 1) * 512)],
                    writes=[pres(j * 512, (j + 1) * 512)])
            P.op("dve", lambda ec, dc=dc, pb=pb: ec.tensor_copy(out=BUF_B[:, dc * 2048:(dc + 1) * 2048], in_=pb[:, :]),
                 reads=[pres()], writes=[rB(dc * 2048, (dc + 1) * 2048)])
        for kt in range(TTN):
            pb, pres = (PS0, rP0) if kt % 2 == 0 else (PS1, rP1)
            for dc in range(DCN):
                P.op("pe", lambda ec, kt=kt, dc=dc, pb=pb: ec.matmul(
                    out=pb[:, dc * 128:(dc + 1) * 128],
                    lhsT=BUF_A[:, dc * 2048 + kt * 128: dc * 2048 + (kt + 1) * 128],
                    rhs=GVTBUF[:, dc * 128:(dc + 1) * 128], start=True, stop=True),
                    reads=[rA(dc * 2048 + kt * 128, dc * 2048 + (kt + 1) * 128), rS("GVTBUF")],
                    writes=[pres(dc * 128, (dc + 1) * 128)])
            P.op("dve", lambda ec, kt=kt, pb=pb: ec.tensor_copy(out=BUF_C[:, kt * 1024:(kt + 1) * 1024],
                                                                in_=pb[:, 0:1024]),
                 reads=[pres(0, 1024)], writes=[rC(kt * 1024, (kt + 1) * 1024)])
        P.phase()
        def emit_atrans(q):
            pbq, presq = (PS0, rP0) if q % 2 == 0 else (PS1, rP1)
            for kt in range(8):
                off = kt * 2048
                pc = (kt % 2) * 128
                P.op("pe", lambda ec, kt=kt, pbq=pbq, pc=pc: ec.matmul(
                    out=pbq[:, pc:pc + 128],
                    lhsT=ASTG[:, kt * 128:(kt + 1) * 128], rhs=IDENTF[:, :],
                    start=True, stop=True, is_transpose=True),
                    reads=[("ASTG", kt * 128, (kt + 1) * 128), rS("IDENTF")],
                    writes=[presq(pc, pc + 128)])
                P.op("dve", lambda ec, kt=kt, q=q, off=off, pbq=pbq, pc=pc: ec.tensor_copy(
                    out=BUF_A[:, off + q * 128: off + (q + 1) * 128], in_=pbq[:, pc:pc + 128]),
                    reads=[presq(pc, pc + 128)], writes=[rA(off + q * 128, off + (q + 1) * 128)])
            P.op("dve", lambda ec: ec.tensor_copy(out=XC[:, 0:512].bitcast(BF16), in_=ASTG[:, 1024:2048]),
                 reads=[("ASTG", 1024, 2048)], writes=[("XC", 0, 1024)])
            DMA("sp", a_dram[q * 128:(q + 1) * 128, 0:1024], XC[:, 0:512].bitcast(BF16),
                reads=[("XC", 0, 512)], writes=[rD("a", q * 128, (q + 1) * 128)])

        for qt in range(TTN):
            pb, pres = (PS0, rP0) if qt % 2 == 0 else (PS1, rP1)
            for dc in range(DCN):
                for j in range(4):
                    P.op("pe", lambda ec, qt=qt, dc=dc, j=j, pb=pb: ec.matmul(
                        out=pb[:, j * 512:(j + 1) * 512],
                        lhsT=BUF_A[:, dc * 2048 + qt * 128: dc * 2048 + (qt + 1) * 128],
                        rhs=BUF_B[:, dc * 2048 + j * 512: dc * 2048 + (j + 1) * 512],
                        start=(dc == 0), stop=(dc == DCN - 1)),
                        reads=[rA(dc * 2048 + qt * 128, dc * 2048 + (qt + 1) * 128),
                               rB(dc * 2048 + j * 512, dc * 2048 + (j + 1) * 512)],
                        writes=[pres(j * 512, (j + 1) * 512)])
            cb = 16 + (qt % 2) * 8
            P.op("dve", lambda ec, pb=pb, cb=cb: ec.reduce_max(out=COLS[:, cb:cb + 1], in_=pb[:, :], axis=AX.X),
                 reads=[pres()], writes=[("COLS", cb, cb + 1)])
            P.op("dve", lambda ec, pb=pb, cb=cb: ec.tensor_reduce(out=COLS[:, cb + 1:cb + 2], in_=pb[:, :],
                                                                  axis=AX.X, op=ALU.min),
                 reads=[pres()], writes=[("COLS", cb + 1, cb + 2)])
            P.op("dve", lambda ec, cb=cb: ec.tensor_sub(out=COLS[:, cb + 2:cb + 3], in0=COLS[:, cb:cb + 1],
                                                        in1=COLS[:, cb + 1:cb + 2]),
                 reads=[("COLS", cb, cb + 2)], writes=[("COLS", cb + 2, cb + 3)])
            P.op("dve", lambda ec, cb=cb: ec.tensor_mul(out=COLS[:, cb + 2:cb + 3], in0=COLS[:, cb + 2:cb + 3],
                                                        in1=SGNPOS[:, :]),
                 reads=[("COLS", cb + 2, cb + 3), rS("SGNPOS")], writes=[("COLS", cb + 2, cb + 3)])
            P.op("dve", lambda ec, cb=cb: ec.tensor_add(out=COLS[:, cb + 2:cb + 3], in0=COLS[:, cb + 2:cb + 3],
                                                        in1=COLS[:, cb + 1:cb + 2]),
                 reads=[("COLS", cb + 1, cb + 3)], writes=[("COLS", cb + 2, cb + 3)])
            P.op("dve", lambda ec, cb=cb: ec.tensor_mul(out=COLS[:, cb + 3:cb + 4], in0=COLS[:, cb + 2:cb + 3],
                                                        in1=ALPHAC[:, :]),
                 reads=[("COLS", cb + 2, cb + 3), rS("ALPHAC")], writes=[("COLS", cb + 3, cb + 4)])
            P.op("dve", lambda ec, cb=cb: ec.tensor_single_scalar(out=COLS[:, cb + 3:cb + 4], in_=COLS[:, cb + 3:cb + 4],
                                                                  scalar=-1.0, op=ALU.mult),
                 reads=[("COLS", cb + 3, cb + 4)], writes=[("COLS", cb + 3, cb + 4)])
            if qt > 0:
                emit_atrans(qt - 1)
            P.op("act", lambda ec, pb=pb, cb=cb: ec.activation(
                out=ASTG[:, :], in_=pb[:, :], func=AF.Exp,
                bias=COLS[:, cb + 3:cb + 4], scale=ALPHAC[:, :]),
                reads=[pres(), ("COLS", cb + 3, cb + 4), rS("ALPHAC")],
                writes=[("ASTG", 0, 2048)])
        emit_atrans(TTN - 1)
        for kt in range(8, TTN):
            off = (kt % 8) * 2048
            DMA("sp", BUF_B[:, off:off + 2048], a_dram[:, (kt - 8) * 128:(kt - 7) * 128],
                reads=[rD("a")], writes=[rB(off, off + 2048)], transpose=True)
        for qt in range(TTN):
            pb, pres = (PS0, rP0) if qt % 2 == 0 else (PS1, rP1)
            for kt in range(TTN):
                dstbuf = BUF_A if kt < 8 else BUF_B
                rr = rA if kt < 8 else rB
                off = (kt % 8) * 2048
                for hf in range(2):
                    P.op("pe", lambda ec, qt=qt, kt=kt, hf=hf, pb=pb, dstbuf=dstbuf, off=off: ec.matmul(
                        out=pb[:, hf * 512:(hf + 1) * 512],
                        lhsT=dstbuf[:, off + qt * 128: off + (qt + 1) * 128],
                        rhs=BUF_C[:, kt * 1024 + hf * 512: kt * 1024 + (hf + 1) * 512],
                        start=(kt == 0), stop=(kt == TTN - 1)),
                        reads=[rr(off + qt * 128, off + (qt + 1) * 128),
                               rC(kt * 1024 + hf * 512, kt * 1024 + (hf + 1) * 512)],
                        writes=[pres(hf * 512, (hf + 1) * 512)])
            cb = 16 + (qt % 2) * 8
            P.op("act", lambda ec, pb=pb, cb=cb, qt=qt: ec.activation(
                out=ASTG[:, (qt % 2) * 1024:(qt % 2) * 1024 + 1024], in_=pb[:, 0:1024], func=AF.Square,
                accum_out=COLS[:, cb + 6:cb + 7]),
                reads=[pres(0, 1024)],
                writes=[("ASTG", (qt % 2) * 1024, (qt % 2) * 1024 + 1024), ("COLS", cb + 6, cb + 7)])
            P.op("dve", lambda ec, cb=cb: ec.tensor_single_scalar(out=COLS[:, cb + 6:cb + 7], in_=COLS[:, cb + 6:cb + 7],
                                                                  scalar=1e-8, op=ALU.max),
                 reads=[("COLS", cb + 6, cb + 7)], writes=[("COLS", cb + 6, cb + 7)])
            P.op("act", lambda ec, cb=cb: ec.activation(out=COLS[:, cb + 7:cb + 8], in_=COLS[:, cb + 6:cb + 7],
                                                        func=AF.Sqrt),
                 reads=[("COLS", cb + 6, cb + 7)], writes=[("COLS", cb + 7, cb + 8)])
            P.op("dve", lambda ec, cb=cb: ec.reciprocal(out=COLS[:, cb + 6:cb + 7], in_=COLS[:, cb + 7:cb + 8]),
                 reads=[("COLS", cb + 7, cb + 8)], writes=[("COLS", cb + 6, cb + 7)])
            P.op("dve", lambda ec, qt=qt, pb=pb, cb=cb: ec.scalar_tensor_tensor(
                out=XS[:, qt * 1024:(qt + 1) * 1024], in0=pb[:, 0:1024], scalar=COLS[:, cb + 6:cb + 7],
                in1=XS[:, qt * 1024:(qt + 1) * 1024], op0=ALU.mult, op1=ALU.add),
                reads=[pres(0, 1024), ("COLS", cb + 6, cb + 7), rXS(qt)], writes=[rXS(qt)])
        P.phase()

    ck_dram = {}

    def ckpt(name):
        t = nc.dram_tensor("ck_" + name, [T, D], F32, kind="ExternalOutput").ap()
        ck_dram[name] = t
        DMA("sp", t.rearrange("(tt p) d -> p tt d", p=128),
            XS[:, :].rearrange("p (tt d) -> p tt d", d=1024),
            reads=[("XS", 0, TTN * 1024)], writes=[("CK_" + name, 0, 1)])

    emit_ffn("f1")
    if stage in (45, 46):
        ckpt("x1")
    if stage >= 5:
        emit_attn()
    if stage == 46:
        ckpt("x2")
    if stage >= 6 and stage != 45:
        emit_ffn("f3")

    fin = DMA("sp", out.rearrange("(tt p) d -> p tt d", p=128),
              XS[:, :].rearrange("p (tt d) -> p tt d", d=1024),
              reads=[("XS", 0, TTN * 1024)], writes=[("OUT", 0, 1)])
    tap_toks = [fin]
    allw = [(s, P.val[s]) for s in SEMNAMES]
    _sbufs = {"WGB": WGB[:, :], "WGF": WGF[:, :], "BUFB": BUF_B[:, :], "BUFA": BUF_A[:, :], "LSTG": LSTG[:, :], "TOKIDX": TOKIDX[:, :], "GTB": GTB[:, :], "C0COL": C0COL[:, :], "MYSLOT": MYSLOT[:, :], "MYGATE": MYGATE[:, :], "IOTA8T": IOTA8T[:, :], "UT": UT[:, :]}
    sbuf_taps = {n: _sbufs[n] for n in (sbuf_taps or ())}
    for name, ap in (sbuf_taps or {}).items():
        shp = [ap.shape[0], ap.free_size()]
        tp = nc.dram_tensor("stap_" + name, shp, ap.dtype, kind="ExternalOutput").ap()
        tap_toks.append(P.op("pool", lambda ec, tp=tp, ap=ap: ec.dma_start(out=tp, in_=ap),
                             waits=allw, kind="dma"))
    for name in taps:
        srct = dram_by_name[name]
        tp = nc.dram_tensor("tap_" + name, list(srct.tensor.shape), srct.dtype, kind="ExternalOutput").ap()
        tap_toks.append(DMA("pool", tp, srct, waits=allw))
    P.nop_op("sp", lambda ec: ec.nop(), waits=tap_toks + [(s, P.val[s]) for s in SEMNAMES])
    P.phase()

    with nc.Block() as block:
        n = P.emit(block, sems, nc)
    es.close()
    return nc


def host_inputs(L0_attn_Q, L0_attn_rot_w, L0_attn_scale, L0_attn_bias,
                L0_ffn_wg, L0_ffn_w1, L0_ffn_b1, L0_ffn_w2, L0_ffn_b2):
    Q = np.ascontiguousarray(np.asarray(L0_attn_Q, dtype=np.float32))
    rw = np.ascontiguousarray(np.asarray(L0_attn_rot_w, dtype=np.float32)).reshape(1, 3 * D)
    sc = np.asarray(L0_attn_scale, dtype=np.float32).reshape(1, 1)
    wg = np.ascontiguousarray(np.asarray(L0_ffn_wg, dtype=np.float32)).reshape(DCN, 128, 8)
    w1 = np.ascontiguousarray(np.asarray(L0_ffn_w1, dtype=np.float32))
    b1 = np.ascontiguousarray(np.asarray(L0_ffn_b1, dtype=np.float32))
    w2 = np.ascontiguousarray(np.asarray(L0_ffn_w2, dtype=np.float32))
    b2 = np.ascontiguousarray(np.asarray(L0_ffn_b2, dtype=np.float32))

    ut = np.triu(np.ones((128, 128), np.float32), 1)
    onesrow = np.ones((1, 128), np.float32)
    ones128 = np.ones((128, 1), np.float32)
    identf = np.eye(128, dtype=np.float32)
    iotap = np.arange(128, dtype=np.float32).reshape(128, 1)
    iota8t = np.tile(np.arange(8, dtype=np.float32), (128, 128))
    iota128 = np.tile(np.arange(128, dtype=np.float32), (128, 1))
    scol32 = (np.arange(128, dtype=np.float32)[:, None] + 128.0 * np.arange(32, dtype=np.float32)[None, :])
    diag0 = np.eye(128, dtype=np.float32)
    diagu = np.zeros((128, 128), np.float32)
    diagl = np.zeros((128, 128), np.float32)
    for i in range(64):
        diagu[2 * i, 2 * i + 1] = 1.0
        diagl[2 * i + 1, 2 * i] = 1.0

    in_maps = []
    for c in range(8):
        sel0b = np.zeros((128, 16), np.float32)
        sel0b[:, c] = 1.0
        sel1b = np.zeros((128, 16), np.float32)
        sel1b[:, 8 + c] = 1.0
        cval = np.full((128, 1), float(c), np.float32)
        myidx = (c * 128 + np.arange(128, dtype=np.int32)).reshape(128, 1)
        in_maps.append({
            "xq": Q[c], "rwf": rw, "scale1": sc, "wg3": wg,
            "w1c": w1[c], "w2c": w2[c], "b1c": b1[c].reshape(HTN, 128, 1), "b2c": b2[c].reshape(1, D),
            "ut_in": ut, "onesrow_in": onesrow, "ones128_in": ones128, "identf_in": identf,
            "iotap_in": iotap, "iota8t_in": iota8t, "iota128_in": iota128, "scol32_in": scol32, "diag0_in": diag0, "diagu_in": diagu, "diagl_in": diagl,
            "sel0b_in": sel0b, "sel1b_in": sel1b, "cval_in": cval, "myidx_in": myidx,
        })
    return in_maps


def kernel(L0_attn_Q, L0_attn_rot_w, L0_attn_scale, L0_attn_bias,
           L0_ffn_wg, L0_ffn_w1, L0_ffn_b1, L0_ffn_w2, L0_ffn_b2):
    in_maps = host_inputs(L0_attn_Q, L0_attn_rot_w, L0_attn_scale, L0_attn_bias,
                          L0_ffn_wg, L0_ffn_w1, L0_ffn_b1, L0_ffn_w2, L0_ffn_b2)
    nc = build_nc()
    res = run_bass_kernel_spmd(nc, in_maps, core_ids=list(range(8)))
    return np.stack([r["out"] for r in res.results], axis=0)


# version 06
# speedup vs baseline: 1.1881x; 1.1801x over previous
import numpy as np
import ml_dtypes
import concourse.bass as bass
import concourse.mybir as mybir
from concourse.bass_utils import run_bass_kernel_spmd
from concourse.bass_core_rust import IndirectOffsetOnAxis

F32 = mybir.dt.float32
BF16 = mybir.dt.bfloat16
I32 = mybir.dt.int32
ALU = mybir.AluOpType
AX = mybir.AxisListType
AF = mybir.ActivationFunctionType

T = 2048
TTN = 16
D = 1024
DCN = 8
NG = 16384
E = 8
CAP = 4096
H = 4096
HTN = 32
SCN = 8
STN = 32

ENGS = ("pe", "act", "dve", "pool", "sp")
ENG2BLOCK = {"pe": "tensor", "act": "scalar", "dve": "vector", "pool": "gpsimd", "sp": "sync"}

STAGE = 99
TAPS = ()


SEM_BASE = 1 << 24


NSPD = 12
NPLD = 12
COMPUTE_SEMS = ("pe", "act", "dve", "plc")
SEMNAMES = list(COMPUTE_SEMS) + ["cc"] + [f"spd{i}" for i in range(NSPD)] + [f"pld{i}" for i in range(NPLD)]


class Prog:

    def __init__(self):
        self.phases = []
        self.q = {e: [] for e in ENGS}
        self.val = {s: (SEM_BASE if s in COMPUTE_SEMS else 0) for s in SEMNAMES}
        self.rr = {"sp": 0, "pool": 0}
        self.res = {}

    def op(self, eng, fn, reads=(), writes=(), waits=(), inc=1, kind="compute"):
        need = dict()

        def add(tok):
            if tok is None:
                return
            s, v = tok
            if need.get(s, 0) < v:
                need[s] = v

        for w in waits:
            add(w)
        for (buf, lo, hi) in reads:
            for (l2, h2, tok, wr) in self.res.get(buf, ()):
                if wr and l2 < hi and lo < h2:
                    if tok[0] == "pe" and eng == "pe":
                        continue
                    add(tok)
        for (buf, lo, hi) in writes:
            for (l2, h2, tok, wr) in self.res.get(buf, ()):
                if l2 < hi and lo < h2:
                    if tok[0] == "pe" and eng == "pe" and wr:
                        continue
                    add(tok)
        if kind == "dma":
            assert eng in ("sp", "pool")
            i = self.rr[eng]
            self.rr[eng] = (i + 1) % (NSPD if eng == "sp" else NPLD)
            sname = (f"spd{i}" if eng == "sp" else f"pld{i}")
            add((sname, self.val[sname]))
            inc = 16
        elif kind == "cc":
            sname = "cc"
            inc = 1
        else:
            sname = {"pe": "pe", "act": "act", "dve": "dve", "pool": "plc", "sp": "plc"}[eng]
            assert eng != "sp", "sync engine has no compute ops with incs"
        self.val[sname] += inc
        tok = (sname, self.val[sname])
        self.q[eng].append((fn, tuple(sorted(need.items())), inc, sname))
        for (buf, lo, hi) in writes:
            ent = self.res.setdefault(buf, [])
            ent[:] = [x for x in ent if not (lo <= x[0] and x[1] <= hi)]
            ent.append((lo, hi, tok, True))
        for (buf, lo, hi) in reads:
            self.res.setdefault(buf, []).append((lo, hi, tok, False))
        return tok

    def nop_op(self, eng, fn, waits=()):
        self.q[eng].append((fn, tuple(sorted({s: v for (s, v) in waits}.items())), 0, None))

    def phase(self):
        if any(self.q[e] for e in ENGS):
            self.phases.append(self.q)
            self.q = {e: [] for e in ENGS}

    def emit(self, block, sems, nc=None):
        self.phase()

        @block.gpsimd
        def _(g):
            g.dma_reset()
            for s in SEMNAMES:
                g.sem_clear(sems[s])
            for s in COMPUTE_SEMS:
                g.sem_inc(sems[s], SEM_BASE)
        first = {e: True for e in ENGS}
        last_wait = {e: {s: 0 for s in SEMNAMES} for e in ENGS}
        for ph in self.phases:
            for eng in ENGS:
                ops = ph[eng]
                if not ops:
                    continue
                dec = getattr(block, ENG2BLOCK[eng])

                def body(ectx, ops=ops, eng=eng):
                    lw = last_wait[eng]
                    if first[eng]:
                        first[eng] = False
                        if eng in ("pe", "act", "dve"):
                            ectx.wait_ge(sems[eng], SEM_BASE)
                            lw[eng] = SEM_BASE
                        elif eng == "sp":
                            ectx.wait_ge(sems["dve"], SEM_BASE)
                            lw["dve"] = SEM_BASE
                    for fn, waits, inc, sname in ops:
                        for (ws, wval) in waits:
                            if lw[ws] >= wval:
                                continue
                            ectx.wait_ge(sems[ws], wval)
                            lw[ws] = wval
                        inst = fn(ectx)
                        if inc:
                            inst.then_inc(sems[sname], inc)

                dec(body)
        n = sum(len(ph[e]) for ph in self.phases for e in ENGS)
        return n


def build_nc(stage=None, taps=None, sbuf_taps=None, rstop=99):
    stage = STAGE if stage is None else stage
    taps = TAPS if taps is None else taps
    nc = bass.Bass("TRN2", target_bir_lowering=False, detect_race_conditions=False)
    P = Prog()

    def ein(name, shape, dtype):
        return nc.dram_tensor(name, shape, dtype, kind="ExternalInput").ap()

    xq = ein("xq", [T, D], F32)
    rwf = ein("rwf", [1, 3 * D], F32)
    scale1 = ein("scale1", [1, 1], F32)
    wg3 = ein("wg3", [DCN, 128, 8], F32)
    w1c = ein("w1c", [D, H], F32)
    w2c = ein("w2c", [H, D], F32)
    b1c = ein("b1c", [HTN, 128, 1], F32)
    b2c = ein("b2c", [1, D], F32)
    ut_in = ein("ut_in", [128, 128], F32)
    onesrow_in = ein("onesrow_in", [1, 128], F32)
    ones128_in = ein("ones128_in", [128, 1], F32)
    identf_in = ein("identf_in", [128, 128], F32)
    iotap_in = ein("iotap_in", [128, 1], F32)
    iota8t_in = ein("iota8t_in", [128, 1024], F32)
    iota128_in = ein("iota128_in", [128, 128], F32)
    scol32_in = ein("scol32_in", [128, 32], F32)
    diag0_in = ein("diag0_in", [128, 128], F32)
    diagu_in = ein("diagu_in", [128, 128], F32)
    diagl_in = ein("diagl_in", [128, 128], F32)
    sel0b_in = ein("sel0b_in", [128, 16], F32)
    sel1b_in = ein("sel1b_in", [128, 16], F32)
    cval_in = ein("cval_in", [128, 1], F32)
    myidx_in = ein("myidx_in", [128, 1], I32)

    out = nc.dram_tensor("out", [T, D], F32, kind="ExternalOutput").ap()

    xln_dram = nc.dram_tensor("xln_dram", [T, D], BF16).ap()
    agx_out = nc.dram_tensor("agx_out", [NG, D], BF16, addr_space="Shared").ap()
    logit_dram = nc.dram_tensor("logit_dram", [T, 8], F32).ap()
    agl_out = nc.dram_tensor("agl_out", [NG, 8], F32, addr_space="Shared").ap()
    pack_dram = nc.dram_tensor("pack_dram", [2048], F32).ap()
    off_dram = nc.dram_tensor("off_dram", [2048], F32).ap()
    gt_dram = nc.dram_tensor("gt_dram", [2, 128], F32).ap()
    ginc0_dram = nc.dram_tensor("ginc0_dram", [NG], F32).ap()
    ginc1_dram = nc.dram_tensor("ginc1_dram", [NG], F32).ap()
    slots_dram = nc.dram_tensor("slots_dram", [NG // 16, 32], F32).ap()
    gates_dram = nc.dram_tensor("gates_dram", [NG // 16, 32], F32).ap()
    myslots_dram = nc.dram_tensor("myslots_dram", [T, 2], F32).ap()
    mygates_dram = nc.dram_tensor("mygates_dram", [T, 2], F32).ap()
    disp_dram = nc.dram_tensor("disp_dram", [CAP, D], BF16).ap()
    y_dram = nc.dram_tensor("y_dram", [CAP, D], BF16).ap()
    agy_out = nc.dram_tensor("agy_out", [SCN * E * 512, D], BF16, addr_space="Shared").ap()
    aln2_dram = nc.dram_tensor("aln2_dram", [T, D], BF16).ap()
    a_dram = nc.dram_tensor("a_dram", [T, T], BF16).ap()
    mb_dram = nc.dram_tensor("mb_dram", [6, 1024], F32).ap()
    w1t_dram = nc.dram_tensor("w1t_dram", [HTN, 128, DCN * 128], BF16).ap()
    w2bf_dram = nc.dram_tensor("w2bf_dram", [H, D], BF16).ap()
    dram_by_name = {t.tensor.name: t for t in [
        xln_dram, agx_out, logit_dram, agl_out, slots_dram, gates_dram, myslots_dram,
        mygates_dram, disp_dram, y_dram, agy_out, aln2_dram, a_dram]}

    from contextlib import ExitStack
    es = ExitStack()

    def sbt(name, shape, dtype):
        return es.enter_context(nc.sbuf_tensor(name, shape, dtype))

    XS = sbt("XS", [128, TTN * 1024], F32)
    BUF_A = sbt("BUF_A", [128, 16384], BF16)
    BUF_B = sbt("BUF_B", [128, 16384], BF16)
    BUF_C = sbt("BUF_C", [128, 16384], BF16)
    ASTG = sbt("ASTG", [128, 2048], F32)
    XC = sbt("XC", [128, 1024], F32)
    MTBUF = sbt("MTBUF", [128, 1024], BF16)
    GVTBUF = sbt("GVTBUF", [128, 1024], BF16)
    B2B = sbt("B2B", [128, 1024], F32)
    B1 = sbt("B1", [128, HTN], F32)
    WGB = sbt("WGB", [128, 64], BF16)
    WGF = sbt("WGF", [128, 64], F32)
    UT = sbt("UT", [128, 128], F32)
    ONESROW = sbt("ONESROW", [1, 128], F32)
    ONES128 = sbt("ONES128", [128, 1], F32)
    IDENTF = sbt("IDENTF", [128, 128], F32)
    IOTAP = sbt("IOTAP", [128, 1], F32)
    IOTA8T = sbt("IOTA8T", [128, 1024], F32)
    IOTA128T = sbt("IOTA128T", [128, 128], F32)
    DIAG0 = sbt("DIAG0", [128, 128], F32)
    DIAGU = sbt("DIAGU", [128, 128], F32)
    DIAGL = sbt("DIAGL", [128, 128], F32)
    SEL0B = sbt("SEL0B", [128, 16], F32)
    SEL1B = sbt("SEL1B", [128, 16], F32)
    CVAL = sbt("CVAL", [128, 1], F32)
    MYIDX = sbt("MYIDX", [128, 1], I32)
    RW = sbt("RW", [1, D], F32)
    RSC = sbt("RSC", [1, 2048], F32)
    SCB = sbt("SCB", [1, 8], F32)
    ALPHAC = sbt("ALPHAC", [128, 1], F32)
    SGNPOS = sbt("SGNPOS", [128, 1], F32)
    COLS = sbt("COLS", [128, 32], F32)
    LSTG = sbt("LSTG", [128, 128], F32)
    XLNBF = sbt("XLNBF", [128, 1024], BF16)
    TTILE = sbt("TTILE", [128, 16], F32)
    TPEX = sbt("TPEX", [128, 16], F32)
    CROW = sbt("CROW", [1, 16], F32)
    TROW = RSC
    GTB = sbt("GTB", [128, 256], F32)
    C0COL = sbt("C0COL", [128, 1], F32)
    TOKIDX = sbt("TOKIDX", [128, 32], I32)
    MYSLOT = sbt("MYSLOT", [128, 32], F32)
    MYSLOTI = sbt("MYSLOTI", [128, 32], I32)
    MYGATE = sbt("MYGATE", [128, 32], F32)
    SRCH = sbt("SRCH", [128, 512], F32)
    SRCHI = sbt("SRCHI", [128, 64], I32)
    SCOL32 = sbt("SCOL32", [128, 32], F32)
    MYRAW = sbt("MYRAW", [128, 64], F32)
    EPSC = sbt("EPSC", [128, 1], F32)

    AF32 = BUF_A[:, :].bitcast(F32)
    BF32 = BUF_B[:, :].bitcast(F32)
    LG = AF32[:, 0:1024]
    MARR = AF32[:, 1024:3072]
    GEXI = AF32[:, 3072:5120]
    GEX = AF32[:, 5120:7168]
    SCR1 = AF32[:, 7168:8192]
    OH1 = BF32[:, 0:1024]
    OH2 = BF32[:, 1024:2048]
    SCR2 = BF32[:, 2048:4096]
    RS2 = BF32[:, 4096:4352]
    RG2 = BF32[:, 4352:4608]
    RSMALL = BF32[:, 4608:6656]

    def RS(i, w=128):
        return RSMALL[:, i:i + w]

    PS0 = es.enter_context(nc.psum_tensor("ps0", [128, 2048], F32))
    PS1 = es.enter_context(nc.psum_tensor("ps1", [128, 2048], F32))
    PSROW = PS1[0:1, :]

    sems = {s: es.enter_context(nc.semaphore("sem_" + s)) for s in SEMNAMES}

    def rA(lo, hi):
        return ("A", lo, hi)

    def rB(lo, hi):
        return ("B", lo, hi)

    def rC(lo, hi):
        return ("C", lo, hi)

    def rXS(tt):
        return ("XS", tt * 1024, (tt + 1) * 1024)

    def rP0(lo=0, hi=2048):
        return ("P0", lo, hi)

    def rP1(lo=0, hi=2048):
        return ("P1", lo, hi)

    def rD(name, lo=0, hi=1 << 40):
        return ("D_" + name, lo, hi)

    def rS(name, lo=0, hi=1 << 40):
        return ("S_" + name, lo, hi)

    def DMA(eng, out_ap, in_ap, reads=(), writes=(), waits=(), **kw):
        return P.op(eng, lambda ec, o=out_ap, i=in_ap, kw=kw: ec.dma_start(out=o, in_=i, **kw),
                    reads=reads, writes=writes, waits=waits, kind="dma")

    DMA("sp", XS[:, :].rearrange("p (tt d) -> p tt d", d=1024),
        xq.rearrange("(tt p) d -> p tt d", p=128),
        writes=[("XS", 0, TTN * 1024)])
    for dst, src, key in [
            (UT[:, :], ut_in[:, :], "UT"),
            (ONESROW[:, :], onesrow_in[:, :], "ONESROW"), (ONES128[:, :], ones128_in[:, :], "ONES128"),
            (IDENTF[:, :], identf_in[:, :], "IDENTF"), (IOTAP[:, :], iotap_in[:, :], "IOTAP"),
            (IOTA8T[:, :], iota8t_in[:, :], "IOTA8T"), (IOTA128T[:, :], iota128_in[:, :], "IOTA128T"), (SCOL32[:, :], scol32_in[:, :], "SCOL32"), (DIAG0[:, :], diag0_in[:, :], "DIAG0"),
            (DIAGU[:, :], diagu_in[:, :], "DIAGU"), (DIAGL[:, :], diagl_in[:, :], "DIAGL"),
            (SEL0B[:, :], sel0b_in[:, :], "SEL0B"), (SEL1B[:, :], sel1b_in[:, :], "SEL1B"),
            (CVAL[:, :], cval_in[:, :], "CVAL"), (MYIDX[:, :], myidx_in[:, :], "MYIDX")]:
        DMA("sp", dst, src, writes=[rS(key)])
    DMA("sp", SCB[:, 0:1], scale1[:, :], writes=[rS("SCB", 0, 1)])
    for ht in range(HTN):
        DMA("sp", B1[:, ht:ht + 1], b1c[ht, :, :], writes=[rS("B1", ht, ht + 1)])
    for dc in range(DCN):
        DMA("sp", WGF[:, dc * 8:(dc + 1) * 8], wg3[dc, :, :],
            writes=[rS("WGF", dc * 8, (dc + 1) * 8)])
    P.op("dve", lambda ec: ec.tensor_copy(out=WGB[:, :], in_=WGF[:, :]),
         reads=[rS("WGF", 0, 64)], writes=[rS("WGB")])

    P.op("dve", lambda ec: ec.memset(EPSC[:, :], 1e-5), writes=[rS("EPSC")])
    BREG = {}

    def _setup_pool_regs(ec):
        r1 = ec.alloc_register("bnd127")
        ec.reg_mov(r1, 127)
        BREG["b127"] = r1
        r2 = ec.alloc_register("bndNG")
        ec.reg_mov(r2, NG - 1)
        BREG["bNG"] = r2
        return ec.nop()

    P.nop_op("pool", _setup_pool_regs)
    if rstop != 99:
        P.op("pool", lambda ec: ec.memset(BUF_A[:, :], 0.0), writes=[rA(0, 16384)])
        P.op("pool", lambda ec: ec.memset(BUF_B[:, :], 0.0), writes=[rB(0, 16384)])

    DMA("sp", RSC[:, 0:1024], b2c[:, :], writes=[rS("RSC", 0, 1024)])
    for j in range(2):
        P.op("pe", lambda ec, j=j: ec.matmul(out=PS0[:, j * 512:(j + 1) * 512], lhsT=ONESROW[:, :],
                                             rhs=RSC[:, j * 512:(j + 1) * 512], start=True, stop=True),
             reads=[rS("ONESROW"), rS("RSC", j * 512, (j + 1) * 512)], writes=[rP0(j * 512, (j + 1) * 512)])
    P.op("dve", lambda ec: ec.tensor_copy(out=B2B[:, :], in_=PS0[:, 0:1024]),
         reads=[rP0(0, 1024)], writes=[rS("B2B")])

    P.op("dve", lambda ec: ec.reciprocal(out=SCB[:, 1:2], in_=SCB[:, 0:1]),
         reads=[rS("SCB", 0, 1)], writes=[rS("SCB", 1, 2)])
    P.op("dve", lambda ec: ec.tensor_single_scalar(out=SCB[:, 2:3], in_=SCB[:, 1:2], scalar=2.0, op=ALU.mult),
         reads=[rS("SCB", 1, 2)], writes=[rS("SCB", 2, 3)])
    P.op("dve", lambda ec: ec.tensor_single_scalar(out=SCB[:, 3:4], in_=SCB[:, 0:1], scalar=0.0, op=ALU.is_gt),
         reads=[rS("SCB", 0, 1)], writes=[rS("SCB", 3, 4)])
    P.op("pe", lambda ec: ec.matmul(out=PS1[:, 0:1], lhsT=ONESROW[:, :], rhs=SCB[:, 2:3], start=True, stop=True),
         reads=[rS("ONESROW"), rS("SCB", 2, 3)], writes=[rP1(0, 1)])
    P.op("pe", lambda ec: ec.matmul(out=PS1[:, 1:2], lhsT=ONESROW[:, :], rhs=SCB[:, 3:4], start=True, stop=True),
         reads=[rS("ONESROW"), rS("SCB", 3, 4)], writes=[rP1(1, 2)])
    P.op("dve", lambda ec: ec.tensor_copy(out=ALPHAC[:, :], in_=PS1[:, 0:1]),
         reads=[rP1(0, 1)], writes=[rS("ALPHAC")])
    P.op("dve", lambda ec: ec.tensor_copy(out=SGNPOS[:, :], in_=PS1[:, 1:2]),
         reads=[rP1(1, 2)], writes=[rS("SGNPOS")])

    def prep_cs(r, cdst, sdst):
        DMA("sp", RW[:, :], rwf[:, r * 1024:(r + 1) * 1024], writes=[rS("RW")])
        g0 = RW[:, :].rearrange("o (i two) -> o i two", two=2)[:, :, 0:1].squeeze(2)
        g1 = RW[:, :].rearrange("o (i two) -> o i two", two=2)[:, :, 1:2].squeeze(2)
        P.op("dve", lambda ec: ec.tensor_mul(out=RSC[:, 1024:1536], in0=g0, in1=g0),
             reads=[rS("RW")], writes=[rS("RSC", 1024, 1536)])
        P.op("dve", lambda ec: ec.tensor_mul(out=RSC[:, 1536:2048], in0=g1, in1=g1),
             reads=[rS("RW")], writes=[rS("RSC", 1536, 2048)])
        P.op("dve", lambda ec: ec.tensor_add(out=RSC[:, 1024:1536], in0=RSC[:, 1024:1536], in1=RSC[:, 1536:2048]),
             reads=[rS("RSC", 1024, 2048)], writes=[rS("RSC", 1024, 1536)])
        P.op("act", lambda ec: ec.activation(out=RSC[:, 1536:2048], in_=RSC[:, 1024:1536], func=AF.Sqrt),
             reads=[rS("RSC", 1024, 1536)], writes=[rS("RSC", 1536, 2048)])
        P.op("dve", lambda ec: ec.reciprocal(out=RSC[:, 1024:1536], in_=RSC[:, 1536:2048]),
             reads=[rS("RSC", 1536, 2048)], writes=[rS("RSC", 1024, 1536)])
        P.op("dve", lambda ec, cdst=cdst: ec.tensor_mul(out=RSC[:, cdst:cdst + 512], in0=g0, in1=RSC[:, 1024:1536]),
             reads=[rS("RW"), rS("RSC", 1024, 1536)], writes=[rS("RSC", cdst, cdst + 512)])
        P.op("dve", lambda ec, sdst=sdst: ec.tensor_mul(out=RSC[:, sdst:sdst + 512], in0=g1, in1=RSC[:, 1024:1536]),
             reads=[rS("RW"), rS("RSC", 1024, 1536)], writes=[rS("RSC", sdst, sdst + 512)])

    prep_cs(0, 0, 512)
    P.op("dve", lambda ec: ec.tensor_copy(out=ASTG[0:1, 0:1024], in_=RSC[:, 0:1024]),
         reads=[rS("RSC", 0, 1024)], writes=[("ASTG", 0, 1024)])
    prep_cs(1, 0, 512)
    P.op("dve", lambda ec: ec.tensor_mul(out=RSC[:, 1024:1536], in0=ASTG[0:1, 0:512], in1=RSC[:, 0:512]),
         reads=[("ASTG", 0, 512), rS("RSC", 0, 512)], writes=[rS("RSC", 1024, 1536)])
    P.op("dve", lambda ec: ec.tensor_mul(out=RSC[:, 1536:2048], in0=ASTG[0:1, 512:1024], in1=RSC[:, 512:1024]),
         reads=[("ASTG", 512, 1024), rS("RSC", 512, 1024)], writes=[rS("RSC", 1536, 2048)])
    P.op("dve", lambda ec: ec.tensor_add(out=RSC[:, 1024:1536], in0=RSC[:, 1024:1536], in1=RSC[:, 1536:2048]),
         reads=[rS("RSC", 1024, 2048)], writes=[rS("RSC", 1024, 1536)])
    P.op("dve", lambda ec: ec.tensor_mul(out=RSC[:, 1536:2048], in0=ASTG[0:1, 0:512], in1=RSC[:, 512:1024]),
         reads=[("ASTG", 0, 512), rS("RSC", 512, 1024)], writes=[rS("RSC", 1536, 2048)])
    P.op("dve", lambda ec: ec.tensor_mul(out=RSC[:, 0:512], in0=ASTG[0:1, 512:1024], in1=RSC[:, 0:512]),
         reads=[("ASTG", 512, 1024), rS("RSC", 0, 512)], writes=[rS("RSC", 0, 512)])
    P.op("dve", lambda ec: ec.tensor_sub(out=RSC[:, 1536:2048], in0=RSC[:, 1536:2048], in1=RSC[:, 0:512]),
         reads=[rS("RSC", 0, 512), rS("RSC", 1536, 2048)], writes=[rS("RSC", 1536, 2048)])

    def write_rep2(mbrow, src_lo, negate=False):
        tgt = ASTG[0:1, 1024:2048].rearrange("o (i two) -> o i two", two=2)
        if negate:
            P.op("dve", lambda ec: ec.tensor_single_scalar(out=ASTG[0:1, 512:1024], in_=RSC[:, src_lo:src_lo + 512],
                                                           scalar=-1.0, op=ALU.mult),
                 reads=[rS("RSC", src_lo, src_lo + 512)], writes=[("ASTG", 512, 1024)])
            srcap = ASTG[0:1, 512:1024]
            srcres = ("ASTG", 512, 1024)
        else:
            srcap = RSC[:, src_lo:src_lo + 512]
            srcres = rS("RSC", src_lo, src_lo + 512)
        P.op("dve", lambda ec: ec.tensor_copy(out=tgt[:, :, 0:1].squeeze(2), in_=srcap),
             reads=[srcres], writes=[("ASTG", 1024, 2048)])
        P.op("dve", lambda ec: ec.tensor_copy(out=tgt[:, :, 1:2].squeeze(2), in_=srcap),
             reads=[srcres], writes=[("ASTG", 1024, 2048)])
        return DMA("sp", mb_dram[mbrow:mbrow + 1, :], ASTG[0:1, 1024:2048],
                   reads=[("ASTG", 1024, 2048)], writes=[rD("mb", mbrow * 1024, (mbrow + 1) * 1024)])

    write_rep2(0, 1024)
    write_rep2(1, 1536)
    write_rep2(2, 1536, negate=True)
    prep_cs(2, 0, 512)
    write_rep2(3, 0)
    write_rep2(4, 512)
    write_rep2(5, 512, negate=True)

    for rows, dstbuf, key in [((0, 1, 2), MTBUF, "MTBUF"), ((3, 4, 5), GVTBUF, "GVTBUF")]:
        for dc in range(DCN):
            for j, r in enumerate(rows):
                DMA("sp", SRCH[:, 384 + j:385 + j], mb_dram[r:r + 1, dc * 128:(dc + 1) * 128].rearrange("o p -> (o p)").unsqueeze(1),
                    reads=[rD("mb", r * 1024 + dc * 128, r * 1024 + (dc + 1) * 128)],
                    writes=[rS("SRCH", 384 + j, 385 + j)])
            P.op("dve", lambda ec: ec.tensor_scalar(out=SRCH[:, 0:128], in0=DIAG0[:, :], scalar1=SRCH[:, 384:385],
                                                    scalar2=None, op0=ALU.mult),
                 reads=[rS("DIAG0"), rS("SRCH", 384, 385)], writes=[rS("SRCH", 0, 128)])
            P.op("dve", lambda ec: ec.tensor_scalar(out=SRCH[:, 128:256], in0=DIAGU[:, :], scalar1=SRCH[:, 385:386],
                                                    scalar2=None, op0=ALU.mult),
                 reads=[rS("DIAGU"), rS("SRCH", 385, 386)], writes=[rS("SRCH", 128, 256)])
            P.op("dve", lambda ec: ec.tensor_add(out=SRCH[:, 0:128], in0=SRCH[:, 0:128], in1=SRCH[:, 128:256]),
                 reads=[rS("SRCH", 0, 256)], writes=[rS("SRCH", 0, 128)])
            P.op("dve", lambda ec: ec.tensor_scalar(out=SRCH[:, 128:256], in0=DIAGL[:, :], scalar1=SRCH[:, 386:387],
                                                    scalar2=None, op0=ALU.mult),
                 reads=[rS("DIAGL"), rS("SRCH", 386, 387)], writes=[rS("SRCH", 128, 256)])
            P.op("dve", lambda ec: ec.tensor_add(out=SRCH[:, 0:128], in0=SRCH[:, 0:128], in1=SRCH[:, 128:256]),
                 reads=[rS("SRCH", 0, 256)], writes=[rS("SRCH", 0, 128)])
            P.op("dve", lambda ec, dc=dc, dstbuf=dstbuf, key=key: ec.tensor_copy(
                out=dstbuf[:, dc * 128:(dc + 1) * 128], in_=SRCH[:, 0:128]),
                reads=[rS("SRCH", 0, 128)], writes=[rS(key, dc * 128, (dc + 1) * 128)])
    P.phase()

    for ht in range(HTN):
        src = w1c.rearrange("(dc p) (ht h) -> ht p dc h", p=128, h=128)[ht]
        DMA("pool", w1t_dram[ht].rearrange("p (dc h) -> p dc h", h=128), src,
            writes=[rD("w1t", ht, ht + 1)])
    for i in range(8):
        DMA("pool", w2bf_dram[i * 512:(i + 1) * 512, :], w2c[i * 512:(i + 1) * 512, :],
            writes=[rD("w2bf", i * 512, (i + 1) * 512)])
    P.phase()

    def emit_ln(dram_dst, dname):
        toks = []
        for tt in range(TTN):
            xt = XS[:, tt * 1024:(tt + 1) * 1024]
            cb = (tt % 2) * 8
            P.op("dve", lambda ec, xt=xt, cb=cb: ec.reduce_sum(out=COLS[:, cb:cb + 1], in_=xt, axis=AX.X),
                 reads=[rXS(tt)], writes=[("COLS", cb, cb + 1)])
            P.op("dve", lambda ec, cb=cb: ec.tensor_single_scalar(out=COLS[:, cb + 1:cb + 2], in_=COLS[:, cb:cb + 1],
                                                                  scalar=1.0 / 1024, op=ALU.mult),
                 reads=[("COLS", cb, cb + 1)], writes=[("COLS", cb + 1, cb + 2)])
            P.op("dve", lambda ec, xt=xt, cb=cb: ec.tensor_scalar(out=XC[:, :], in0=xt, scalar1=COLS[:, cb + 1:cb + 2],
                                                                  scalar2=None, op0=ALU.subtract),
                 reads=[rXS(tt), ("COLS", cb + 1, cb + 2)], writes=[("XC", 0, 1024)])
            P.op("act", lambda ec, cb=cb, tt=tt: ec.activation(out=ASTG[:, (tt % 2) * 1024:(tt % 2) * 1024 + 1024],
                                                               in_=XC[:, :], func=AF.Square,
                                                               accum_out=COLS[:, cb + 2:cb + 3]),
                 reads=[("XC", 0, 1024)],
                 writes=[("ASTG", (tt % 2) * 1024, (tt % 2) * 1024 + 1024), ("COLS", cb + 2, cb + 3)])
            P.op("act", lambda ec, cb=cb: ec.activation(out=COLS[:, cb + 3:cb + 4], in_=COLS[:, cb + 2:cb + 3],
                                                        func=AF.Sqrt, bias=EPSC[:, 0:1], scale=1.0 / 1024),
                 reads=[("COLS", cb + 2, cb + 3), rS("EPSC")], writes=[("COLS", cb + 3, cb + 4)])
            P.op("dve", lambda ec, cb=cb: ec.reciprocal(out=COLS[:, cb + 4:cb + 5], in_=COLS[:, cb + 3:cb + 4]),
                 reads=[("COLS", cb + 3, cb + 4)], writes=[("COLS", cb + 4, cb + 5)])
            P.op("dve", lambda ec, cb=cb: ec.tensor_scalar(out=XLNBF[:, :], in0=XC[:, :], scalar1=COLS[:, cb + 4:cb + 5],
                                                           scalar2=None, op0=ALU.mult),
                 reads=[("XC", 0, 1024), ("COLS", cb + 4, cb + 5)], writes=[("XLNBF", 0, 1024)])
            toks.append(DMA("sp", dram_dst[tt * 128:(tt + 1) * 128, :], XLNBF[:, :],
                            reads=[("XLNBF", 0, 1024)], writes=[rD(dname, tt * 128, (tt + 1) * 128)]))
        return toks

    def emit_ag(in_ap, out_ap, rkey, wkey):
        return P.op("pool", lambda ec, i=in_ap, o=out_ap: ec.collective_compute(
            "AllGather", ALU.bypass, replica_groups=[list(range(8))], ins=[i], outs=[o]),
            reads=[rkey], writes=[wkey], kind="cc")

    def emit_ffn(tag):
        emit_ln(xln_dram, "xln")
        P.phase()
        for dc in range(DCN):
            DMA("sp", BUF_B[:, dc * 2048:(dc + 1) * 2048], xln_dram[:, dc * 128:(dc + 1) * 128],
                reads=[rD("xln")], writes=[rB(dc * 2048, (dc + 1) * 2048)], transpose=True)
        for tt in range(TTN):
            for dc in range(DCN):
                P.op("pe", lambda ec, tt=tt, dc=dc: ec.matmul(
                    out=PS0[:, 0:8],
                    lhsT=BUF_B[:, dc * 2048 + tt * 128: dc * 2048 + (tt + 1) * 128],
                    rhs=WGB[:, dc * 8:(dc + 1) * 8],
                    start=(dc == 0), stop=(dc == DCN - 1)),
                    reads=[rB(dc * 2048 + tt * 128, dc * 2048 + (tt + 1) * 128), rS("WGB")],
                    writes=[rP0(0, 8)])
            P.op("dve", lambda ec, tt=tt: ec.tensor_copy(out=LSTG[:, tt * 8:(tt + 1) * 8], in_=PS0[:, 0:8]),
                 reads=[rP0(0, 8)], writes=[("LSTG", tt * 8, (tt + 1) * 8)])
        DMA("sp", logit_dram.rearrange("(tt p) e -> p tt e", p=128),
            LSTG[:, :].rearrange("p (tt e) -> p tt e", e=8),
            reads=[("LSTG", 0, 128)], writes=[rD("logit")])
        P.phase()
        emit_ag(logit_dram[:, :], agl_out[:, :], rD("logit"), rD("agl"))
        P.phase()
        emit_ag(xln_dram[:, :], agx_out[:, :], rD("xln"), rD("agx"))
        P.phase()
        if stage <= 1:
            return

        DMA("sp", LG.rearrange("p (tau e) -> p tau e", e=8), agl_out.rearrange("(tau p) e -> p tau e", p=128),
            reads=[rD("agl")], writes=[rA(0, 2048)])
        lg3 = LG.rearrange("p (g e) -> p g e", e=8)
        io3 = IOTA8T[:, :].rearrange("p (g e) -> p g e", e=8)
        sc3 = SCR1[:, :].rearrange("p (g e) -> p g e", e=8)
        P.op("dve", lambda ec: ec.reduce_max(out=RS(1536), in_=lg3, axis=AX.X),
             reads=[rA(0, 2048)], writes=[rB(12288, 12544)])
        if rstop == 1:
            return
        P.op("dve", lambda ec: ec.tensor_tensor(out=sc3, in0=lg3, in1=RS(1536).unsqueeze(2).to_broadcast([128, 128, 8]),
                                                op=ALU.is_equal),
             reads=[rA(0, 2048), rB(12288, 12544)], writes=[rA(14336, 16384)])
        if rstop == 2:
            return
        P.op("dve", lambda ec: ec.scalar_tensor_tensor(out=SCR1[:, :], in0=SCR1[:, :], scalar=-16.0, in1=IOTA8T[:, :],
                                                       op0=ALU.mult, op1=ALU.add),
             reads=[rA(14336, 16384), rS("IOTA8T")], writes=[rA(14336, 16384)])
        if rstop == 3:
            return
        P.op("dve", lambda ec: ec.tensor_reduce(out=RS(0), in_=sc3, axis=AX.X, op=ALU.min),
             reads=[rA(14336, 16384)], writes=[rB(9216, 9472)])
        if rstop == 4:
            return
        P.op("dve", lambda ec: ec.tensor_single_scalar(out=RS(0), in_=RS(0), scalar=16.0, op=ALU.add),
             reads=[rB(9216, 9472)], writes=[rB(9216, 9472)])
        P.op("dve", lambda ec: ec.tensor_tensor(out=OH1[:, :].rearrange("p (g e) -> p g e", e=8), in0=io3,
                                                in1=RS(0).unsqueeze(2).to_broadcast([128, 128, 8]), op=ALU.is_equal),
             reads=[rS("IOTA8T"), rB(9216, 9472)], writes=[rB(0, 2048)])
        P.op("dve", lambda ec: ec.scalar_tensor_tensor(out=SCR1[:, :], in0=OH1[:, :], scalar=-1e30, in1=LG[:, :],
                                                       op0=ALU.mult, op1=ALU.add),
             reads=[rB(0, 2048), rA(0, 2048)], writes=[rA(14336, 16384)])
        P.op("dve", lambda ec: ec.reduce_max(out=RS(1664), in_=sc3, axis=AX.X),
             reads=[rA(14336, 16384)], writes=[rB(12544, 12800)])
        P.op("dve", lambda ec: ec.tensor_tensor(out=sc3, in0=sc3, in1=RS(1664).unsqueeze(2).to_broadcast([128, 128, 8]),
                                                op=ALU.is_equal),
             reads=[rA(14336, 16384), rB(12544, 12800)], writes=[rA(14336, 16384)])
        P.op("dve", lambda ec: ec.scalar_tensor_tensor(out=SCR1[:, :], in0=SCR1[:, :], scalar=-16.0, in1=IOTA8T[:, :],
                                                       op0=ALU.mult, op1=ALU.add),
             reads=[rA(14336, 16384), rS("IOTA8T")], writes=[rA(14336, 16384)])
        P.op("dve", lambda ec: ec.tensor_reduce(out=RS(128), in_=sc3, axis=AX.X, op=ALU.min),
             reads=[rA(14336, 16384)], writes=[rB(9472, 9728)])
        P.op("dve", lambda ec: ec.tensor_single_scalar(out=RS(128), in_=RS(128), scalar=16.0, op=ALU.add),
             reads=[rB(9472, 9728)], writes=[rB(9472, 9728)])
        P.op("dve", lambda ec: ec.tensor_tensor(out=OH2[:, :].rearrange("p (g e) -> p g e", e=8), in0=io3,
                                                in1=RS(128).unsqueeze(2).to_broadcast([128, 128, 8]), op=ALU.is_equal),
             reads=[rS("IOTA8T"), rB(9472, 9728)], writes=[rB(2048, 4096)])
        P.op("dve", lambda ec: ec.tensor_sub(out=RS(1792), in0=RS(1664), in1=RS(1536)),
             reads=[rB(12288, 12800)], writes=[rB(12800, 13056)])
        P.op("act", lambda ec: ec.activation(out=RS(256), in_=RS(1792), func=AF.Sigmoid, scale=-1.0),
             reads=[rB(12800, 13056)], writes=[rB(9728, 9984)])
        P.op("dve", lambda ec: ec.tensor_scalar(out=RS(384), in0=RS(256), scalar1=-1.0, scalar2=1.0,
                                                op0=ALU.mult, op1=ALU.add),
             reads=[rB(9728, 9984)], writes=[rB(9984, 10240)])
        m3 = MARR.rearrange("p (tau j) -> p tau j", j=16)
        P.op("dve", lambda ec: ec.tensor_copy(out=m3[:, :, 0:8], in_=OH1[:, :].rearrange("p (g e) -> p g e", e=8)),
             reads=[rB(0, 2048)], writes=[rA(2048, 6144)])
        P.op("dve", lambda ec: ec.tensor_copy(out=m3[:, :, 8:16], in_=OH2[:, :].rearrange("p (g e) -> p g e", e=8)),
             reads=[rB(2048, 4096)], writes=[rA(2048, 6144)])
        for j in range(4):
            P.op("pe", lambda ec, j=j: ec.matmul(out=PS0[:, j * 512:(j + 1) * 512], lhsT=UT[:, :],
                                                 rhs=MARR[:, j * 512:(j + 1) * 512], start=True, stop=True),
                 reads=[rS("UT"), rA(2048, 6144)], writes=[rP0(j * 512, (j + 1) * 512)])
        P.op("dve", lambda ec: ec.tensor_copy(out=GEXI[:, :], in_=PS0[:, :]),
             reads=[rP0()], writes=[rA(6144, 10240)])
        for j in range(4):
            P.op("pe", lambda ec, j=j: ec.matmul(out=PSROW[:, j * 512:(j + 1) * 512], lhsT=ONES128[:, :],
                                                 rhs=MARR[:, j * 512:(j + 1) * 512], start=True, stop=True),
                 reads=[rS("ONES128"), rA(2048, 6144)], writes=[rP1(j * 512, (j + 1) * 512)])
        P.op("dve", lambda ec: ec.tensor_copy(out=TROW[:, :], in_=PSROW[:, :]),
             reads=[rP1()], writes=[rS("RSC", 0, 2048)])
        DMA("sp", pack_dram.unsqueeze(0), TROW[:, :], reads=[rS("RSC", 0, 2048)], writes=[rD("pack")])
        DMA("sp", TTILE[:, :], pack_dram.rearrange("(tau j) -> tau j", j=16),
            reads=[rD("pack")], writes=[rS("TTILE")])
        P.op("pe", lambda ec: ec.matmul(out=PS1[:, 0:16], lhsT=UT[:, :], rhs=TTILE[:, :], start=True, stop=True),
             reads=[rS("UT"), rS("TTILE")], writes=[rP1(0, 16)])
        P.op("dve", lambda ec: ec.tensor_copy(out=TPEX[:, :], in_=PS1[:, 0:16]),
             reads=[rP1(0, 16)], writes=[rS("TPEX")])
        P.op("pe", lambda ec: ec.matmul(out=PS0[0:1, 0:16], lhsT=ONES128[:, :], rhs=TTILE[:, :], start=True, stop=True),
             reads=[rS("ONES128"), rS("TTILE")], writes=[rP0(0, 16)])
        P.op("dve", lambda ec: ec.tensor_copy(out=CROW[:, :], in_=PS0[0:1, 0:16]),
             reads=[rP0(0, 16)], writes=[rS("CROW")])
        P.op("dve", lambda ec: ec.tensor_add(out=SCR2[:, 0:16], in0=TPEX[:, :], in1=TTILE[:, :]),
             reads=[rS("TPEX"), rS("TTILE")], writes=[rB(4096, 4128)])
        P.op("dve", lambda ec: ec.tensor_mul(out=SCR2[:, 16:32], in0=SCR2[:, 0:16], in1=SEL0B[:, :]),
             reads=[rB(4096, 4128), rS("SEL0B")], writes=[rB(4128, 4160)])
        P.op("dve", lambda ec: ec.reduce_sum(out=SCR2[:, 48:49], in_=SCR2[:, 16:32], axis=AX.X),
             reads=[rB(4128, 4160)], writes=[rB(4192, 4194)])
        P.op("dve", lambda ec: ec.tensor_mul(out=SCR2[:, 16:32], in0=SCR2[:, 0:16], in1=SEL1B[:, :]),
             reads=[rB(4096, 4128), rS("SEL1B")], writes=[rB(4128, 4160)])
        P.op("dve", lambda ec: ec.reduce_sum(out=SCR2[:, 49:50], in_=SCR2[:, 16:32], axis=AX.X),
             reads=[rB(4128, 4160)], writes=[rB(4194, 4196)])
        DMA("sp", gt_dram[0:1, :].rearrange("o p -> (o p)").unsqueeze(1), SCR2[:, 48:49],
            reads=[rB(4192, 4194)], writes=[rD("gt", 0, 128)])
        DMA("sp", gt_dram[1:2, :].rearrange("o p -> (o p)").unsqueeze(1), SCR2[:, 49:50],
            reads=[rB(4194, 4196)], writes=[rD("gt", 128, 256)])
        DMA("sp", TROW[:, 0:128], gt_dram[0:1, :], reads=[rD("gt", 0, 128)], writes=[rS("RSC", 0, 128)])
        DMA("sp", TROW[:, 128:256], gt_dram[1:2, :], reads=[rD("gt", 128, 256)], writes=[rS("RSC", 128, 256)])
        P.op("pe", lambda ec: ec.matmul(out=PS1[:, 256:384], lhsT=ONESROW[:, :], rhs=TROW[:, 0:128],
                                        start=True, stop=True),
             reads=[rS("ONESROW"), rS("RSC", 0, 128)], writes=[rP1(256, 384)])
        P.op("pe", lambda ec: ec.matmul(out=PS1[:, 384:512], lhsT=ONESROW[:, :], rhs=TROW[:, 128:256],
                                        start=True, stop=True),
             reads=[rS("ONESROW"), rS("RSC", 128, 256)], writes=[rP1(384, 512)])
        P.op("dve", lambda ec: ec.tensor_copy(out=GTB[:, :], in_=PS1[:, 256:512]),
             reads=[rP1(256, 512)], writes=[rS("GTB")])
        P.op("dve", lambda ec: ec.tensor_mul(out=TROW[:, 256:272], in0=CROW[:, :], in1=SEL0B[0:1, :]),
             reads=[rS("CROW"), rS("SEL0B")], writes=[rS("RSC", 256, 272)])
        P.op("dve", lambda ec: ec.reduce_sum(out=TROW[:, 272:273], in_=TROW[:, 256:272], axis=AX.X),
             reads=[rS("RSC", 256, 272)], writes=[rS("RSC", 272, 273)])
        P.op("pe", lambda ec: ec.matmul(out=PS1[:, 512:513], lhsT=ONESROW[:, :], rhs=TROW[:, 272:273],
                                        start=True, stop=True),
             reads=[rS("ONESROW"), rS("RSC", 272, 273)], writes=[rP1(512, 513)])
        P.op("dve", lambda ec: ec.tensor_copy(out=C0COL[:, :], in_=PS1[:, 512:513]),
             reads=[rP1(512, 513)], writes=[rS("C0COL")])
        DMA("sp", off_dram.rearrange("(tau j) -> tau j", j=16), TPEX[:, :],
            reads=[rS("TPEX")], writes=[rD("off")])
        DMA("sp", TROW[:, :], off_dram.unsqueeze(0), reads=[rD("off")], writes=[rS("RSC", 0, 2048)])
        for j in range(4):
            P.op("pe", lambda ec, j=j: ec.matmul(out=PS0[:, j * 512:(j + 1) * 512], lhsT=ONESROW[:, :],
                                                 rhs=TROW[:, j * 512:(j + 1) * 512], start=True, stop=True),
                 reads=[rS("ONESROW"), rS("RSC", 0, 2048)], writes=[rP0(j * 512, (j + 1) * 512)])
        P.op("dve", lambda ec: ec.tensor_add(out=GEX[:, :], in0=GEXI[:, :], in1=PS0[:, :]),
             reads=[rA(6144, 10240), rP0()], writes=[rA(10240, 14336)])
        gex3 = GEX.rearrange("p (tau j) -> p tau j", j=16)
        P.op("dve", lambda ec: ec.tensor_tensor(out=sc3, in0=gex3[:, :, 0:8],
                                                in1=OH1[:, :].rearrange("p (g e) -> p g e", e=8), op=ALU.mult),
             reads=[rA(10240, 14336), rB(0, 2048)], writes=[rA(14336, 16384)])
        P.op("dve", lambda ec: ec.reduce_sum(out=RS(512), in_=sc3, axis=AX.X),
             reads=[rA(14336, 16384)], writes=[rB(10240, 10496)])
        P.op("dve", lambda ec: ec.tensor_tensor(out=sc3, in0=gex3[:, :, 8:16],
                                                in1=OH2[:, :].rearrange("p (g e) -> p g e", e=8), op=ALU.mult),
             reads=[rA(10240, 14336), rB(2048, 4096)], writes=[rA(14336, 16384)])
        P.op("dve", lambda ec: ec.reduce_sum(out=RS(640), in_=sc3, axis=AX.X),
             reads=[rA(14336, 16384)], writes=[rB(10496, 10752)])
        P.op("pe", lambda ec: ec.matmul(out=PS1[:, 516:524], lhsT=ONESROW[:, :], rhs=CROW[0:1, 0:8],
                                        start=True, stop=True),
             reads=[rS("ONESROW"), rS("CROW")], writes=[rP1(516, 524)])
        P.op("dve", lambda ec: ec.tensor_copy(out=SRCH[:, 0:8], in_=PS1[:, 516:524]),
             reads=[rP1(516, 524)], writes=[rS("SRCH", 0, 8)])
        P.op("dve", lambda ec: ec.tensor_tensor(out=sc3, in0=OH2[:, :].rearrange("p (g e) -> p g e", e=8),
                                                in1=SRCH[:, 0:8].unsqueeze(1).to_broadcast([128, 128, 8]),
                                                op=ALU.mult),
             reads=[rB(2048, 4096), rS("SRCH", 0, 8)], writes=[rA(14336, 16384)])
        P.op("dve", lambda ec: ec.reduce_sum(out=RS(1536), in_=sc3, axis=AX.X),
             reads=[rA(14336, 16384)], writes=[rB(12288, 12544)])
        P.op("dve", lambda ec: ec.tensor_add(out=RS(640), in0=RS(640), in1=RS(1536)),
             reads=[rB(12288, 12544), rB(10496, 10752)], writes=[rB(10496, 10752)])
        P.op("dve", lambda ec: ec.tensor_single_scalar(out=RS(768), in_=RS(512), scalar=float(CAP), op=ALU.is_lt),
             reads=[rB(10240, 10496)], writes=[rB(10752, 11008)])
        P.op("dve", lambda ec: ec.tensor_single_scalar(out=RS(896), in_=RS(640), scalar=float(CAP), op=ALU.is_lt),
             reads=[rB(10496, 10752)], writes=[rB(11008, 11264)])
        P.op("dve", lambda ec: ec.tensor_mul(out=RS(1024), in0=RS(256), in1=RS(768)),
             reads=[rB(9728, 9984), rB(10752, 11008)], writes=[rB(11264, 11520)])
        P.op("dve", lambda ec: ec.tensor_mul(out=RS(1152), in0=RS(384), in1=RS(896)),
             reads=[rB(9984, 10240), rB(11008, 11264)], writes=[rB(11520, 11776)])
        for k, (posoff, keepoff, eoff, dstoff) in enumerate([(512, 768, 0, 1280), (640, 896, 128, 1408)]):
            P.op("dve", lambda ec, posoff=posoff: ec.tensor_single_scalar(out=RS(1536), in_=RS(posoff),
                                                                          scalar=2048.0, op=ALU.is_ge),
                 reads=[rB(9216 + 2*(posoff), 9216 + 2*(posoff + 128))], writes=[rB(12288, 12544)])
            P.op("dve", lambda ec, posoff=posoff: ec.scalar_tensor_tensor(out=RS(1664), in0=RS(1536), scalar=-2048.0,
                                                                          in1=RS(posoff), op0=ALU.mult, op1=ALU.add),
                 reads=[rB(12288, 12544), rB(9216 + 2*(posoff), 9216 + 2*(posoff + 128))], writes=[rB(12544, 12800)])
            P.op("dve", lambda ec: ec.tensor_single_scalar(out=RS(1536), in_=RS(1536), scalar=16384.0, op=ALU.mult),
                 reads=[rB(12288, 12544)], writes=[rB(12288, 12544)])
            P.op("dve", lambda ec, eoff=eoff: ec.scalar_tensor_tensor(out=RS(1536), in0=RS(eoff), scalar=2048.0,
                                                                      in1=RS(1536), op0=ALU.mult, op1=ALU.add),
                 reads=[rB(9216 + 2*(eoff), 9216 + 2*(eoff + 128)), rB(12288, 12544)], writes=[rB(12288, 12544)])
            P.op("dve", lambda ec: ec.tensor_add(out=RS(1536), in0=RS(1536), in1=RS(1664)),
                 reads=[rB(12288, 12800)], writes=[rB(12288, 12544)])
            P.op("dve", lambda ec, eoff=eoff: ec.tensor_single_scalar(out=RS(1664), in_=RS(eoff), scalar=2048.0,
                                                                      op=ALU.mult),
                 reads=[rB(9216 + 2*(eoff), 9216 + 2*(eoff + 128))], writes=[rB(12544, 12800)])
            P.op("dve", lambda ec: ec.tensor_sub(out=RS(1536), in0=RS(1536), in1=RS(1664)),
                 reads=[rB(12288, 12800)], writes=[rB(12288, 12544)])
            P.op("dve", lambda ec, keepoff=keepoff: ec.tensor_mul(out=RS(1536), in0=RS(1536), in1=RS(keepoff)),
                 reads=[rB(12288, 12544), rB(9216 + 2*(keepoff), 9216 + 2*(keepoff + 128))], writes=[rB(12288, 12544)])
            P.op("dve", lambda ec, dstoff=dstoff: ec.tensor_add(out=RS(dstoff), in0=RS(1536), in1=RS(1664)),
                 reads=[rB(12288, 12800)], writes=[rB(9216 + 2*(dstoff), 9216 + 2*(dstoff + 128))])
        rs2v = RS2.rearrange("p (tau k) -> p tau k", k=2)
        rg2v = RG2.rearrange("p (tau k) -> p tau k", k=2)
        P.op("dve", lambda ec: ec.tensor_copy(out=rs2v[:, :, 0:1].squeeze(2), in_=RS(1280)),
             reads=[rB(11776, 12032)], writes=[rB(8192, 8704)])
        P.op("dve", lambda ec: ec.tensor_copy(out=rs2v[:, :, 1:2].squeeze(2), in_=RS(1408)),
             reads=[rB(12032, 12288)], writes=[rB(8192, 8704)])
        P.op("dve", lambda ec: ec.tensor_copy(out=rg2v[:, :, 0:1].squeeze(2), in_=RS(1024)),
             reads=[rB(11264, 11520)], writes=[rB(8704, 9216)])
        P.op("dve", lambda ec: ec.tensor_copy(out=rg2v[:, :, 1:2].squeeze(2), in_=RS(1152)),
             reads=[rB(11520, 11776)], writes=[rB(8704, 9216)])
        P.op("dve", lambda ec: ec.tensor_single_scalar(out=MYRAW[:, 0:1], in_=CVAL[:, :], scalar=16.0, op=ALU.mult),
             reads=[rS("CVAL")], writes=[rS("MYRAW", 0, 1)])
        for tt in range(TTN):
            P.op("dve", lambda ec, tt=tt: ec.tensor_single_scalar(out=MYRAW[:, 1:2], in_=MYRAW[:, 0:1],
                                                                  scalar=float(tt), op=ALU.add),
                 reads=[rS("MYRAW", 0, 1)], writes=[rS("MYRAW", 1, 2)])
            P.op("dve", lambda ec: ec.tensor_scalar(out=SRCH[:, 128:256],
                                                    in0=IOTA128T[:, :], scalar1=MYRAW[:, 1:2],
                                                    scalar2=None, op0=ALU.is_equal),
                 reads=[rS("IOTA128T"), rS("MYRAW", 1, 2)], writes=[rS("SRCH", 128, 256)])
            for (packv, packres, dst, dstres) in [
                    (rs2v, rB(8192, 8704), MYSLOT, "MYSLOT"), (rg2v, rB(8704, 9216), MYGATE, "MYGATE")]:
                P.op("dve", lambda ec, packv=packv: ec.tensor_tensor(
                    out=SRCH[:, 256:512].rearrange("p (k tau) -> p k tau", tau=128),
                    in0=packv.transpose([0, 2, 1]),
                    in1=SRCH[:, 128:256].unsqueeze(1).to_broadcast([128, 2, 128]), op=ALU.mult),
                    reads=[packres, rS("SRCH", 128, 256)], writes=[rS("SRCH", 256, 512)])
                P.op("dve", lambda ec, tt=tt, dst=dst: ec.reduce_sum(
                    out=dst[:, 2 * tt:2 * tt + 2],
                    in_=SRCH[:, 256:512].rearrange("p (k tau) -> p k tau", tau=128), axis=AX.X),
                    reads=[rS("SRCH", 256, 512)], writes=[rS(dstres, 2 * tt, 2 * tt + 2)])
        P.op("dve", lambda ec: ec.tensor_copy(out=MYSLOTI[:, :], in_=MYSLOT[:, :]),
             reads=[rS("MYSLOT", 0, 32)], writes=[rS("MYSLOTI")])
        DMA("sp", myslots_dram.rearrange("(tt p) k -> p tt k", p=128),
            MYSLOT[:, :].rearrange("p (tt k) -> p tt k", k=2),
            reads=[rS("MYSLOT", 0, 32)], writes=[rD("myslots")])
        DMA("sp", mygates_dram.rearrange("(tt p) k -> p tt k", p=128),
            MYGATE[:, :].rearrange("p (tt k) -> p tt k", k=2),
            reads=[rS("MYGATE", 0, 32)], writes=[rD("mygates")])
        for mi_, (selb, selkey, ekey, eoff) in enumerate([(SEL0B, "SEL0B", "RSM", 0), (SEL1B, "SEL1B", "RSM", 128)]):
            P.op("dve", lambda ec, selb=selb: ec.tensor_tensor(
                out=SCR2[:, :].rearrange("p (tau j) -> p tau j", j=16), in0=gex3,
                in1=selb[:, :].unsqueeze(1).to_broadcast([128, 128, 16]), op=ALU.mult),
                reads=[rA(10240, 14336), rS(selkey)], writes=[rB(4096, 8192)])
            P.op("dve", lambda ec: ec.reduce_sum(out=SRCH[:, 0:128],
                                                 in_=SCR2[:, :].rearrange("p (tau j) -> p tau j", j=16), axis=AX.X),
                 reads=[rB(4096, 8192)], writes=[rS("SRCH", 0, 128)])
            P.op("dve", lambda ec, eoff=eoff: ec.tensor_scalar(out=SRCH[:, 128:256], in0=RS(eoff),
                                                               scalar1=CVAL[:, :], scalar2=None, op0=ALU.is_equal),
                 reads=[rB(9216 + 2*(eoff), 9216 + 2*(eoff + 128)), rS("CVAL")], writes=[rS("SRCH", 128, 256)])
            P.op("dve", lambda ec: ec.tensor_add(out=SRCH[:, 0:128], in0=SRCH[:, 0:128], in1=SRCH[:, 128:256]),
                 reads=[rS("SRCH", 0, 256)], writes=[rS("SRCH", 0, 128)])
            P.op("pe", lambda ec: ec.transpose(out=PS1[:, 1024:1152],
                                               in_=SRCH[:, 0:128], identity=IDENTF[:, :]),
                 reads=[rS("SRCH", 0, 128), rS("IDENTF")], writes=[rP1(1024, 1152)])
            P.op("dve", lambda ec: ec.tensor_copy(out=SRCH[:, 256:384], in_=PS1[:, 1024:1152]),
                 reads=[rP1(1024, 1152)], writes=[rS("SRCH", 256, 384)])
            DMA("sp", (ginc0_dram if mi_ == 0 else ginc1_dram).rearrange("(tau p) -> tau p", p=128),
                SRCH[:, 256:384],
                reads=[rS("SRCH", 256, 384)], writes=[rD("ginc", mi_ * NG, (mi_ + 1) * NG)])
        P.phase()
        if stage <= 2:
            return

        CF32 = BUF_C[:, :].bitcast(F32)
        P.op("dve", lambda ec: ec.tensor_scalar(out=SRCH[:, 64:96], in0=SCOL32[:, :], scalar1=C0COL[:, :],
                                                scalar2=None, op0=ALU.subtract),
             reads=[rS("SCOL32"), rS("C0COL")], writes=[rS("SRCH", 64, 96)])
        P.op("dve", lambda ec: ec.tensor_scalar(out=SRCH[:, 96:128], in0=SCOL32[:, :], scalar1=C0COL[:, :],
                                                scalar2=None, op0=ALU.is_lt),
             reads=[rS("SCOL32"), rS("C0COL")], writes=[rS("SRCH", 96, 128)])
        for m, (gtl, gth, rcol, rres) in enumerate([(0, 128, SCOL32[:, :], rS("SCOL32")),
                                                    (128, 256, SRCH[:, 64:96], rS("SRCH", 64, 96))]):
            P.op("dve", lambda ec, gtl=gtl, gth=gth, rcol=rcol: ec.tensor_tensor(
                out=CF32[:, 4096:8192].rearrange("p (c t) -> p c t", t=128),
                in0=GTB[:, gtl:gth].unsqueeze(1).to_broadcast([128, 32, 128]),
                in1=rcol.unsqueeze(2).to_broadcast([128, 32, 128]), op=ALU.is_le),
                reads=[rS("GTB"), rres], writes=[rC(8192, 16384)])
            P.op("dve", lambda ec, m=m: ec.reduce_sum(out=SRCH[:, m * 32:m * 32 + 32],
                                                      in_=CF32[:, 4096:8192].rearrange("p (c t) -> p c t", t=128),
                                                      axis=AX.X),
                 reads=[rC(8192, 16384)], writes=[rS("SRCH", m * 32, m * 32 + 32)])
        P.op("dve", lambda ec: ec.tensor_copy(out=SRCHI[:, :], in_=SRCH[:, 0:64]),
             reads=[rS("SRCH", 0, 64)], writes=[rS("SRCHI", 0, 64)])
        for m in range(2):
            gsrc = ginc0_dram if m == 0 else ginc1_dram
            for sc in range(STN):
                j = m * 32 + sc
                P.op("pool", lambda ec, gsrc=gsrc, j=j, m=m, sc=sc: ec.indirect_dma_start(
                    out=CF32[:, j * 128:(j + 1) * 128], out_offset=None,
                    in_=gsrc.rearrange("(b w) -> b w", w=128),
                    in_offset=IndirectOffsetOnAxis(SRCHI[:, j:j + 1], 0),
                    bounds_check=BREG["b127"], oob_is_err=False),
                    reads=[rD("ginc", m * NG, (m + 1) * NG), rS("SRCHI", j, j + 1)],
                    writes=[rC(j * 256, (j + 1) * 256)], kind="dma")
        for m, (rcol, rres) in enumerate([(SCOL32[:, :], rS("SCOL32")), (SRCH[:, 64:96], rS("SRCH", 64, 96))]):
            for h in range(2):
                P.op("dve", lambda ec, m=m, h=h, rcol=rcol: ec.tensor_tensor(
                    out=SCR2[:, :].rearrange("p (c t) -> p c t", t=128),
                    in0=CF32[:, m * 4096 + h * 2048: m * 4096 + (h + 1) * 2048].rearrange("p (c t) -> p c t", t=128),
                    in1=rcol[:, h * 16:(h + 1) * 16].unsqueeze(2).to_broadcast([128, 16, 128]), op=ALU.is_le),
                    reads=[rC(m * 8192 + h * 4096, m * 8192 + (h + 1) * 4096), rres], writes=[rB(4096, 8192)])
                P.op("dve", lambda ec, m=m, h=h: ec.reduce_sum(
                    out=SRCH[:, 128 + m * 32 + h * 16: 128 + m * 32 + (h + 1) * 16],
                    in_=SCR2[:, :].rearrange("p (c t) -> p c t", t=128), axis=AX.X),
                    reads=[rB(4096, 8192)],
                    writes=[rS("SRCH", 128 + m * 32 + h * 16, 128 + m * 32 + (h + 1) * 16)])
        P.op("dve", lambda ec: ec.scalar_tensor_tensor(out=SRCH[:, 192:224], in0=SRCH[:, 0:32], scalar=128.0,
                                                       in1=SRCH[:, 128:160], op0=ALU.mult, op1=ALU.add),
             reads=[rS("SRCH", 0, 32), rS("SRCH", 128, 160)], writes=[rS("SRCH", 192, 224)])
        P.op("dve", lambda ec: ec.scalar_tensor_tensor(out=SRCH[:, 224:256], in0=SRCH[:, 32:64], scalar=128.0,
                                                       in1=SRCH[:, 160:192], op0=ALU.mult, op1=ALU.add),
             reads=[rS("SRCH", 32, 64), rS("SRCH", 160, 192)], writes=[rS("SRCH", 224, 256)])
        P.op("dve", lambda ec: ec.tensor_sub(out=SRCH[:, 256:288], in0=SRCH[:, 192:224], in1=SRCH[:, 224:256]),
             reads=[rS("SRCH", 192, 256)], writes=[rS("SRCH", 256, 288)])
        P.op("dve", lambda ec: ec.tensor_mul(out=SRCH[:, 256:288], in0=SRCH[:, 256:288], in1=SRCH[:, 96:128]),
             reads=[rS("SRCH", 256, 288), rS("SRCH", 96, 128)], writes=[rS("SRCH", 256, 288)])
        P.op("dve", lambda ec: ec.tensor_add(out=SRCH[:, 256:288], in0=SRCH[:, 256:288], in1=SRCH[:, 224:256]),
             reads=[rS("SRCH", 224, 288)], writes=[rS("SRCH", 256, 288)])
        P.op("dve", lambda ec: ec.tensor_copy(out=TOKIDX[:, :], in_=SRCH[:, 256:288]),
             reads=[rS("SRCH", 256, 288)], writes=[rS("TOKIDX", 0, 32)])
        P.phase()

        for st in range(STN):
            par = st % 2
            dst = BUF_C[:, 8192 + par * 1024: 8192 + (par + 1) * 1024]
            dres = rC(8192 + par * 1024, 8192 + (par + 1) * 1024)
            P.op("pool", lambda ec, dst=dst: ec.memset(dst, 0.0), reads=[], writes=[dres])
            P.op("pool", lambda ec, st=st, dst=dst: ec.indirect_dma_start(
                out=dst, out_offset=None,
                in_=agx_out[:, :],
                in_offset=IndirectOffsetOnAxis(TOKIDX[:, st:st + 1], 0),
                bounds_check=BREG["bNG"], oob_is_err=False),
                reads=[rD("agx"), rS("TOKIDX", st, st + 1)], writes=[dres], kind="dma")
            DMA("sp", disp_dram[st * 128:(st + 1) * 128, :], dst,
                reads=[dres], writes=[rD("disp", st * 128, (st + 1) * 128)])
        P.phase()
        if stage <= 3:
            return

        PSH = [PS0[:, 0:512], PS0[:, 512:1024]]
        psh_res = [rP0(0, 512), rP0(512, 1024)]
        psy = [(PS0[:, 0:1024], rP0(0, 1024)), (PS0[:, 1024:2048], rP0(1024, 2048)),
               (PS1[:, 0:1024], rP1(0, 1024)), (PS1[:, 1024:2048], rP1(1024, 2048))]
        W1B = [(BUF_C[:, 10240:11264], rC(10240, 11264)), (BUF_C[:, 11264:12288], rC(11264, 12288))]
        W2B = [(BUF_C[:, 12288:13312], rC(12288, 13312)), (BUF_C[:, 13312:14336], rC(13312, 14336))]
        YSTG = BUF_C[:, 14336:15360]
        ystg_res = rC(14336, 15360)
        for sc8 in range(SCN):
            par8 = sc8 % 2
            dbase = par8 * 4096
            for dc in range(DCN):
                DMA("sp", BUF_B[:, dbase + dc * 512: dbase + (dc + 1) * 512],
                    disp_dram[sc8 * 512:(sc8 + 1) * 512, dc * 128:(dc + 1) * 128],
                    reads=[rD("disp", sc8 * 512, (sc8 + 1) * 512)],
                    writes=[rB(dbase + dc * 512, dbase + (dc + 1) * 512)], transpose=True)
            for ht in range(HTN):
                parw = ht % 2
                DMA("sp", W1B[parw][0], w1t_dram[ht], reads=[rD("w1t", ht, ht + 1)], writes=[W1B[parw][1]])
                for dc in range(DCN):
                    P.op("pe", lambda ec, ht=ht, dc=dc, parw=parw, dbase=dbase: ec.matmul(
                        out=PSH[ht % 2], lhsT=BUF_C[:, 10240 + parw * 1024 + dc * 128: 10240 + parw * 1024 + (dc + 1) * 128],
                        rhs=BUF_B[:, dbase + dc * 512: dbase + (dc + 1) * 512],
                        start=(dc == 0), stop=(dc == DCN - 1)),
                        reads=[W1B[parw][1], rB(dbase + dc * 512, dbase + (dc + 1) * 512)],
                        writes=[psh_res[ht % 2]])
                P.op("act", lambda ec, ht=ht: ec.activation(out=BUF_A[:, ht * 512:(ht + 1) * 512],
                                                            in_=PSH[ht % 2], func=AF.Gelu,
                                                            bias=B1[:, ht:ht + 1], scale=1.0),
                     reads=[psh_res[ht % 2], rS("B1", ht, ht + 1)], writes=[rA(ht * 512, (ht + 1) * 512)])
            for ht in range(HTN):
                parw = ht % 2
                DMA("sp", W2B[parw][0], w2bf_dram[ht * 128:(ht + 1) * 128, :],
                    reads=[rD("w2bf", ht * 128, (ht + 1) * 128)], writes=[W2B[parw][1]])
                for st4 in range(4):
                    for hf in range(2):
                        P.op("pe", lambda ec, ht=ht, st4=st4, hf=hf, parw=parw: ec.matmul(
                            out=psy[st4][0][:, hf * 512:(hf + 1) * 512],
                            lhsT=BUF_A[:, ht * 512 + st4 * 128: ht * 512 + (st4 + 1) * 128],
                            rhs=BUF_C[:, 12288 + parw * 1024 + hf * 512: 12288 + parw * 1024 + (hf + 1) * 512],
                            start=(ht == 0), stop=(ht == HTN - 1)),
                            reads=[rA(ht * 512, (ht + 1) * 512), W2B[parw][1]],
                            writes=[psy[st4][1]])
            for st4 in range(4):
                P.op("dve", lambda ec, st4=st4: ec.tensor_add(out=YSTG[:, :], in0=psy[st4][0], in1=B2B[:, :]),
                     reads=[psy[st4][1], rS("B2B")], writes=[ystg_res])
                DMA("sp", y_dram[sc8 * 512 + st4 * 128: sc8 * 512 + (st4 + 1) * 128, :], YSTG[:, :],
                    reads=[ystg_res], writes=[rD("y", sc8 * 512 + st4 * 128, sc8 * 512 + (st4 + 1) * 128)])
            if sc8 in (3, 7):
                half = sc8 // 4
                P.phase()
                emit_ag(y_dram[half * 2048:(half + 1) * 2048, :],
                        agy_out[half * 16384:(half + 1) * 16384, :],
                        rD("y", half * 2048, (half + 1) * 2048),
                        rD("agy", half * 16384, (half + 1) * 16384))
                P.phase()

        for tt in range(TTN):
            par = tt % 2
            yk0 = BUF_C[:, 8192 + par * 2048: 8192 + par * 2048 + 1024]
            yk1 = BUF_C[:, 8192 + par * 2048 + 1024: 8192 + par * 2048 + 2048]
            r0 = rC(8192 + par * 2048, 8192 + par * 2048 + 1024)
            r1 = rC(8192 + par * 2048 + 1024, 8192 + par * 2048 + 2048)
            P.op("pool", lambda ec, tt=tt, yk0=yk0: ec.indirect_dma_start(
                out=yk0, out_offset=None, in_=agy_out[:, :],
                in_offset=IndirectOffsetOnAxis(MYSLOTI[:, 2 * tt:2 * tt + 1], 0)),
                reads=[rD("agy"), rS("MYSLOTI")], writes=[r0], kind="dma")
            P.op("pool", lambda ec, tt=tt, yk1=yk1: ec.indirect_dma_start(
                out=yk1, out_offset=None, in_=agy_out[:, :],
                in_offset=IndirectOffsetOnAxis(MYSLOTI[:, 2 * tt + 1:2 * tt + 2], 0)),
                reads=[rD("agy"), rS("MYSLOTI")], writes=[r1], kind="dma")
            astg_r = ("ASTG", par * 1024, (par + 1) * 1024)
            P.op("dve", lambda ec, tt=tt, yk0=yk0, par=par: ec.tensor_scalar(
                out=ASTG[:, par * 1024:(par + 1) * 1024], in0=yk0, scalar1=MYGATE[:, 2 * tt:2 * tt + 1],
                scalar2=None, op0=ALU.mult),
                reads=[r0, rS("MYGATE")], writes=[astg_r])
            P.op("dve", lambda ec, tt=tt, yk1=yk1, par=par: ec.scalar_tensor_tensor(
                out=ASTG[:, par * 1024:(par + 1) * 1024], in0=yk1, scalar=MYGATE[:, 2 * tt + 1:2 * tt + 2],
                in1=ASTG[:, par * 1024:(par + 1) * 1024], op0=ALU.mult, op1=ALU.add),
                reads=[r1, rS("MYGATE"), astg_r], writes=[astg_r])
            P.op("dve", lambda ec, tt=tt, par=par: ec.scalar_tensor_tensor(
                out=XS[:, tt * 1024:(tt + 1) * 1024], in0=ASTG[:, par * 1024:(par + 1) * 1024], scalar=0.5,
                in1=XS[:, tt * 1024:(tt + 1) * 1024], op0=ALU.mult, op1=ALU.add),
                reads=[astg_r, rXS(tt)], writes=[rXS(tt)])
        P.phase()

    def emit_attn():
        emit_ln(aln2_dram, "aln2")
        P.phase()
        for dc in range(DCN):
            DMA("sp", BUF_A[:, dc * 2048:(dc + 1) * 2048], aln2_dram[:, dc * 128:(dc + 1) * 128],
                reads=[rD("aln2")], writes=[rA(dc * 2048, (dc + 1) * 2048)], transpose=True)
        for dc in range(DCN):
            pb, pres = (PS0, rP0) if dc % 2 == 0 else (PS1, rP1)
            for j in range(4):
                P.op("pe", lambda ec, dc=dc, j=j, pb=pb: ec.matmul(
                    out=pb[:, j * 512:(j + 1) * 512], lhsT=MTBUF[:, dc * 128:(dc + 1) * 128],
                    rhs=BUF_A[:, dc * 2048 + j * 512: dc * 2048 + (j + 1) * 512], start=True, stop=True),
                    reads=[rS("MTBUF"), rA(dc * 2048 + j * 512, dc * 2048 + (j + 1) * 512)],
                    writes=[pres(j * 512, (j + 1) * 512)])
            P.op("dve", lambda ec, dc=dc, pb=pb: ec.tensor_copy(out=BUF_B[:, dc * 2048:(dc + 1) * 2048], in_=pb[:, :]),
                 reads=[pres()], writes=[rB(dc * 2048, (dc + 1) * 2048)])
        for kt in range(TTN):
            pb, pres = (PS0, rP0) if kt % 2 == 0 else (PS1, rP1)
            for dc in range(DCN):
                P.op("pe", lambda ec, kt=kt, dc=dc, pb=pb: ec.matmul(
                    out=pb[:, dc * 128:(dc + 1) * 128],
                    lhsT=BUF_A[:, dc * 2048 + kt * 128: dc * 2048 + (kt + 1) * 128],
                    rhs=GVTBUF[:, dc * 128:(dc + 1) * 128], start=True, stop=True),
                    reads=[rA(dc * 2048 + kt * 128, dc * 2048 + (kt + 1) * 128), rS("GVTBUF")],
                    writes=[pres(dc * 128, (dc + 1) * 128)])
            P.op("dve", lambda ec, kt=kt, pb=pb: ec.tensor_copy(out=BUF_C[:, kt * 1024:(kt + 1) * 1024],
                                                                in_=pb[:, 0:1024]),
                 reads=[pres(0, 1024)], writes=[rC(kt * 1024, (kt + 1) * 1024)])
        P.phase()
        for qt in range(TTN):
            pb, pres = (PS0, rP0) if qt % 2 == 0 else (PS1, rP1)
            for dc in range(DCN):
                for j in range(4):
                    P.op("pe", lambda ec, qt=qt, dc=dc, j=j, pb=pb: ec.matmul(
                        out=pb[:, j * 512:(j + 1) * 512],
                        lhsT=BUF_A[:, dc * 2048 + qt * 128: dc * 2048 + (qt + 1) * 128],
                        rhs=BUF_B[:, dc * 2048 + j * 512: dc * 2048 + (j + 1) * 512],
                        start=(dc == 0), stop=(dc == DCN - 1)),
                        reads=[rA(dc * 2048 + qt * 128, dc * 2048 + (qt + 1) * 128),
                               rB(dc * 2048 + j * 512, dc * 2048 + (j + 1) * 512)],
                        writes=[pres(j * 512, (j + 1) * 512)])
            cb = 16 + (qt % 2) * 8
            P.op("dve", lambda ec, pb=pb, cb=cb: ec.reduce_max(out=COLS[:, cb:cb + 1], in_=pb[:, :], axis=AX.X),
                 reads=[pres()], writes=[("COLS", cb, cb + 1)])
            P.op("dve", lambda ec, pb=pb, cb=cb: ec.tensor_reduce(out=COLS[:, cb + 1:cb + 2], in_=pb[:, :],
                                                                  axis=AX.X, op=ALU.min),
                 reads=[pres()], writes=[("COLS", cb + 1, cb + 2)])
            P.op("dve", lambda ec, cb=cb: ec.tensor_sub(out=COLS[:, cb + 2:cb + 3], in0=COLS[:, cb:cb + 1],
                                                        in1=COLS[:, cb + 1:cb + 2]),
                 reads=[("COLS", cb, cb + 2)], writes=[("COLS", cb + 2, cb + 3)])
            P.op("dve", lambda ec, cb=cb: ec.tensor_mul(out=COLS[:, cb + 2:cb + 3], in0=COLS[:, cb + 2:cb + 3],
                                                        in1=SGNPOS[:, :]),
                 reads=[("COLS", cb + 2, cb + 3), rS("SGNPOS")], writes=[("COLS", cb + 2, cb + 3)])
            P.op("dve", lambda ec, cb=cb: ec.tensor_add(out=COLS[:, cb + 2:cb + 3], in0=COLS[:, cb + 2:cb + 3],
                                                        in1=COLS[:, cb + 1:cb + 2]),
                 reads=[("COLS", cb + 1, cb + 3)], writes=[("COLS", cb + 2, cb + 3)])
            P.op("dve", lambda ec, cb=cb: ec.tensor_mul(out=COLS[:, cb + 3:cb + 4], in0=COLS[:, cb + 2:cb + 3],
                                                        in1=ALPHAC[:, :]),
                 reads=[("COLS", cb + 2, cb + 3), rS("ALPHAC")], writes=[("COLS", cb + 3, cb + 4)])
            P.op("dve", lambda ec, cb=cb: ec.tensor_single_scalar(out=COLS[:, cb + 3:cb + 4], in_=COLS[:, cb + 3:cb + 4],
                                                                  scalar=-1.0, op=ALU.mult),
                 reads=[("COLS", cb + 3, cb + 4)], writes=[("COLS", cb + 3, cb + 4)])
            P.op("act", lambda ec, pb=pb, cb=cb: ec.activation(out=ASTG[:, :], in_=pb[:, :], func=AF.Exp,
                                                               bias=COLS[:, cb + 3:cb + 4], scale=ALPHAC[:, :],
                                                               accum_out=COLS[:, cb + 4:cb + 5]),
                 reads=[pres(), ("COLS", cb + 3, cb + 4), rS("ALPHAC")],
                 writes=[("ASTG", 0, 2048), ("COLS", cb + 4, cb + 5)])
            P.op("dve", lambda ec, cb=cb: ec.reciprocal(out=COLS[:, cb + 5:cb + 6], in_=COLS[:, cb + 4:cb + 5]),
                 reads=[("COLS", cb + 4, cb + 5)], writes=[("COLS", cb + 5, cb + 6)])
            P.op("dve", lambda ec, cb=cb: ec.tensor_scalar(out=XC[:, :].bitcast(BF16),
                                                           in0=ASTG[:, :], scalar1=COLS[:, cb + 5:cb + 6],
                                                           scalar2=None, op0=ALU.mult),
                 reads=[("ASTG", 0, 2048), ("COLS", cb + 5, cb + 6)], writes=[("XC", 0, 1024)])
            DMA("sp", a_dram[qt * 128:(qt + 1) * 128, :], XC[:, :].bitcast(BF16),
                reads=[("XC", 0, 1024)], writes=[rD("a", qt * 128, (qt + 1) * 128)])
        P.phase()
        for kt in range(TTN):
            dstbuf = BUF_A if kt < 8 else BUF_B
            rr = rA if kt < 8 else rB
            off = (kt % 8) * 2048
            DMA("sp", dstbuf[:, off:off + 2048], a_dram[:, kt * 128:(kt + 1) * 128],
                reads=[rD("a")], writes=[rr(off, off + 2048)], transpose=True)
        for qt in range(TTN):
            pb, pres = (PS0, rP0) if qt % 2 == 0 else (PS1, rP1)
            for kt in range(TTN):
                dstbuf = BUF_A if kt < 8 else BUF_B
                rr = rA if kt < 8 else rB
                off = (kt % 8) * 2048
                for hf in range(2):
                    P.op("pe", lambda ec, qt=qt, kt=kt, hf=hf, pb=pb, dstbuf=dstbuf, off=off: ec.matmul(
                        out=pb[:, hf * 512:(hf + 1) * 512],
                        lhsT=dstbuf[:, off + qt * 128: off + (qt + 1) * 128],
                        rhs=BUF_C[:, kt * 1024 + hf * 512: kt * 1024 + (hf + 1) * 512],
                        start=(kt == 0), stop=(kt == TTN - 1)),
                        reads=[rr(off + qt * 128, off + (qt + 1) * 128),
                               rC(kt * 1024 + hf * 512, kt * 1024 + (hf + 1) * 512)],
                        writes=[pres(hf * 512, (hf + 1) * 512)])
            cb = 16 + (qt % 2) * 8
            P.op("act", lambda ec, pb=pb, cb=cb, qt=qt: ec.activation(
                out=ASTG[:, (qt % 2) * 1024:(qt % 2) * 1024 + 1024], in_=pb[:, 0:1024], func=AF.Square,
                accum_out=COLS[:, cb + 6:cb + 7]),
                reads=[pres(0, 1024)],
                writes=[("ASTG", (qt % 2) * 1024, (qt % 2) * 1024 + 1024), ("COLS", cb + 6, cb + 7)])
            P.op("dve", lambda ec, cb=cb: ec.tensor_single_scalar(out=COLS[:, cb + 6:cb + 7], in_=COLS[:, cb + 6:cb + 7],
                                                                  scalar=1e-8, op=ALU.max),
                 reads=[("COLS", cb + 6, cb + 7)], writes=[("COLS", cb + 6, cb + 7)])
            P.op("act", lambda ec, cb=cb: ec.activation(out=COLS[:, cb + 7:cb + 8], in_=COLS[:, cb + 6:cb + 7],
                                                        func=AF.Sqrt),
                 reads=[("COLS", cb + 6, cb + 7)], writes=[("COLS", cb + 7, cb + 8)])
            P.op("dve", lambda ec, cb=cb: ec.reciprocal(out=COLS[:, cb + 6:cb + 7], in_=COLS[:, cb + 7:cb + 8]),
                 reads=[("COLS", cb + 7, cb + 8)], writes=[("COLS", cb + 6, cb + 7)])
            P.op("dve", lambda ec, qt=qt, pb=pb, cb=cb: ec.scalar_tensor_tensor(
                out=XS[:, qt * 1024:(qt + 1) * 1024], in0=pb[:, 0:1024], scalar=COLS[:, cb + 6:cb + 7],
                in1=XS[:, qt * 1024:(qt + 1) * 1024], op0=ALU.mult, op1=ALU.add),
                reads=[pres(0, 1024), ("COLS", cb + 6, cb + 7), rXS(qt)], writes=[rXS(qt)])
        P.phase()

    ck_dram = {}

    def ckpt(name):
        t = nc.dram_tensor("ck_" + name, [T, D], F32, kind="ExternalOutput").ap()
        ck_dram[name] = t
        DMA("sp", t.rearrange("(tt p) d -> p tt d", p=128),
            XS[:, :].rearrange("p (tt d) -> p tt d", d=1024),
            reads=[("XS", 0, TTN * 1024)], writes=[("CK_" + name, 0, 1)])

    emit_ffn("f1")
    if stage == 45:
        ckpt("x1")
    if stage >= 5:
        emit_attn()
    if stage == 46:
        ckpt("x2")
    if stage >= 6 and stage != 45:
        emit_ffn("f3")

    fins = [DMA("sp", out[tt * 128:(tt + 1) * 128, :], XS[:, tt * 1024:(tt + 1) * 1024],
                reads=[("XS", tt * 1024, (tt + 1) * 1024)], writes=[("OUT", tt, tt + 1)])
            for tt in range(TTN)]
    tap_toks = list(fins)
    allw = [(s, P.val[s]) for s in SEMNAMES]
    _sbufs = {"WGB": WGB[:, :], "WGF": WGF[:, :], "BUFB": BUF_B[:, :], "BUFA": BUF_A[:, :], "LSTG": LSTG[:, :], "TOKIDX": TOKIDX[:, :], "GTB": GTB[:, :], "C0COL": C0COL[:, :], "MYSLOT": MYSLOT[:, :], "MYGATE": MYGATE[:, :], "IOTA8T": IOTA8T[:, :], "UT": UT[:, :]}
    sbuf_taps = {n: _sbufs[n] for n in (sbuf_taps or ())}
    for name, ap in (sbuf_taps or {}).items():
        shp = [ap.shape[0], ap.free_size()]
        tp = nc.dram_tensor("stap_" + name, shp, ap.dtype, kind="ExternalOutput").ap()
        tap_toks.append(P.op("pool", lambda ec, tp=tp, ap=ap: ec.dma_start(out=tp, in_=ap),
                             waits=allw, kind="dma"))
    for name in taps:
        srct = dram_by_name[name]
        tp = nc.dram_tensor("tap_" + name, list(srct.tensor.shape), srct.dtype, kind="ExternalOutput").ap()
        tap_toks.append(DMA("pool", tp, srct, waits=allw))
    P.nop_op("sp", lambda ec: ec.nop(), waits=tap_toks + [(s, P.val[s]) for s in SEMNAMES])
    P.phase()

    with nc.Block() as block:
        n = P.emit(block, sems, nc)
    es.close()
    return nc


def host_inputs(L0_attn_Q, L0_attn_rot_w, L0_attn_scale, L0_attn_bias,
                L0_ffn_wg, L0_ffn_w1, L0_ffn_b1, L0_ffn_w2, L0_ffn_b2):
    Q = np.ascontiguousarray(np.asarray(L0_attn_Q, dtype=np.float32))
    rw = np.ascontiguousarray(np.asarray(L0_attn_rot_w, dtype=np.float32)).reshape(1, 3 * D)
    sc = np.asarray(L0_attn_scale, dtype=np.float32).reshape(1, 1)
    wg = np.ascontiguousarray(np.asarray(L0_ffn_wg, dtype=np.float32)).reshape(DCN, 128, 8)
    w1 = np.ascontiguousarray(np.asarray(L0_ffn_w1, dtype=np.float32))
    b1 = np.ascontiguousarray(np.asarray(L0_ffn_b1, dtype=np.float32))
    w2 = np.ascontiguousarray(np.asarray(L0_ffn_w2, dtype=np.float32))
    b2 = np.ascontiguousarray(np.asarray(L0_ffn_b2, dtype=np.float32))

    ut = np.triu(np.ones((128, 128), np.float32), 1)
    onesrow = np.ones((1, 128), np.float32)
    ones128 = np.ones((128, 1), np.float32)
    identf = np.eye(128, dtype=np.float32)
    iotap = np.arange(128, dtype=np.float32).reshape(128, 1)
    iota8t = np.tile(np.arange(8, dtype=np.float32), (128, 128))
    iota128 = np.tile(np.arange(128, dtype=np.float32), (128, 1))
    scol32 = (np.arange(128, dtype=np.float32)[:, None] + 128.0 * np.arange(32, dtype=np.float32)[None, :])
    diag0 = np.eye(128, dtype=np.float32)
    diagu = np.zeros((128, 128), np.float32)
    diagl = np.zeros((128, 128), np.float32)
    for i in range(64):
        diagu[2 * i, 2 * i + 1] = 1.0
        diagl[2 * i + 1, 2 * i] = 1.0

    in_maps = []
    for c in range(8):
        sel0b = np.zeros((128, 16), np.float32)
        sel0b[:, c] = 1.0
        sel1b = np.zeros((128, 16), np.float32)
        sel1b[:, 8 + c] = 1.0
        cval = np.full((128, 1), float(c), np.float32)
        myidx = (c * 128 + np.arange(128, dtype=np.int32)).reshape(128, 1)
        in_maps.append({
            "xq": Q[c], "rwf": rw, "scale1": sc, "wg3": wg,
            "w1c": w1[c], "w2c": w2[c], "b1c": b1[c].reshape(HTN, 128, 1), "b2c": b2[c].reshape(1, D),
            "ut_in": ut, "onesrow_in": onesrow, "ones128_in": ones128, "identf_in": identf,
            "iotap_in": iotap, "iota8t_in": iota8t, "iota128_in": iota128, "scol32_in": scol32, "diag0_in": diag0, "diagu_in": diagu, "diagl_in": diagl,
            "sel0b_in": sel0b, "sel1b_in": sel1b, "cval_in": cval, "myidx_in": myidx,
        })
    return in_maps


def kernel(L0_attn_Q, L0_attn_rot_w, L0_attn_scale, L0_attn_bias,
           L0_ffn_wg, L0_ffn_w1, L0_ffn_b1, L0_ffn_w2, L0_ffn_b2):
    in_maps = host_inputs(L0_attn_Q, L0_attn_rot_w, L0_attn_scale, L0_attn_bias,
                          L0_ffn_wg, L0_ffn_w1, L0_ffn_b1, L0_ffn_w2, L0_ffn_b2)
    nc = build_nc()
    res = run_bass_kernel_spmd(nc, in_maps, core_ids=list(range(8)))
    return np.stack([r["out"] for r in res.results], axis=0)
